# version 06
# baseline (speedup 1.0000x reference)
import numpy as np
from contextlib import ExitStack
import concourse.bass as bass
import concourse.mybir as mybir
from concourse.bass import IndirectOffsetOnAxis
from concourse.bass_utils import run_bass_kernel_spmd

F32 = mybir.dt.float32
BF16 = mybir.dt.bfloat16
I32 = mybir.dt.int32
U32 = mybir.dt.uint32
U8 = mybir.dt.uint8
AF = mybir.ActivationFunctionType
ALU = mybir.AluOpType
AX = mybir.AxisListType.X

B, T, D = 2, 4096, 2048
NCORE = 8
TPC = 1024
NTT = TPC // 128
GSZ = 4

ENGS = ("tensor", "vector", "scalar", "gpsimd", "sync")

C_EPS5, C_NHALF, C_P1, C_EPS6, C_M30, C_30 = 0, 1, 2, 3, 4, 5
NCONST = 8
RBIG = 12582912.0
DFF = 8192


def host_consts():
    v = np.zeros((128, NCONST), np.float32)
    v[:, C_EPS5] = 1e-5
    v[:, C_NHALF] = -0.5
    v[:, C_P1] = 0.1
    v[:, C_EPS6] = 1e-6
    v[:, C_M30] = -30.0
    v[:, C_30] = 30.0
    return v


class Prog:

    def __init__(self, nc):
        self.nc = nc
        self.es = ExitStack()
        self.ops = {e: [] for e in ENGS}
        self.sem_handles = {}
        self.counts = {}

    def sem(self, name):
        if name not in self.sem_handles:
            self.sem_handles[name] = self.es.enter_context(self.nc.semaphore(name))
            self.counts[name] = 0
        return name

    def val(self, name):
        pre = name + "~"
        out = [(n, c) for n, c in self.counts.items()
               if (n == name or n.startswith(pre)) and c > 0]
        return out

    def add(self, eng, fn, wait=(), inc=None, inc_amt=1):
        mil = None
        if inc is not None:
            inc = f"{inc}~{eng}"
            self.sem(inc)
            self.counts[inc] += inc_amt
            mil = (inc, self.counts[inc])
        flat = []
        for m in wait:
            if m is None:
                continue
            if isinstance(m, list):
                flat.extend(x for x in m if x is not None)
            else:
                flat.append(m)
        wait = tuple((s, v) for s, v in flat if v > 0)
        for s, _ in wait:
            self.sem(s)
        self.ops[eng].append((fn, wait, inc, inc_amt))
        return mil

    def op(self, eng, name, *args, wait=(), inc=None, inc_amt=1, **kw):
        def fn(e, name=name, args=args, kw=kw):
            return getattr(e, name)(*args, **kw)
        return self.add(eng, fn, wait=wait, inc=inc, inc_amt=inc_amt)

    def dma(self, eng, out, in_, wait=(), inc=None, **kw):
        return self.add(eng, lambda e: e.dma_start(out=out, in_=in_, **kw),
                        wait=wait, inc=inc, inc_amt=16)

    def dma_t(self, eng, out, in_, wait=(), inc=None):
        return self.add(eng, lambda e: e.dma_start_transpose(out, in_),
                        wait=wait, inc=inc, inc_amt=16)

    def sbuf(self, name, shape, dtype=F32):
        return self.es.enter_context(self.nc.sbuf_tensor(name, shape, dtype))

    def psum(self, name, shape, dtype=F32):
        return self.es.enter_context(self.nc.psum_tensor(name, shape, dtype))

    def emit_and_close(self):
        with self.es:
            blk = self.es.enter_context(self.nc.Block())
            handles = self.sem_handles
            for ename in ENGS:
                oplist = self.ops[ename]
                if not oplist:
                    continue

                def make_body(oplist):
                    def body(eng):
                        hw = {}
                        for fn, waits, inc, inc_amt in oplist:
                            for s, val in waits:
                                if hw.get(s, 0) < val:
                                    eng.wait_ge(handles[s], val)
                                    hw[s] = val
                            inst = fn(eng)
                            if inc is not None:
                                inst.then_inc(handles[inc], inc_amt)
                    return body

                getattr(blk, ename)(make_body(oplist))



def emit_ln_stats(P, Xs, cst, stats, sq_scr, tag, x_ready, n_tiles=NTT):
    s_sq = f"{tag}sq"
    s_sum = f"{tag}sm"
    s_ch = f"{tag}ch"
    sums = stats[:, 0 * n_tiles:1 * n_tiles]
    sqs = stats[:, 1 * n_tiles:2 * n_tiles]
    means = stats[:, 2 * n_tiles:3 * n_tiles]
    m2 = stats[:, 3 * n_tiles:4 * n_tiles]
    ve = stats[:, 4 * n_tiles:5 * n_tiles]
    rstd = stats[:, 5 * n_tiles:6 * n_tiles]
    m_prev_sq = None
    for tt in range(n_tiles):
        w = tuple(x_ready[tt]) if x_ready else ()
        m_prev_sq = P.add("scalar",
              (lambda X=Xs[tt], o=sqs[:, tt:tt + 1]:
               lambda e: e.activation(out=sq_scr[:, 0:D], in_=X,
                                      func=AF.Square, accum_out=o))(),
              wait=tuple(w) + ((m_prev_sq,) if m_prev_sq else ()), inc=s_sq)
        P.add("vector",
              (lambda X=Xs[tt], o=sums[:, tt:tt + 1]:
               lambda e: e.tensor_reduce(o, X, axis=AX, op=ALU.add))(),
              wait=w, inc=s_sum)
    m_mean = P.add("vector", lambda e: e.tensor_scalar_mul(means, sums, 1.0 / D),
                   wait=[P.val(s_sum)], inc=s_ch)
    m_m2 = P.add("vector", lambda e: e.tensor_mul(m2, means, means),
                 wait=[m_mean], inc=s_ch)
    m_ve = P.add("vector", lambda e: e.scalar_tensor_tensor(
        ve, sqs, 1.0 / D, m2, ALU.mult, ALU.subtract),
        wait=[m_m2, P.val(s_sq)], inc=s_ch)
    m_ln = P.add("scalar", lambda e: e.activation(
        out=ve, in_=ve, func=AF.Ln, bias=cst[:, C_EPS5:C_EPS5 + 1]),
        wait=[m_ve], inc=s_ch)
    m_rstd = P.add("scalar", lambda e: e.activation(
        out=rstd, in_=ve, func=AF.Exp, scale=cst[:, C_NHALF:C_NHALF + 1]),
        wait=[m_ln], inc=s_ch)
    return m_rstd, means, rstd


def emit_ln_to_xnT(P, Xs, cst, identb, xnT_dst, tag, x_ready, xn_bufs, psTs,
                   n_tiles=NTT, keep_xn=None):
    stats = P.sbuf(f"{tag}_stats", [128, 6 * n_tiles], F32)
    sq_scr = P.sbuf(f"{tag}_sqscr", [128, D], BF16)
    m_rstd, means, rstd = emit_ln_stats(P, Xs, cst, stats, sq_scr, tag,
                                        x_ready, n_tiles)
    s_xn = f"{tag}xn"
    s_tr = f"{tag}tr"
    s_ev = f"{tag}ev"
    xn_dones = []
    ev_mils = []
    tr_mils = []
    for tt in range(n_tiles):
        buf = xn_bufs[tt % 2] if keep_xn is None else keep_xn(tt)
        extra = []
        if keep_xn is None and tt >= 2:
            extra.append(tr_mils[(tt - 1) * 16 - 1])
        m_xn = P.add("vector",
                     (lambda X=Xs[tt], o=buf, m=means[:, tt:tt + 1],
                      r=rstd[:, tt:tt + 1]:
                      lambda e: e.tensor_scalar(o[:, 0:D], X, m, r,
                                                ALU.subtract, ALU.mult))(),
                     wait=[m_rstd] + extra, inc=s_xn)
        xn_dones.append(m_xn)
        for dc in range(16):
            k = tt * 16 + dc
            ps = psTs[k % 2]
            w = [m_xn]
            if k >= 2:
                w.append(ev_mils[k - 2])
            m_tr = P.add("tensor",
                         (lambda b=buf, dc=dc, ps=ps:
                          lambda e: e.transpose(
                              ps[:, 0:128], b[:, dc * 128:(dc + 1) * 128],
                              identb[:]))(),
                         wait=w, inc=s_tr)
            tr_mils.append(m_tr)
            m_ev = P.add("vector",
                         (lambda o=xnT_dst(tt, dc), ps=ps:
                          lambda e: e.tensor_copy(out=o, in_=ps[:, 0:128]))(),
                         wait=[m_tr], inc=s_ev)
            ev_mils.append(m_ev)
    return ev_mils[-1], xn_dones



def emit_f1(P, Xs, cst, identb, w1b, w2b, m_w1b, m_w2b, tag, x_ready, out_cb):
    xn_bufs = [P.sbuf(f"{tag}_xnb{i}", [128, D], BF16) for i in range(2)]
    xnT = P.sbuf(f"{tag}_xnT", [128, 16 * TPC], BF16)
    aT = P.sbuf(f"{tag}_aT", [128, 64 * 512], BF16)
    w1t = [P.sbuf(f"{tag}_w1t{i}", [128, 512], BF16) for i in range(4)]
    w2t = [P.sbuf(f"{tag}_w2t{i}", [128, 512], BF16) for i in range(4)]
    q1 = [P.sbuf(f"{tag}_q1{i}", [128, 512], F32) for i in range(2)]
    q2 = [P.sbuf(f"{tag}_q2{i}", [128, 512], F32) for i in range(2)]
    q3 = [P.sbuf(f"{tag}_q3{i}", [128, 512], F32) for i in range(2)]
    zev = [P.sbuf(f"{tag}_ze{i}", [128, 512], F32) for i in range(2)]
    hps = [P.psum(f"{tag}_hps{i}", [128, 512], F32) for i in range(4)]
    zps = [P.psum(f"{tag}_zps{i}", [128, 512], F32) for i in range(4)]
    psTs = [zps[0].bitcast(BF16)[:, 0:128], zps[1].bitcast(BF16)[:, 0:128]]

    def xnT_dst(tt, dc):
        return xnT[:, dc * TPC + tt * 128: dc * TPC + (tt + 1) * 128]

    m_xnT, _ = emit_ln_to_xnT(P, Xs, cst, identb, xnT_dst, tag + "L",
                              x_ready, xn_bufs, psTs)

    s_w1 = f"{tag}w1"
    s_w1f = f"{tag}w1f"
    s_aq1 = f"{tag}a1"
    s_aq2 = f"{tag}a2"
    s_aq3 = f"{tag}a3"
    s_a = f"{tag}a"
    s_w2 = f"{tag}w2"
    s_w2f = f"{tag}w2f"
    s_z = f"{tag}z"
    s_zev = f"{tag}ze"
    s_zst = f"{tag}zs"

    w1f_mils = []
    w2f_mils = []
    aq2_hist = {}
    aq3_hist = {}
    gelu_hist = {}
    aq1_mils = []
    gelu_mils = [None] * 128
    last_z_of_tw0 = None
    stt_mils = {}
    dma_out_mils = {}

    nslice = 0
    nf = 0
    for tw in range(2):
        for quad in range(16):
            for dc in range(16):
                par = nslice % 4
                f0, d0 = quad * 512, dc * 128
                w = [m_w1b[d0 // 256]]
                if nslice >= 4:
                    w.append(w1f_mils[nslice - 4])
                m_ld = P.dma("sync", w1t[par][:, :],
                             w1b[d0:d0 + 128, f0:f0 + 512], wait=w,
                             inc=s_w1 + str(par))
                rh = xnT[:, dc * TPC + tw * 512: dc * TPC + tw * 512 + 512]
                for fl in range(4):
                    wm = [m_ld]
                    if dc == 0:
                        wm.append(m_xnT)
                        if quad + tw * 16 >= 1:
                            wm.append(aq1_mils[-1])
                    mm = P.add(
                        "tensor",
                        (lambda ps=hps[fl],
                         lt=w1t[par][:, fl * 128:(fl + 1) * 128],
                         rh=rh, dc=dc:
                         lambda e: e.matmul(ps[:, :], lt, rh,
                                            start=(dc == 0), stop=(dc == 15)))(),
                        wait=wm,
                        inc=(s_w1f if fl == 3 else None))
                    if fl == 3:
                        w1f_mils.append(mm)
                nslice += 1
            for fl in range(4):
                ft = quad * 4 + fl
                apar = nf % 2
                w1_ = [w1f_mils[-1]]
                if apar in aq2_hist:
                    w1_.append(aq2_hist[apar])
                m1 = P.add("vector",
                           (lambda o=q1[apar], i=hps[fl]:
                            lambda e: e.tensor_scalar(o[:, :], i[:, :], 10.0,
                                                      RBIG, ALU.mult, ALU.add))(),
                           wait=w1_, inc=s_aq1)
                aq1_mils.append(m1)
                w2_ = [m1]
                if apar in aq3_hist:
                    w2_.append(aq3_hist[apar])
                m2_ = P.add("vector",
                            (lambda o=q2[apar], i=q1[apar]:
                             lambda e: e.tensor_scalar(o[:, :], i[:, :], RBIG,
                                                       32767.0, ALU.subtract,
                                                       ALU.min))(),
                            wait=w2_, inc=s_aq2)
                aq2_hist[apar] = m2_
                w3_ = [m2_]
                if apar in gelu_hist:
                    w3_.append(gelu_hist[apar])
                m3 = P.add("vector",
                           (lambda o=q3[apar], i=q2[apar]:
                            lambda e: e.tensor_scalar(o[:, :], i[:, :],
                                                      -32768.0, 0.1,
                                                      ALU.max, ALU.mult))(),
                           wait=w3_, inc=s_aq3)
                aq3_hist[apar] = m3
                wg = [m3]
                if tw == 1:
                    wg.append(last_z_of_tw0)
                mg = P.add("scalar",
                           (lambda o=aT[:, ft * 512:(ft + 1) * 512], i=q3[apar]:
                            lambda e: e.activation(out=o, in_=i[:, :],
                                                   func=AF.Gelu_apprx_tanh))(),
                           wait=wg, inc=s_a)
                gelu_hist[apar] = mg
                gelu_mils[ft] = mg
                nf += 1
        w2slice = 0
        for dq in range(4):
            zmm_last = [None] * 4
            for ft in range(64):
                par = (tw * 256 + dq * 64 + ft) % 4
                w = [m_w2b[ft // 16]]
                if len(w2f_mils) >= 4:
                    w.append(w2f_mils[-4])
                m_ld = P.dma("sync", w2t[par][:, :],
                             w2b[ft * 128:(ft + 1) * 128,
                                 dq * 512:(dq + 1) * 512],
                             wait=w, inc=s_w2 + str(par))
                for ttl in range(4):
                    tt = tw * 4 + ttl
                    wm = [m_ld, gelu_mils[ft]]
                    if ft == 0:
                        wm.append(m_xnT)
                        if (tw, dq - 1, ttl) in stt_mils:
                            wm.append(stt_mils[(tw, dq - 1, ttl)])
                        if tw == 1 and (0, 3, ttl) in stt_mils:
                            wm.append(stt_mils[(0, 3, ttl)])
                    mm = P.add(
                        "tensor",
                        (lambda ps=zps[ttl],
                         lt=aT[:, ft * 512 + ttl * 128: ft * 512 + (ttl + 1) * 128],
                         rh=w2t[par], ft=ft:
                         lambda e: e.matmul(ps[:, :], lt, rh[:, :],
                                            start=(ft == 0), stop=(ft == 63)))(),
                        wait=wm,
                        inc=(s_w2f if ttl == 3 else (s_z if ft == 63 else None)))
                    if ttl == 3:
                        w2f_mils.append(mm)
                    if ft == 63:
                        zmm_last[ttl] = mm
                w2slice += 1
            for ttl in range(4):
                tt = tw * 4 + ttl
                epar = (tw * 16 + dq * 4 + ttl) % 2
                w = [zmm_last[ttl] if zmm_last[ttl] is not None else P.val(s_z)]
                if epar in dma_out_mils:
                    w.append(dma_out_mils[epar])
                m_stt = P.add(
                    "vector",
                    (lambda o=zev[epar], zp=zps[ttl],
                     xs=Xs[tt][:, dq * 512:(dq + 1) * 512]:
                     lambda e: e.scalar_tensor_tensor(
                         o[:, :], zp[:, :], 0.5, xs, ALU.mult, ALU.add))(),
                    wait=w, inc=s_zst)
                stt_mils[(tw, dq, ttl)] = m_stt
                m_out = out_cb(tt, dq, zev[epar], m_stt)
                dma_out_mils[epar] = m_out
        if tw == 0:
            last_z_of_tw0 = w2f_mils[-1]
    return



def build_f1_launch():
    nc = bass.Bass("TRN2", target_bir_lowering=False, debug=False)
    xin = nc.dram_tensor("xin", [TPC, D], F32, kind="ExternalInput").ap()
    w1 = nc.dram_tensor("w1", [D, DFF], F32, kind="ExternalInput").ap()
    w2 = nc.dram_tensor("w2", [DFF, D], F32, kind="ExternalInput").ap()
    cst_in = nc.dram_tensor("cst", [128, NCONST], F32, kind="ExternalInput").ap()
    ident_in = nc.dram_tensor("ident", [128, 128], F32, kind="ExternalInput").ap()
    xo = nc.dram_tensor("xo", [TPC, D], F32, kind="ExternalOutput").ap()
    w1b = nc.dram_tensor("w1b", [D, DFF], BF16, kind="Internal").ap()
    w2b = nc.dram_tensor("w2b", [DFF, D], BF16, kind="Internal").ap()

    P = Prog(nc)
    Xbuf = P.sbuf("Xbuf", [128, NTT * D], F32)
    Xs = [Xbuf[:, tt * D:(tt + 1) * D] for tt in range(NTT)]
    cst = P.sbuf("cstt", [128, NCONST], F32)
    identb = P.sbuf("identb", [128, 128], BF16)

    P.dma("gpsimd", cst[:, :], cst_in, inc="boot")
    P.dma("gpsimd", identb[:, :], ident_in, inc="boot")
    for tt in range(NTT):
        P.dma("gpsimd", Xs[tt], xin[tt * 128:(tt + 1) * 128, :], inc="xld")
    x_ready = [[P.val("xld"), P.val("boot")] for _ in range(NTT)]
    m_w1b = []
    for i in range(8):
        m = P.dma("gpsimd", w1b[i * 256:(i + 1) * 256, :],
                  w1[i * 256:(i + 1) * 256, :], inc=f"wc1_{i}")
        m_w1b.append(m)
    m_w2b = []
    for i in range(4):
        m = P.dma("gpsimd", w2b[i * 2048:(i + 1) * 2048, :],
                  w2[i * 2048:(i + 1) * 2048, :], inc=f"wc2_{i}")
        m_w2b.append(m)

    zoctr = [0]

    def out_cb(tt, dq, src, mil):
        zoctr[0] += 1
        return P.dma("gpsimd", xo[tt * 128:(tt + 1) * 128,
                                  dq * 512:(dq + 1) * 512],
                     src[:, :], wait=[mil], inc=f"zout{zoctr[0] % 2}")

    emit_f1(P, Xs, cst, identb, w1b, w2b, m_w1b, m_w2b, "f1", x_ready, out_cb)
    P.add("sync", lambda e: e.nop(), wait=[P.val("zout0"), P.val("zout1")])
    P.emit_and_close()
    return nc


def _core_slices(c):
    r = c % GSZ
    return [(r + GSZ * j) * 128 for j in range(NTT)]


def _gather_own(xb, c):
    return np.concatenate([xb[o:o + 128] for o in _core_slices(c)], 0)


def _scatter_own(dst, xo, c):
    for j, o in enumerate(_core_slices(c)):
        dst[o:o + 128] = xo[j * 128:(j + 1) * 128]


def run_f1_stage(x, w1, w2):
    nc = build_f1_launch()
    cst = host_consts()
    ident = np.eye(128, dtype=np.float32)
    w1t = np.ascontiguousarray(np.asarray(w1).T)
    w2t = np.ascontiguousarray(np.asarray(w2).T)
    in_maps = []
    for c in range(NCORE):
        xb = x[c // GSZ]
        in_maps.append({"xin": np.ascontiguousarray(_gather_own(xb, c)),
                        "w1": w1t, "w2": w2t, "cst": cst, "ident": ident})
    res = run_bass_kernel_spmd(nc, in_maps, core_ids=list(range(NCORE)))
    out = np.empty_like(x)
    for c in range(NCORE):
        _scatter_own(out[c // GSZ], np.asarray(res.results[c]["xo"]), c)
    return out



NB_T = T // 128


def emit_stream_ln(P, xin, xin_own, xnt, xnto, xbuf, xnb, blkb, st1, sq_scr,
                   cstt, identb, psT, boot, tag):
    nrm_mils = {}
    blk_dma = {}
    sq_hist = {}
    ev1 = []

    def stream_tile(idx, src_rows, dst_cols_ap):
        xb = xbuf[idx % 2]
        c0 = 8 * (idx % 2)
        w = [boot]
        if idx >= 2:
            w.append(nrm_mils[idx - 2])
        m_ld = P.dma("gpsimd", xb[:, :], src_rows, wait=w,
                     inc=f"{tag}x1ld{idx % 2}")
        wst = [m_ld]
        if idx >= 1:
            wst.append(sq_hist[idx - 1])
        m_sq = P.op("scalar", "activation", out=sq_scr[:, :], in_=xb[:, :],
                    func=AF.Square, accum_out=st1[:, c0 + 1:c0 + 2], wait=wst,
                    inc=f"{tag}p1")
        sq_hist[idx] = m_sq
        m_sm = P.op("vector", "tensor_reduce", st1[:, c0:c0 + 1], xb[:, :],
                    axis=AX, op=ALU.add, wait=[m_ld], inc=f"{tag}p1")
        m_me = P.op("vector", "tensor_scalar_mul", st1[:, c0 + 2:c0 + 3],
                    st1[:, c0:c0 + 1], 1.0 / D, wait=[m_sm], inc=f"{tag}p1")
        m_m2 = P.op("vector", "tensor_mul", st1[:, c0 + 3:c0 + 4],
                    st1[:, c0 + 2:c0 + 3], st1[:, c0 + 2:c0 + 3],
                    wait=[m_me], inc=f"{tag}p1")
        m_ve = P.op("vector", "scalar_tensor_tensor", st1[:, c0 + 4:c0 + 5],
                    st1[:, c0 + 1:c0 + 2], 1.0 / D, st1[:, c0 + 3:c0 + 4],
                    ALU.mult, ALU.subtract,
                    wait=[m_m2, m_sq], inc=f"{tag}p1")
        m_ln = P.op("scalar", "activation", out=st1[:, c0 + 4:c0 + 5],
                    in_=st1[:, c0 + 4:c0 + 5],
                    func=AF.Ln, bias=cstt[:, C_EPS5:C_EPS5 + 1],
                    wait=[m_ve], inc=f"{tag}p1")
        m_rs = P.op("scalar", "activation", out=st1[:, c0 + 5:c0 + 6],
                    in_=st1[:, c0 + 4:c0 + 5],
                    func=AF.Exp, scale=cstt[:, C_NHALF:C_NHALF + 1],
                    wait=[m_ln], inc=f"{tag}p1")
        xn = xnb[idx % 2]
        wn = [m_rs]
        if idx >= 2:
            wn.append(ev1[(idx - 2) * 16 + 15])
        m_n = P.op("vector", "tensor_scalar", xn[:, :], xb[:, :],
                   st1[:, c0 + 2:c0 + 3], st1[:, c0 + 5:c0 + 6],
                   ALU.subtract, ALU.mult,
                   wait=wn, inc=f"{tag}p1")
        nrm_mils[idx] = m_n
        bb = blkb[idx % 2]
        for dc in range(16):
            k = idx * 16 + dc
            ps = psT[k % 2]
            wtr = [m_n]
            if k >= 2:
                wtr.append(ev1[k - 2])
            m_tr = P.op("tensor", "transpose", ps,
                        xn[:, dc * 128:(dc + 1) * 128], identb[:, :],
                        wait=wtr, inc=f"{tag}p1tr")
            wev = [m_tr]
            if idx >= 2 and dc == 0:
                wev.append(blk_dma[idx - 2])
            m_ev = P.op("vector", "tensor_copy",
                        out=bb[:, dc * 128:(dc + 1) * 128], in_=ps,
                        wait=wev, inc=f"{tag}p1ev")
            ev1.append(m_ev)
        m_bd = P.dma("sync", dst_cols_ap,
                     bb[:, :].rearrange("p (dc c) -> p dc c", c=128),
                     wait=[ev1[-1]], inc=f"{tag}p1blk{idx % 2}")
        blk_dma[idx] = m_bd

    for bt in range(NB_T):
        stream_tile(bt, xin[bt * 128:(bt + 1) * 128, :],
                    xnt.rearrange("(dc p) t -> p dc t", p=128)[
                        :, :, bt * 128:(bt + 1) * 128])
    for j in range(NTT):
        stream_tile(NB_T + j, xin_own[j * 128:(j + 1) * 128, :],
                    xnto.rearrange("(dc p) t -> p dc t", p=128)[
                        :, :, j * 128:(j + 1) * 128])
    return P.val(f"{tag}p1blk0") + P.val(f"{tag}p1blk1")



C_I128, C_NI128 = 6, 7


def emit_a1_prep(P, src_ps, sw_ps, st_ps, sb_ps, sq_ps, scr, tabs, dst,
                 m_src, chain, m_tab=None):
    Ct, St = tabs
    kb, ks = scr["kb"], scr["ks"]
    t1b, t2b, t3b = scr["t1b"], scr["t2b"], scr["t3b"]
    stb, m2b, veb, rsb = scr["stb"], scr["m2b"], scr["veb"], scr["rsb"]
    cstt, lnsbt = scr["cst"], scr["lnsb"]
    ones1, sel0, sel1 = scr["ones1"], scr["sel0"], scr["sel1"]
    reads = []
    ma = P.op("scalar", "activation", out=kb[:, :], in_=src_ps[:, :],
              func=AF.Copy, wait=[m_src, chain], inc="pp")
    reads.append(ma)
    msw = P.op("tensor", "matmul", sw_ps[:, :], scr["p64b"][:, :], kb[:, :],
               start=True, stop=True, wait=[ma], inc="pp")
    wt1 = [ma, chain] + ([m_tab] if m_tab else [])
    m1 = P.op("vector", "tensor_tensor", t1b[:, :], kb[:, :], Ct, ALU.mult,
              wait=wt1, inc="pp")
    m2_ = P.op("vector", "scalar_tensor_tensor", t2b[:, :], sw_ps[:, :], 1.0,
               St, ALU.mult, ALU.mult, wait=[msw, m1], inc="pp")
    reads.append(m2_)
    m3 = P.op("vector", "tensor_add", t3b[:, :], t1b[:, :], t2b[:, :],
              wait=[m2_], inc="pp")
    msq = P.op("scalar", "activation", out=ks[:, :], in_=t3b[:, :],
               func=AF.Square, wait=[m3], inc="pp")
    ms1 = P.op("tensor", "matmul", st_ps[0:1, :], ones1[:, :], t3b[:, :],
               start=True, stop=True, wait=[m3], inc="pp")
    ms2 = P.op("tensor", "matmul", sw_ps[0:1, :], ones1[:, :], ks[:, :],
               start=True, stop=True, wait=[msq, m2_], inc="pp")
    mst1 = P.op("vector", "tensor_copy", out=stb[:, 0:512], in_=st_ps[0:1, :],
                wait=[ms1], inc="pp")
    reads.append(mst1)
    mst = P.op("vector", "tensor_copy", out=stb[:, 512:1024],
               in_=sw_ps[0:1, :], wait=[ms2, mst1], inc="pp")
    reads.append(mst)
    mb1 = P.op("tensor", "matmul", sb_ps[:, :], scr["onesc"][:, :],
               stb[:, 0:512], start=True, stop=True, wait=[mst1], inc="pp")
    mb2 = P.op("tensor", "matmul", sq_ps[:, :], scr["onesc"][:, :],
               stb[:, 512:1024], start=True, stop=True, wait=[mst], inc="pp")
    mm2 = P.op("scalar", "activation", out=m2b[:, :], in_=sb_ps[:, :],
               func=AF.Square, scale=cstt[:, C_I128:C_I128 + 1],
               wait=[mb1], inc="pp")
    mve = P.op("vector", "scalar_tensor_tensor", veb[:, :], sq_ps[:, :],
               1.0 / 128.0, m2b[:, :], ALU.mult, ALU.subtract,
               wait=[mb2, mm2], inc="pp")
    reads.append(mve)
    mln = P.op("scalar", "activation", out=veb[:, :], in_=veb[:, :],
               func=AF.Ln, bias=cstt[:, C_EPS5:C_EPS5 + 1],
               wait=[mve], inc="pp")
    mrs = P.op("scalar", "activation", out=rsb[:, :], in_=veb[:, :],
               func=AF.Exp, scale=cstt[:, C_NHALF:C_NHALF + 1],
               wait=[mln], inc="pp")
    mc = P.op("vector", "scalar_tensor_tensor", t1b[:, :], sb_ps[:, :],
              -1.0 / 128.0, t3b[:, :], ALU.mult, ALU.add,
              wait=[mrs, m3], inc="pp")
    reads.append(mc)
    mt4 = P.op("vector", "tensor_mul", t2b[:, :], t1b[:, :], rsb[:, :],
               wait=[mc], inc="pp")
    mout = P.op("vector", "tensor_scalar", dst, t2b[:, :],
                lnsbt[:, 0:1], lnsbt[:, 1:2], ALU.mult, ALU.add,
                wait=[mt4], inc="pp")
    return mout, reads


def emit_a1(P, xin, xin_own, xo, wqkv, wout, lnsb_in, cst_in, ident_in,
            p64_in, ropeC_hbm, ropeS_hbm, ropeCo_in, ropeSo_in, sel_in,
            masks_in, tag):
    nc = P.nc
    xnt = nc.dram_tensor(f"{tag}_xnt", [D, T], BF16, kind="Internal").ap()
    xnto = nc.dram_tensor(f"{tag}_xnto", [D, TPC], BF16, kind="Internal").ap()
    wqb = nc.dram_tensor(f"{tag}_wqb", [D, 3 * D], BF16, kind="Internal").ap()
    wob = nc.dram_tensor(f"{tag}_wob", [D, D], BF16, kind="Internal").ap()

    identb = P.sbuf(f"{tag}_identb", [128, 128], BF16)
    p64b = P.sbuf(f"{tag}_p64b", [128, 128], BF16)
    cstt = P.sbuf(f"{tag}_cstt", [128, NCONST], F32)
    lnsbt = P.sbuf(f"{tag}_lnsbt", [128, 2], F32)
    selb = P.sbuf(f"{tag}_selb", [2, 256], BF16)
    ones1 = P.sbuf(f"{tag}_ones1", [128, 1], BF16)
    onesc = P.sbuf(f"{tag}_onesc", [1, 128], F32)
    ropeCo = P.sbuf(f"{tag}_ropeCo", [128, TPC], BF16)
    ropeSo = P.sbuf(f"{tag}_ropeSo", [128, TPC], BF16)
    maskb = P.sbuf(f"{tag}_maskb", [128, 16 * 128], BF16)
    for dst_, src_ in ((identb, ident_in), (p64b, p64_in), (cstt, cst_in),
                       (lnsbt, lnsb_in), (selb, sel_in),
                       (ropeCo, ropeCo_in), (ropeSo, ropeSo_in)):
        P.dma("gpsimd", dst_[:, :], src_, inc="boot")
    P.dma("gpsimd", maskb[:, :],
          masks_in.rearrange("n p c -> p n c"), inc="boot")
    P.op("vector", "memset", ones1[:, :], 1.0, inc="bootv")
    P.op("vector", "memset", onesc[:, :], 1.0, inc="bootv")
    bootd = P.val("boot")
    bootv = P.val("bootv")
    boot = bootd

    def _BOOT():
        return [bootd, bootv]

    def mask_tile(delta):
        return maskb[:, delta * 128:(delta + 1) * 128]

    m_wqb = P.dma("gpsimd", wqb[:, :], wqkv, wait=[boot], inc="wc")
    m_wob = P.dma("gpsimd", wob[:, :], wout, wait=[m_wqb], inc="wc")

    PB = [P.psum(f"{tag}_pb{i}", [128, 512], F32) for i in range(8)]
    psT = [PB[0][:, :].bitcast(BF16)[:, 0:128],
           PB[1][:, :].bitcast(BF16)[:, 0:128]]

    xbuf = [P.sbuf(f"{tag}_xb{i}", [128, D], F32) for i in range(2)]
    xnb = [P.sbuf(f"{tag}_xnb{i}", [128, D], BF16) for i in range(2)]
    blkb = [P.sbuf(f"{tag}_blkb{i}", [128, 16 * 128], BF16) for i in range(2)]
    st1 = P.sbuf(f"{tag}_st1", [128, 16], F32)
    sq_scr = P.sbuf(f"{tag}_sqscr", [128, D], F32)
    outb2 = P.sbuf(f"{tag}_outb2", [128, D], F32)

    m_xnt = emit_stream_ln(P, xin, xin_own, xnt, xnto, xbuf, xnb, blkb, st1,
                           sq_scr, cstt, identb, psT, boot, tag)
    m_xnt2 = []

    y_fm = P.sbuf(f"{tag}_yfm", [128, 16 * TPC], BF16)
    knT = P.sbuf(f"{tag}_knT", [128, 4 * T], BF16)
    VW = 132
    Vg = P.sbuf(f"{tag}_Vg", [128, 4 * NB_T * VW], BF16)
    qnT = P.sbuf(f"{tag}_qnT", [128, 4 * TPC], BF16)
    wblk = P.sbuf(f"{tag}_wblk", [128, 16 * 512], BF16)
    xsl = [P.sbuf(f"{tag}_xsl{i}", [128, 512], BF16) for i in range(2)]
    rtb = [P.sbuf(f"{tag}_rtb{i}", [128, 1024], BF16) for i in range(2)]
    scr = {
        "kb": P.sbuf(f"{tag}_kb", [128, 512], BF16),
        "ks": P.sbuf(f"{tag}_ks", [128, 512], BF16),
        "t1b": P.sbuf(f"{tag}_t1b", [128, 512], BF16),
        "t2b": P.sbuf(f"{tag}_t2b", [128, 512], BF16),
        "t3b": P.sbuf(f"{tag}_t3b", [128, 512], BF16),
        "stb": P.sbuf(f"{tag}_stb", [1, 1024], F32),
        "onesc": None,
        "m2b": P.sbuf(f"{tag}_m2b", [128, 512], F32),
        "veb": P.sbuf(f"{tag}_veb", [128, 512], F32),
        "rsb": P.sbuf(f"{tag}_rsb", [128, 512], BF16),
        "cst": cstt, "lnsb": lnsbt, "ones1": ones1,
        "sel0": selb[:, 0:128], "sel1": selb[:, 128:256],
        "p64b": p64b,
    }
    scr["onesc"] = onesc
    s2b = P.sbuf(f"{tag}_s2b", [128, 512], BF16)
    a4b = [P.sbuf(f"{tag}_a4b{i}", [128, 512], BF16) for i in range(2)]
    ytb = P.sbuf(f"{tag}_ytb", [128, 512], BF16)
    denb = P.sbuf(f"{tag}_denb", [1, 512], F32)
    recb = P.sbuf(f"{tag}_recb", [1, 512], F32)

    def vg_tile(hl, kt):
        o = (hl * NB_T + kt) * VW
        return Vg[:, o:o + VW]

    for hl in range(4):
        for kt in range(NB_T):
            P.op("gpsimd", "memset", vg_tile(hl, kt)[:, 128:132], 0.0,
                 wait=[P.val("vone")], inc="vone")
            P.op("gpsimd", "memset", vg_tile(hl, kt)[:, 128:129], 1.0,
                 wait=[P.val("vone")], inc="vone")
    m_vone = P.val("vone")

    chain = bootv
    last_score_read = []
    nxs = 0
    xs_free = {}
    kprep_reads_all = {}

    for g in range(4):
        m_wk = P.dma("sync", wblk[:, :].rearrange("p (dk c) -> p dk c", c=512),
                     wqb.rearrange("(dk p) c -> p dk c", p=128)[
                         :, :, D + g * 512:D + (g + 1) * 512],
                     wait=[m_wqb] + last_score_read, inc="wblk")
        for w in range(8):
            rpar = w % 2
            wrt = [boot, chain]
            m_rt = P.dma("gpsimd", rtb[rpar][:, 0:512],
                         ropeC_hbm[:, w * 512:(w + 1) * 512],
                         wait=wrt, inc=f"rt{rpar}")
            m_rt2 = P.dma("gpsimd", rtb[rpar][:, 512:1024],
                          ropeS_hbm[:, w * 512:(w + 1) * 512],
                          wait=wrt, inc=f"rt{rpar}")
            kms = [None] * 4
            mm = None
            for dk in range(16):
                par = nxs % 2
                ew = [m_xnt, m_xnt2]
                if nxs >= 2:
                    ew.append(xs_free[nxs - 2])
                m_xs = P.dma("sync", xsl[par][:, :],
                             xnt[dk * 128:(dk + 1) * 128,
                                 w * 512:(w + 1) * 512],
                             wait=ew, inc=f"xs{par}")
                for hl in range(4):
                    wm = [m_xs, m_wk]
                    if dk == 0:
                        wm += last_score_read
                        if kprep_reads_all.get(hl) is not None:
                            wm.append(kprep_reads_all[hl])
                    mm = P.op("tensor", "matmul", PB[hl][:, :],
                              wblk[:, dk * 512 + hl * 128:
                                   dk * 512 + (hl + 1) * 128],
                              xsl[par][:, :],
                              start=(dk == 0), stop=(dk == 15),
                              wait=wm,
                              inc=("xf" if hl == 3 else None))
                    if dk == 15:
                        kms[hl] = mm
                xs_free[nxs] = mm
                nxs += 1
            for hl in range(4):
                tabs = (rtb[rpar][:, 0:512], rtb[rpar][:, 512:1024])
                dst = knT[:, hl * T + w * 512: hl * T + (w + 1) * 512]
                chain, reads = emit_a1_prep(
                    P, PB[hl], PB[4][:, :], PB[5], PB[6][:, :], PB[7][:, :],
                    scr, tabs, dst, kms[3], chain, m_tab=m_rt2)
                kprep_reads_all[hl] = chain
        kprep_final = chain
        m_wv = P.dma("sync", wblk[:, :].rearrange("p (dk c) -> p dk c", c=512),
                     wqb.rearrange("(dk p) c -> p dk c", p=128)[
                         :, :, 2 * D + g * 512:2 * D + (g + 1) * 512],
                     wait=[m_wqb, kprep_final], inc="wblk")
        vev_mils = []
        for w in range(8):
            vms = [None] * 4
            mm = None
            for dk in range(16):
                par = nxs % 2
                ew = [m_xnt, m_xnt2]
                if nxs >= 2:
                    ew.append(xs_free[nxs - 2])
                m_xs = P.dma("sync", xsl[par][:, :],
                             xnt[dk * 128:(dk + 1) * 128,
                                 w * 512:(w + 1) * 512],
                             wait=ew, inc=f"xs{par}")
                for sb in range(4):
                    wm = [m_xs, m_wv]
                    if dk == 0:
                        wm.append(kprep_final)
                        if vev_mils:
                            wm.append(vev_mils[-1])
                    mm = P.op("tensor", "matmul", PB[4 + sb][:, :],
                              xsl[par][:, sb * 128:(sb + 1) * 128],
                              wblk[:, dk * 512:(dk + 1) * 512],
                              start=(dk == 0), stop=(dk == 15),
                              wait=wm,
                              inc=("xf" if sb == 3 else None))
                    if dk == 15:
                        vms[sb] = mm
                xs_free[nxs] = mm
                nxs += 1
            for sb in range(4):
                kt = w * 4 + sb
                for hl in range(4):
                    m = P.op("vector", "tensor_copy",
                             out=vg_tile(hl, kt)[:, 0:128],
                             in_=PB[4 + sb][:, hl * 128:(hl + 1) * 128],
                             wait=[vms[3]], inc="vev")
                    vev_mils.append(m)
        m_v_all = vev_mils[-1]
        m_wq = P.dma("sync", wblk[:, :].rearrange("p (dk c) -> p dk c", c=512),
                     wqb.rearrange("(dk p) c -> p dk c", p=128)[
                         :, :, g * 512:(g + 1) * 512],
                     wait=[m_wqb, P.val("vev")], inc="wblk")
        for ow in range(2):
            qms = [None] * 4
            mm = None
            for dk in range(16):
                par = nxs % 2
                ew = [m_xnt, m_xnt2]
                if nxs >= 2:
                    ew.append(xs_free[nxs - 2])
                m_xs = P.dma("sync", xsl[par][:, :],
                             xnto[dk * 128:(dk + 1) * 128,
                                  ow * 512:(ow + 1) * 512],
                             wait=ew, inc=f"xs{par}")
                for hl in range(4):
                    wm = [m_xs, m_wq]
                    if dk == 0:
                        wm.append(P.val("pp"))
                    mm = P.op("tensor", "matmul", PB[hl][:, :],
                              wblk[:, dk * 512 + hl * 128:
                                   dk * 512 + (hl + 1) * 128],
                              xsl[par][:, :],
                              start=(dk == 0), stop=(dk == 15),
                              wait=wm,
                              inc=("xf" if hl == 3 else None))
                    if dk == 15:
                        qms[hl] = mm
                xs_free[nxs] = mm
                nxs += 1
            for hl in range(4):
                tabs = (ropeCo[:, ow * 512:(ow + 1) * 512],
                        ropeSo[:, ow * 512:(ow + 1) * 512])
                dst = qnT[:, hl * TPC + ow * 512: hl * TPC + (ow + 1) * 512]
                chain, reads = emit_a1_prep(
                    P, PB[hl], PB[4][:, :], PB[5], PB[6][:, :], PB[7][:, :],
                    scr, tabs, dst, qms[3], chain)
        m_prep_all = chain
        last_score_read = []
        prev_norm = []
        for hl in range(4):
            for ow in range(2):
                amax = 15 + 16 * ow
                s2_hist = {}
                av_hist = {}
                prev_a4mul = None
                av_last = None
                dn_last = None
                for kt in range(amax + 1):
                    sp = PB[kt % 2]
                    wm = [m_prep_all]
                    if s2_hist.get(kt % 2) is not None:
                        wm.append(s2_hist[kt % 2])
                    m_s = P.op("tensor", "matmul", sp[:, :],
                               knT[:, hl * T + kt * 128:
                                    hl * T + (kt + 1) * 128],
                               qnT[:, hl * TPC + ow * 512:
                                   hl * TPC + (ow + 1) * 512],
                               start=True, stop=True, wait=wm, inc="sc")
                    ws2 = [m_s]
                    if prev_a4mul is not None:
                        ws2.append(prev_a4mul)
                    m_s2 = P.op("scalar", "activation", out=s2b[:, :],
                                in_=sp[:, :], func=AF.Square,
                                wait=ws2, inc="sc")
                    s2_hist[kt % 2] = m_s2
                    a4 = a4b[kt % 2]
                    wa4 = [m_s2]
                    if av_hist.get(kt % 2) is not None:
                        wa4.append(av_hist[kt % 2])
                    m_a4 = P.op("vector", "tensor_mul", a4[:, :], s2b[:, :],
                                s2b[:, :], wait=wa4, inc="sc")
                    prev_a4mul = m_a4
                    fix = m_a4
                    for j in range(4):
                        delta = kt - 4 * (4 * ow + j)
                        if 0 <= delta <= 15:
                            fix = P.op("vector", "tensor_mul",
                                       a4[:, j * 128:(j + 1) * 128],
                                       a4[:, j * 128:(j + 1) * 128],
                                       mask_tile(delta),
                                       wait=[m_a4, boot], inc="sc")
                            prev_a4mul = fix
                    wav = [fix, m_v_all, m_vone]
                    if kt == 0:
                        wav += prev_norm
                    m_av = P.op("tensor", "matmul", PB[2][:, :],
                                vg_tile(hl, kt)[:, 0:128], a4[:, :],
                                start=(kt == 0), stop=(kt == amax),
                                wait=wav, inc="sc")
                    m_dn = P.op("tensor", "matmul", PB[3][0:1, :],
                                vg_tile(hl, kt)[:, 128:129], a4[:, :],
                                start=(kt == 0), stop=(kt == amax),
                                wait=wav, inc="sc")
                    av_hist[kt % 2] = m_dn
                    av_last, dn_last = m_av, m_dn
                m_dc = P.op("vector", "tensor_copy", out=denb[:, :],
                            in_=PB[3][0:1, :], wait=[dn_last], inc="sc")
                m_rc = P.op("vector", "reciprocal", recb[:, :], denb[:, :],
                            wait=[m_dc], inc="sc")
                m_rb = P.op("tensor", "matmul", PB[4][:, :], onesc[:, :],
                            recb[:, :], start=True, stop=True,
                            wait=[m_rc] + prev_norm, inc="sc")
                m_yc = P.op("scalar", "activation", out=ytb[:, :],
                            in_=PB[2][:, :], func=AF.Copy,
                            wait=[av_last], inc="sc")
                m_yf = P.op("vector", "tensor_tensor",
                            y_fm[:, (4 * g + hl) * TPC + ow * 512:
                                 (4 * g + hl) * TPC + (ow + 1) * 512],
                            ytb[:, :], PB[4][:, :], ALU.mult,
                            wait=[m_yc, m_rb], inc="sc")
                prev_norm = [m_dc, m_yc, m_yf]
                last_score_read += prev_norm
        chain = last_score_read[-1]

    wos = blkb
    xop = xbuf
    outb = [sq_scr, outb2]
    m_z_prev = {}
    stt_mils = {}
    stt_all = {}
    out_dma = {}
    for pair in range(4):
        for h in range(16):
            par = h % 2
            w = [m_wob] + last_score_read
            if (pair, h - 2) in m_z_prev:
                w.append(m_z_prev[(pair, h - 2)])
            if pair > 0 and h < 2:
                w.append(m_z_prev[(pair - 1, h + 14)])
            if pair == 0 and h < 2:
                w.append(m_xnt)
            m_wl = P.dma("sync", wos[par][:, :],
                         wob[h * 128:(h + 1) * 128, :], wait=w,
                         inc=f"wo{par}")
            mm = None
            for ti in range(2):
                tto = pair * 2 + ti
                for dq in range(4):
                    wm = [m_wl]
                    if h == 0 and (pair - 1, ti, dq) in stt_mils:
                        wm.append(stt_mils[(pair - 1, ti, dq)])
                    mm = P.op("tensor", "matmul", PB[ti * 4 + dq][:, :],
                              y_fm[:, h * TPC + tto * 128:
                                   h * TPC + (tto + 1) * 128],
                              wos[par][:, dq * 512:(dq + 1) * 512],
                              start=(h == 0), stop=(h == 15),
                              wait=wm,
                              inc=("zpf" if (ti == 1 and dq == 3) else None))
            m_z_prev[(pair, h)] = mm
        m_z_done = m_z_prev[(pair, 15)]
        for ti in range(2):
            tto = pair * 2 + ti
            wxl = []
            if (pair - 1, ti) in stt_all:
                wxl.append(stt_all[(pair - 1, ti)])
            if pair == 0:
                wxl.append(m_xnt)
            m_xl = P.dma("gpsimd", xop[ti][:, :],
                         xin_own[tto * 128:(tto + 1) * 128, :],
                         wait=wxl, inc=f"xol{ti}")
            ms = None
            for dq in range(4):
                w = [m_z_done, m_xl]
                if (pair - 1, ti) in out_dma:
                    w.append(out_dma[(pair - 1, ti)])
                ms = P.op("vector", "tensor_add",
                          outb[ti][:, dq * 512:(dq + 1) * 512],
                          PB[ti * 4 + dq][:, :],
                          xop[ti][:, dq * 512:(dq + 1) * 512],
                          wait=w, inc="stt")
                stt_mils[(pair, ti, dq)] = ms
            stt_all[(pair, ti)] = ms
            m_od = P.dma("gpsimd", xo[tto * 128:(tto + 1) * 128, :],
                         outb[ti][:, :], wait=[ms], inc=f"xodma{ti}")
            out_dma[(pair, ti)] = m_od
    P.op("sync", "nop", wait=[P.val("xodma0"), P.val("xodma1")])


def a1_host_tables():
    p64 = np.zeros((128, 128), np.float32)
    for i in range(128):
        p64[i, (i + 64) % 128] = 1.0
    invf = (1.0 / 10000.0 ** (np.arange(0, 128, 2, np.float32) / 128.0))
    t = np.arange(T, dtype=np.float64)[None, :]
    ang = t * invf.astype(np.float64)[:, None]
    cos = np.cos(ang).astype(np.float32)
    sin = np.sin(ang).astype(np.float32)
    C2 = np.concatenate([cos, cos], 0)
    S2 = np.concatenate([-sin, sin], 0)
    sel = np.zeros((2, 256), np.float32)
    sel[0, 0:128] = 1.0
    sel[1, 128:256] = 1.0
    return p64, C2, S2, sel


def a1_core_tables(C2, S2, r):
    own = np.concatenate([np.arange((r + GSZ * j) * 128, (r + GSZ * j) * 128 + 128)
                          for j in range(NTT)])
    Co = np.ascontiguousarray(C2[:, own])
    So = np.ascontiguousarray(S2[:, own])
    tril = np.tril(np.ones((128, 128), np.float32))
    masks = np.zeros((16, 128, 128), np.float32)
    for d in range(16):
        if d < r:
            masks[d] = 1.0
        elif d == r:
            masks[d] = tril.T
    return Co, So, masks


def build_a1_launch():
    nc = bass.Bass("TRN2", target_bir_lowering=False, debug=False)
    xin = nc.dram_tensor("xin", [T, D], F32, kind="ExternalInput").ap()
    xin_own = nc.dram_tensor("xin_own", [TPC, D], F32, kind="ExternalInput").ap()
    wqkv = nc.dram_tensor("wqkv", [D, 3 * D], F32, kind="ExternalInput").ap()
    wout = nc.dram_tensor("wout", [D, D], F32, kind="ExternalInput").ap()
    lnsb = nc.dram_tensor("lnsb", [128, 2], F32, kind="ExternalInput").ap()
    cst = nc.dram_tensor("cst", [128, NCONST], F32, kind="ExternalInput").ap()
    ident = nc.dram_tensor("ident", [128, 128], F32, kind="ExternalInput").ap()
    p64 = nc.dram_tensor("p64", [128, 128], F32, kind="ExternalInput").ap()
    ropeC = nc.dram_tensor("ropeC", [128, T], F32, kind="ExternalInput").ap()
    ropeS = nc.dram_tensor("ropeS", [128, T], F32, kind="ExternalInput").ap()
    ropeCo = nc.dram_tensor("ropeCo", [128, TPC], F32, kind="ExternalInput").ap()
    ropeSo = nc.dram_tensor("ropeSo", [128, TPC], F32, kind="ExternalInput").ap()
    sel = nc.dram_tensor("sel", [2, 256], F32, kind="ExternalInput").ap()
    masks = nc.dram_tensor("masks", [16, 128, 128], F32, kind="ExternalInput").ap()
    xo = nc.dram_tensor("xo", [TPC, D], F32, kind="ExternalOutput").ap()
    P = Prog(nc)
    emit_a1(P, xin, xin_own, xo, wqkv, wout, lnsb, cst, ident, p64,
            ropeC, ropeS, ropeCo, ropeSo, sel, masks, "a1")
    P.emit_and_close()
    return nc


def run_a1_stage(x, w_qkv, w_out, ln_scale, ln_bias):
    nc = build_a1_launch()
    cstv = host_consts()
    identv = np.eye(128, dtype=np.float32)
    p64v, C2, S2, selv = a1_host_tables()
    lnsbv = np.stack([ln_scale, ln_bias], 1).astype(np.float32)
    in_maps = []
    for c in range(NCORE):
        r = c % GSZ
        xb = x[c // GSZ]
        Co, So, maskv = a1_core_tables(C2, S2, r)
        in_maps.append({
            "xin": np.ascontiguousarray(xb), "xin_own": _gather_own(xb, c),
            "wqkv": w_qkv, "wout": w_out, "lnsb": lnsbv, "cst": cstv,
            "ident": identv, "p64": p64v, "ropeC": C2, "ropeS": S2,
            "ropeCo": Co, "ropeSo": So, "sel": selv, "masks": maskv,
        })
    res = run_bass_kernel_spmd(nc, in_maps, core_ids=list(range(NCORE)))
    out = np.empty_like(x)
    for c in range(NCORE):
        _scatter_own(out[c // GSZ], np.asarray(res.results[c]["xo"]), c)
    return out



def emit_f0(P, Xs, cst, identb, wq_in, keys_in, e_down, e_up, iota64_in,
            tag, x_ready, out_cb):
    nc = P.nc
    xn_all = P.sbuf(f"{tag}_xn", [128, NTT * D], BF16)
    xnT = P.sbuf(f"{tag}_xnT", [128, 16 * TPC], BF16)
    wqb = P.sbuf(f"{tag}_wqb", [128, 16 * 512], BF16)
    keysb = P.sbuf(f"{tag}_keysb", [128, 4 * 128], BF16)
    iota64 = P.sbuf(f"{tag}_iota", [128, 64], F32)
    qT = P.sbuf(f"{tag}_qT", [128, 4 * TPC], BF16)
    simb = P.sbuf(f"{tag}_simb", [128, 8 * 128], F32)
    s12 = P.sbuf(f"{tag}_s12", [128, 4 * 16], F32)
    i12 = P.sbuf(f"{tag}_i12", [128, 4 * 16], U32)
    i12f = P.sbuf(f"{tag}_i12f", [128, 4 * 16], F32)
    allsc = P.sbuf(f"{tag}_allsc", [128, 4 * 64], F32)
    allix = P.sbuf(f"{tag}_allix", [128, 4 * 64], F32)
    fsc = P.sbuf(f"{tag}_fsc", [128, 32], F32)
    fpos = P.sbuf(f"{tag}_fpos", [128, 32], U32)
    fposf = P.sbuf(f"{tag}_fposf", [128, 32], F32)
    ohb = P.sbuf(f"{tag}_ohb", [128, 2 * 512], F32)
    fidxf = P.sbuf(f"{tag}_fidxf", [128, 32], F32)
    fidx = P.sbuf(f"{tag}_fidx", [128, NTT * 32], I32)
    hid = P.sbuf(f"{tag}_hid", [128, 32], F32)
    coef = P.sbuf(f"{tag}_coef", [128, NTT * 32], F32)
    relu_s = P.sbuf(f"{tag}_relu", [128, 32], F32)
    silu_h = P.sbuf(f"{tag}_silu", [128, 32], F32)
    wd = [P.sbuf(f"{tag}_wd{i}", [128, D], BF16) for i in range(2)]
    wu = [P.sbuf(f"{tag}_wu{i}", [128, D], BF16) for i in range(2)]
    prodb = None
    accb = [P.sbuf(f"{tag}_acc{i}", [128, D], F32) for i in range(2)]
    scrV = ohb[:, :].bitcast(BF16)[:, 0:D]
    scrP = simb[:, :].bitcast(BF16)[:, 0:D]
    prodb = [xnT[:, 0:D], xnT[:, D:2 * D]]
    wd += [wqb[:, 0:D], wqb[:, D:2 * D]]
    wu += [wqb[:, 2 * D:3 * D], wqb[:, 3 * D:4 * D]]
    psA = P.psum(f"{tag}_psA", [128, 512], F32)
    psB = P.psum(f"{tag}_psB", [128, 512], F32)
    psTs = [psA[:, :].bitcast(BF16)[:, 0:128],
            psB[:, :].bitcast(BF16)[:, 0:128]]

    m_wq = P.dma("gpsimd", wqb[:, :].rearrange("p (dk c) -> p dk c", c=512),
                 wq_in.rearrange("(dk p) c -> p dk c", p=128), inc="f0wq")
    m_keys = P.dma("gpsimd", keysb[:, :], keys_in, inc="f0ky")
    m_iota = P.dma("gpsimd", iota64[:, :], iota64_in, inc="f0io")

    def xnT_dst(tt, dc):
        return xnT[:, dc * TPC + tt * 128: dc * TPC + (tt + 1) * 128]

    m_xnT, xn_dones = emit_ln_to_xnT(
        P, Xs, cst, identb, xnT_dst, tag + "L", x_ready, [None, None], psTs,
        keep_xn=lambda tt: xn_all[:, tt * D:(tt + 1) * D])

    qev = []
    for qt in range(4):
        for twi in range(2):
            k8 = qt * 2 + twi
            ps = psA if k8 % 2 == 0 else psB
            mm = None
            for dk in range(16):
                w = [m_wq]
                if dk == 0:
                    w.append(m_xnT)
                    if len(qev) >= 2:
                        w.append(qev[-2])
                mm = P.op("tensor", "matmul", ps[:, :],
                          wqb[:, dk * 512 + qt * 128:dk * 512 + (qt + 1) * 128],
                          xnT[:, dk * TPC + twi * 512:
                              dk * TPC + (twi + 1) * 512],
                          start=(dk == 0), stop=(dk == 15), wait=w, inc="f0qp")
            m_ev = P.op("scalar", "activation",
                        out=qT[:, qt * TPC + twi * 512:
                               qt * TPC + (twi + 1) * 512],
                        in_=ps[:, :], func=AF.Copy, wait=[mm], inc="f0qe")
            qev.append(m_ev)
    m_q_all = qev[-1]

    sim_ev_hist = {}
    dn_free = {}
    dn_act = {}
    up_free = {}
    res_free = {}
    prev_res = None
    prev_out_dma = None
    up_ch = None
    tk_chain = m_iota
    hid_chain = None
    prev_dn_pool = None
    pend_up = None
    for tt in range(NTT):
        xn_tile = xn_all[:, tt * D:(tt + 1) * D]
        sim_mms = []
        for hf in range(8):
            h, half = hf // 2, hf % 2
            ps = psA if hf < 4 else psB
            po = (hf % 4) * 128
            w = [m_q_all, m_keys]
            bank = 0 if hf < 4 else 1
            if (bank,) in sim_ev_hist:
                w.append(sim_ev_hist[(bank,)])
            lhs = qT[half * 64: half * 64 + 64,
                     h * TPC + tt * 128: h * TPC + (tt + 1) * 128]
            mm = P.op("tensor", "matmul", ps[:, po:po + 128], lhs,
                      keysb[half * 64:(half + 1) * 64,
                            h * 128:(h + 1) * 128],
                      start=True, stop=True, wait=w, inc="f0sm")
            wev = [mm]
            if prev_dn_pool is not None:
                wev.append(prev_dn_pool)
            m_ev = P.op("vector", "tensor_copy",
                        out=simb[:, hf * 128:(hf + 1) * 128],
                        in_=ps[:, po:po + 128], wait=wev, inc="f0se")
            sim_ev_hist[(bank,)] = m_ev
            sim_mms.append(m_ev)
        ch = tk_chain
        for h in range(4):
            for half in range(2):
                hf = h * 2 + half
                co = h * 16 + half * 8
                m_mx = P.op("vector", "max", out=s12[:, co:co + 8],
                            in_=simb[:, hf * 128:(hf + 1) * 128],
                            wait=[sim_mms[hf], ch], inc="f0tk")
                m_ix = P.op("vector", "max_index", out=i12[:, co:co + 8],
                            in_max=s12[:, co:co + 8],
                            in_values=simb[:, hf * 128:(hf + 1) * 128],
                            wait=[m_mx], inc="f0tk")
                ch = m_ix
        m_cast = P.op("vector", "tensor_copy", out=i12f[:, :], in_=i12[:, :],
                      wait=[ch], inc="f0tk")
        ch = m_cast
        for h in range(4):
            s1 = s12[:, h * 16:h * 16 + 8].rearrange(
                "p a -> p a ()").to_broadcast([128, 8, 8])
            s2 = s12[:, h * 16 + 8:h * 16 + 16].rearrange(
                "p b -> p () b").to_broadcast([128, 8, 8])
            m_as = P.op("vector", "tensor_tensor",
                        allsc[:, h * 64:(h + 1) * 64].rearrange(
                            "p (a b) -> p a b", b=8),
                        s1, s2, ALU.add, wait=[ch], inc="f0tk")
            i1 = i12f[:, h * 16:h * 16 + 8].rearrange(
                "p a -> p a ()").to_broadcast([128, 8, 8])
            i2 = i12f[:, h * 16 + 8:h * 16 + 16].rearrange(
                "p b -> p () b").to_broadcast([128, 8, 8])
            m_ai = P.op("vector", "scalar_tensor_tensor",
                        allix[:, h * 64:(h + 1) * 64].rearrange(
                            "p (a b) -> p a b", b=8),
                        i1, 128.0, i2, ALU.mult, ALU.add,
                        wait=[m_as], inc="f0tk")
            m_f = P.op("vector", "max", out=fsc[:, h * 8:(h + 1) * 8],
                       in_=allsc[:, h * 64:(h + 1) * 64],
                       wait=[m_ai], inc="f0tk")
            m_fp = P.op("vector", "max_index", out=fpos[:, h * 8:(h + 1) * 8],
                        in_max=fsc[:, h * 8:(h + 1) * 8],
                        in_values=allsc[:, h * 64:(h + 1) * 64],
                        wait=[m_f], inc="f0tk")
            ch = m_fp
        m_cp = P.op("vector", "tensor_copy", out=fposf[:, :], in_=fpos[:, :],
                    wait=[ch], inc="f0tk")
        m_rd = None
        for hh in range(2):
            oh4 = ohb[:, :].rearrange("p (h s c) -> p h s c", s=8, c=64)
            fp2 = fposf[:, hh * 16:(hh + 1) * 16].rearrange(
                "p (h s) -> p h s ()", s=8).to_broadcast([128, 2, 8, 64])
            io2 = iota64[:, :].rearrange("p c -> p () () c").to_broadcast(
                [128, 2, 8, 64])
            woh = [m_cp] + ([m_rd] if m_rd else [])
            m_oh = P.op("vector", "tensor_tensor", oh4, fp2, io2,
                        ALU.is_equal, wait=woh, inc="f0tk")
            aix4 = allix[:, hh * 128:(hh + 1) * 128].rearrange(
                "p (h c) -> p h () c", c=64).to_broadcast([128, 2, 8, 64])
            m_pr = P.op("vector", "tensor_tensor", oh4, oh4,
                        aix4, ALU.mult, wait=[m_oh], inc="f0tk")
            m_rd = P.op("vector", "tensor_reduce",
                        fidxf[:, hh * 16:(hh + 1) * 16].rearrange(
                            "p (h s) -> p h s", s=8),
                        oh4, axis=AX, op=ALU.add, wait=[m_pr], inc="f0tk")
        m_fi = P.op("vector", "tensor_copy",
                    out=fidx[:, tt * 32:(tt + 1) * 32], in_=fidxf[:, :],
                    wait=[m_rd], inc="f0tk")
        wh = [hid_chain] if hid_chain else []
        m_rl = P.op("vector", "tensor_scalar_max", relu_s[:, :], fsc[:, :],
                    0.0, wait=[m_rd] + wh, inc="f0hd")
        for sl in range(32):
            par = sl % 4
            wg = [m_fi]
            if par in dn_free:
                wg.append(dn_free[par])
            m_g = P.add("gpsimd",
                        (lambda o=wd[par], idxap=fidx[:, tt * 32 + sl:
                                                      tt * 32 + sl + 1]:
                         lambda e: e.indirect_dma_start(
                             out=o[:, :], out_offset=None,
                             in_=e_down[:, :],
                             in_offset=IndirectOffsetOnAxis(ap=idxap, axis=0)))(),
                        wait=wg, inc=f"f0gd{par}", inc_amt=16)
            pp = sl % 2
            wp = [m_g]
            if pp in dn_act:
                wp.append(dn_act[pp])
            m_p = P.op("vector", "tensor_tensor", prodb[pp][:, :],
                       wd[par][:, :], xn_tile, ALU.mult, wait=wp, inc="f0dn")
            wa = [m_p] + wh
            if (1 - pp) in dn_act:
                wa.append(dn_act[1 - pp])
            m_a = P.op("scalar", "activation", out=scrV[:, :],
                       in_=prodb[pp][:, :], func=AF.Copy,
                       accum_out=hid[:, sl:sl + 1], wait=wa,
                       inc="f0dna")
            dn_act[pp] = m_a
            dn_free[par] = m_p
        m_dn_all = [dn_act[0], dn_act[1]]
        prev_dn_pool = None
        m_si = P.op("scalar", "activation", out=silu_h[:, :], in_=hid[:, :],
                    func=AF.Silu, wait=m_dn_all, inc="f0hd")
        m_cf = P.op("vector", "tensor_mul", coef[:, tt * 32:(tt + 1) * 32],
                    silu_h[:, :], relu_s[:, :], wait=[m_si, m_rl], inc="f0hd")
        hid_chain = m_cf
        tk_chain = m_cf

        def emit_up(ut, u_fi, u_cf):
            nonlocal up_ch, prev_res, prev_out_dma
            for sl in range(32):
                par = sl % 4
                wg = [u_fi]
                if par in up_free:
                    wg += up_free[par]
                m_g = P.add("gpsimd",
                            (lambda o=wu[par], idxap=fidx[:, ut * 32 + sl:
                                                          ut * 32 + sl + 1]:
                             lambda e: e.indirect_dma_start(
                                 out=o[:, :], out_offset=None,
                                 in_=e_up[:, :],
                                 in_offset=IndirectOffsetOnAxis(
                                     ap=idxap, axis=0)))(),
                            wait=wg, inc=f"f0gu{par}", inc_amt=16)
                cc = coef[:, ut * 32 + sl:ut * 32 + sl + 1]
                aw = [m_g, u_cf]
                if up_ch is not None:
                    aw += up_ch
                if sl <= 1 and prev_res is not None:
                    aw.append(prev_res)
                if sl == 0 and prev_out_dma is not None:
                    aw.append(prev_out_dma)
                src_acc = accb[(sl + 1) % 2]
                dst_acc = accb[sl % 2]
                if sl == 0:
                    m_u = P.op("vector", "tensor_scalar",
                               dst_acc[:, :], wu[par][:, :], cc, None,
                               ALU.mult, wait=aw, inc="f0up")
                else:
                    m_u = P.op("vector", "scalar_tensor_tensor",
                               dst_acc[:, :], wu[par][:, :], cc,
                               src_acc[:, :], ALU.mult, ALU.add,
                               wait=aw, inc="f0up")
                up_ch = [m_u]
                up_free[par] = [m_u]
            wr = list(up_ch) + list(x_ready[ut])
            m_res = P.op("vector", "scalar_tensor_tensor", accb[0][:, :],
                         accb[1][:, :], 0.5, Xs[ut], ALU.mult, ALU.add,
                         wait=wr, inc="f0rs")
            prev_res = m_res
            m_out = out_cb(ut, accb[0], m_res)
            prev_out_dma = m_out

        if pend_up is not None:
            emit_up(*pend_up)
        pend_up = (tt, m_fi, m_cf)
    emit_up(*pend_up)


def build_f0_launch():
    nc = bass.Bass("TRN2", target_bir_lowering=False, debug=False)
    xin = nc.dram_tensor("xin", [TPC, D], F32, kind="ExternalInput").ap()
    wq = nc.dram_tensor("wq", [D, 512], F32, kind="ExternalInput").ap()
    keys = nc.dram_tensor("keys", [128, 512], F32, kind="ExternalInput").ap()
    ed = nc.dram_tensor("ed", [16384, D], F32, kind="ExternalInput").ap()
    eu = nc.dram_tensor("eu", [16384, D], F32, kind="ExternalInput").ap()
    io64 = nc.dram_tensor("io64", [128, 64], F32, kind="ExternalInput").ap()
    cst_in = nc.dram_tensor("cst", [128, NCONST], F32, kind="ExternalInput").ap()
    ident_in = nc.dram_tensor("ident", [128, 128], F32, kind="ExternalInput").ap()
    xo = nc.dram_tensor("xo", [TPC, D], F32, kind="ExternalOutput").ap()

    P = Prog(nc)
    Xbuf = P.sbuf("Xbuf", [128, NTT * D], F32)
    Xs = [Xbuf[:, tt * D:(tt + 1) * D] for tt in range(NTT)]
    cst = P.sbuf("cstt", [128, NCONST], F32)
    identb = P.sbuf("identb", [128, 128], BF16)
    P.dma("gpsimd", cst[:, :], cst_in, inc="boot")
    P.dma("gpsimd", identb[:, :], ident_in, inc="boot")
    for tt in range(NTT):
        P.dma("gpsimd", Xs[tt], xin[tt * 128:(tt + 1) * 128, :], inc="xld")
    x_ready = [[P.val("xld"), P.val("boot")] for _ in range(NTT)]

    oc = [0]

    def out_cb(tt, src, mil):
        oc[0] += 1
        return P.dma("gpsimd", xo[tt * 128:(tt + 1) * 128, :], src[:, :],
                     wait=[mil], inc=f"f0o{oc[0] % 2}")

    emit_f0(P, Xs, cst, identb, wq, keys, ed, eu, io64, "f0", x_ready, out_cb)
    P.add("sync", lambda e: e.nop(), wait=[P.val("f0o0"), P.val("f0o1")])
    P.emit_and_close()
    return nc


def f0_host_tables():
    io64 = np.tile(np.arange(64, dtype=np.float32)[None, :], (128, 1))
    return io64


def run_f0_stage(x, Wq, keys, e_down, e_up):
    nc = build_f0_launch()
    cstv = host_consts()
    identv = np.eye(128, dtype=np.float32)
    io64 = f0_host_tables()
    keys_t = np.ascontiguousarray(
        keys.transpose(2, 3, 0, 1).reshape(128, 512))
    in_maps = []
    for c in range(NCORE):
        xb = x[c // GSZ]
        in_maps.append({"xin": _gather_own(xb, c), "wq": Wq, "keys": keys_t,
                        "ed": e_down, "eu": e_up, "io64": io64, "cst": cstv,
                        "ident": identv})
    res = run_bass_kernel_spmd(nc, in_maps, core_ids=list(range(NCORE)))
    out = np.empty_like(x)
    for c in range(NCORE):
        _scatter_own(out[c // GSZ], np.asarray(res.results[c]["xo"]), c)
    return out



A0_BIG = 1.0e4


def emit_a0_prep(P, bank, scr, nw_ap, bias_ap, tabs, dst, m_src, chain,
                 m_tab=None):
    kb, ks = scr["kb"], scr["ks"]
    t1b, t2b, t3b = scr["t1b"], scr["t2b"], scr["t3b"]
    stb, veb, rsb = scr["stb"], scr["veb"], scr["rsb"]
    cstt = scr["cst"]
    reads = []
    ma = P.op("vector", "tensor_scalar", kb[:, :], bank[:, :],
              bias_ap, None, ALU.add, wait=[m_src, chain], inc="a0pp")
    msq = P.op("scalar", "activation", out=ks[:, :], in_=kb[:, :],
               func=AF.Square, wait=[ma], inc="a0pp")
    ms2 = P.op("tensor", "matmul", bank[0:1, :], scr["ones1"][:, :], ks[:, :],
               start=True, stop=True, wait=[msq], inc="a0pp")
    mst = P.op("vector", "tensor_copy", out=stb[:, 0:512], in_=bank[0:1, :],
               wait=[ms2], inc="a0pp")
    mb2 = P.op("tensor", "matmul", bank[:, :], scr["onesc"][:, :],
               stb[:, 0:512], start=True, stop=True, wait=[mst], inc="a0pp")
    mve = P.op("vector", "tensor_scalar", veb[:, :], bank[:, :],
               1.0 / 128.0, None, ALU.mult, wait=[mb2], inc="a0pp")
    mln = P.op("scalar", "activation", out=veb[:, :], in_=veb[:, :],
               func=AF.Ln, bias=cstt[:, C_EPS6:C_EPS6 + 1],
               wait=[mve], inc="a0pp")
    mrs = P.op("scalar", "activation", out=rsb[:, :], in_=veb[:, :],
               func=AF.Exp, scale=cstt[:, C_NHALF:C_NHALF + 1],
               wait=[mln], inc="a0pp")
    mn1 = P.op("vector", "tensor_mul", t1b[:, :], kb[:, :], rsb[:, :],
               wait=[mrs], inc="a0pp")
    mn2 = P.op("vector", "tensor_scalar", t3b[:, :], t1b[:, :], nw_ap, None,
               ALU.mult, wait=[mn1], inc="a0pp")
    msw = P.op("tensor", "matmul", bank[:, :], scr["ppair"][:, :],
               t3b[:, :], start=True, stop=True, wait=[mn2, mve], inc="a0pp")
    Ct, St = tabs
    wt1 = [mn2]
    if m_tab:
        wt1 += m_tab if isinstance(m_tab, list) else [m_tab]
    m1 = P.op("vector", "tensor_tensor", t1b[:, :], t3b[:, :], Ct, ALU.mult,
              wait=wt1, inc="a0pp")
    m2_ = P.op("vector", "scalar_tensor_tensor", t2b[:, :], bank[:, :], 1.0,
               St, ALU.mult, ALU.mult, wait=[msw, m1], inc="a0pp")
    reads.append(m2_)
    mout = P.op("vector", "tensor_add", dst, t1b[:, :], t2b[:, :],
                wait=[m2_], inc="a0pp")
    return mout, reads


def emit_a0(P, xin, xin_own, xo, wq_in, bq_r, wdn_in, bdn_r, wup_in, bupk_r,
            bupv_bc, wo_in, bo_bc, qn_r, kn_r, gw_in, gb_r, cst_in, ident_in,
            ppair_in, ropeC_hbm, ropeS_hbm, ropeCo_in, ropeSo_in,
            mc_in, eyem_in, dtab_in, maskA_in, tag):
    nc = P.nc
    ND = 44
    xnt = nc.dram_tensor(f"{tag}_xnt", [D, T], BF16, kind="Internal").ap()
    xnto = nc.dram_tensor(f"{tag}_xnto", [D, TPC], BF16, kind="Internal").ap()
    wqb = nc.dram_tensor(f"{tag}_wqb", [D, D], BF16, kind="Internal").ap()
    wob = nc.dram_tensor(f"{tag}_wob", [D, D], BF16, kind="Internal").ap()

    identb = P.sbuf(f"{tag}_identb", [128, 128], BF16)
    ppairb = P.sbuf(f"{tag}_ppairb", [128, 128], BF16)
    cstt = P.sbuf(f"{tag}_cstt", [128, NCONST], F32)
    ones1 = P.sbuf(f"{tag}_ones1", [128, 1], BF16)
    onesc = P.sbuf(f"{tag}_onesc", [1, 128], F32)
    onescb = P.sbuf(f"{tag}_onescb", [1, 128], BF16)
    bqb = P.sbuf(f"{tag}_bqb", [128, 16], F32)
    bdnb = P.sbuf(f"{tag}_bdnb", [128, 4], F32)
    bupkb = P.sbuf(f"{tag}_bupkb", [128, 4], F32)
    bupvb = P.sbuf(f"{tag}_bupvb", [128, 512], BF16)
    qnwb = P.sbuf(f"{tag}_qnwb", [128, 1], F32)
    knwb = P.sbuf(f"{tag}_knwb", [128, 1], F32)
    gbb = P.sbuf(f"{tag}_gbb", [128, 16], F32)
    mcb = P.sbuf(f"{tag}_mcb", [128, 16 * 128], BF16)
    eyeb = P.sbuf(f"{tag}_eyeb", [128, 16 * 128], BF16)
    dtb = P.sbuf(f"{tag}_dtb", [128, ND * 128], BF16)
    wds = [P.sbuf(f"{tag}_wds{i}", [128, 512], BF16) for i in range(2)]
    gwt = [P.sbuf(f"{tag}_gwt{i}", [128, 128], BF16) for i in range(2)]

    for dst_, src_ in ((identb, ident_in), (ppairb, ppair_in), (cstt, cst_in),
                       (bqb, bq_r), (bdnb, bdn_r), (bupkb, bupk_r),
                       (bupvb, bupv_bc), (qnwb, qn_r),
                       (knwb, kn_r), (gbb, gb_r)):
        P.dma("gpsimd", dst_[:, :], src_, inc="boot")
    P.dma("gpsimd", mcb[:, :].rearrange("p (n c) -> p n c", c=128),
          mc_in.rearrange("n p c -> p n c"), inc="boot")
    P.dma("gpsimd", eyeb[:, :].rearrange("p (n c) -> p n c", c=128),
          eyem_in.rearrange("n p c -> p n c"), inc="boot")
    P.dma("gpsimd", dtb[:, :].rearrange("p (n c) -> p n c", c=128),
          dtab_in.rearrange("n p c -> p n c"), inc="boot")
    mAb = None
    P.op("vector", "memset", ones1[:, :], 1.0, inc="bootv")
    P.op("vector", "memset", onesc[:, :], 1.0, inc="bootv")
    P.op("vector", "memset", onescb[:, :], 1.0, inc="bootv")
    boot = P.val("boot") + P.val("bootv")

    m_wq = P.dma("gpsimd", wqb[:, :], wq_in, wait=[boot], inc="wcq")
    m_wo = P.dma("gpsimd", wob[:, :], wo_in, wait=[boot], inc="wco")

    def mt(buf, d):
        return buf[:, d * 128:(d + 1) * 128]

    PB = [P.psum(f"{tag}_pb{i}", [128, 512], F32) for i in range(8)]
    psT = [PB[0][:, :].bitcast(BF16)[:, 0:128],
           PB[1][:, :].bitcast(BF16)[:, 0:128]]

    AR1 = P.sbuf(f"{tag}_ar1", [128, 8192], F32)
    AR2 = P.sbuf(f"{tag}_ar2", [128, 16384], BF16)
    y_fm = P.sbuf(f"{tag}_yfm", [128, 16 * TPC], BF16)
    knT = P.sbuf(f"{tag}_knT", [128, 4 * T], BF16)
    VW = 130
    Vg = P.sbuf(f"{tag}_Vg", [128, 4 * NB_T * VW], BF16)
    qnT = P.sbuf(f"{tag}_qnT", [128, 2 * TPC], BF16)
    wsl = [P.sbuf(f"{tag}_wsl{i}", [128, 1024], BF16) for i in range(2)]
    xsl = [P.sbuf(f"{tag}_xsl{i}", [128, 512], BF16) for i in range(2)]
    ynb = P.sbuf(f"{tag}_ynb", [128, 512], BF16)
    denb = P.sbuf(f"{tag}_denb", [1, 512], F32)
    ytb = P.sbuf(f"{tag}_ytb", [128, 512], BF16)
    gsb = P.sbuf(f"{tag}_gsb", [128, 512], BF16)
    scr = {
        "kb": P.sbuf(f"{tag}_kb", [128, 512], BF16),
        "ks": ynb[:, :],
        "t1b": ytb[:, :],
        "t2b": gsb[:, :],
        "t3b": P.sbuf(f"{tag}_t3b", [128, 512], BF16),
        "stb": denb[:, :],
        "veb": P.sbuf(f"{tag}_veb", [128, 512], BF16),
        "rsb": P.sbuf(f"{tag}_rsb", [128, 512], BF16),
        "cst": cstt, "ones1": ones1, "onesc": onesc, "ppair": ppairb,
    }
    Tb = P.sbuf(f"{tag}_Tb", [128, 512], BF16)
    Cb = scr["kb"]
    Zb = scr["rsb"]
    Eb = [P.sbuf(f"{tag}_Eb{i}", [128, 512], BF16) for i in range(2)]
    recb = AR1[0:1, 6656:7168]
    m8b = P.sbuf(f"{tag}_m8b", [128, 8], F32)
    st1 = P.sbuf(f"{tag}_st1", [128, 16], F32)
    sq_scr = knT[:, 0:D]

    xbuf = [AR1[:, 0:2048], AR1[:, 2048:4096]]
    ar1b = AR1[:, :].bitcast(BF16)
    xnb = [ar1b[:, 8192:10240], ar1b[:, 10240:12288]]
    blkb = [ar1b[:, 12288:14336], ar1b[:, 14336:16384]]
    selA = ar1b[:, 0:4096]
    m01u = AR1[:, :].bitcast(U8)[:, 8192:8192 + 144 * 128]
    latT = AR2[:, :]
    mstore = AR2[:, :]
    ar2f = AR2[:, :].bitcast(F32)
    wos = [AR2[:, 0:2048], AR2[:, 2048:4096]]
    xop2 = [ar2f[:, 2048:4096], ar2f[:, 4096:6144]]
    outb2 = [ar2f[:, 6144:8192], AR1[:, 4096:6144]]

    mAb = y_fm[:, 0:2048]
    m_mA = P.dma("gpsimd", mAb.rearrange("p (n c) -> p n c", c=128),
                 maskA_in.rearrange("n p c -> p n c"), inc="boot2")

    def m01_tile(j, kt):
        idx = sum(4 * (jj + 1) for jj in range(j)) + kt
        return m01u[:, idx * 128:(idx + 1) * 128]

    def vg_tile(g, kt):
        o = (g * NB_T + kt) * VW
        return Vg[:, o:o + VW]

    for g in range(4):
        for kt in range(NB_T):
            P.op("gpsimd", "memset", vg_tile(g, kt)[:, 128:130], 0.0,
                 wait=[P.val("vone")], inc="vone")
            P.op("gpsimd", "memset", vg_tile(g, kt)[:, 128:129], 1.0,
                 wait=[P.val("vone")], inc="vone")
    m_vone = P.val("vone")

    m_xnt = emit_stream_ln(P, xin, xin_own, xnt, xnto, xbuf, xnb, blkb, st1,
                           sq_scr, cstt, identb, psT, boot, tag)

    chain = P.val("bootv")
    nxs = 0
    xs_free = {}
    wd_free = {}
    nwd = 0
    lat_ev = []
    for w in range(8):
        lms = None
        for dk in range(16):
            par = nxs % 2
            ew = [m_xnt]
            if nxs >= 2:
                ew.append(xs_free[nxs - 2])
            m_xs = P.dma("sync", xsl[par][:, :],
                         xnt[dk * 128:(dk + 1) * 128, w * 512:(w + 1) * 512],
                         wait=ew, inc=f"a0xs{par}")
            wdp = nwd % 2
            eww = [boot]
            if nwd >= 2:
                eww.append(wd_free[nwd - 2])
            m_wd = P.dma("gpsimd", wds[wdp][:, :],
                         wdn_in[dk * 128:(dk + 1) * 128, :],
                         wait=eww, inc=f"a0wd{wdp}")
            for lt in range(4):
                wm = [m_xs, m_wd]
                if dk == 0 and len(lat_ev) >= 4:
                    wm.append(lat_ev[-1])
                lms = P.op("tensor", "matmul", PB[lt][:, :],
                           wds[wdp][:, lt * 128:(lt + 1) * 128],
                           xsl[par][:, :], start=(dk == 0), stop=(dk == 15),
                           wait=wm, inc=("a0xf" if lt == 3 else None))
            xs_free[nxs] = lms
            wd_free[nwd] = lms
            nxs += 1
            nwd += 1
        for lt in range(4):
            m_ev = P.op("vector", "tensor_scalar",
                        latT[:, lt * T + w * 512: lt * T + (w + 1) * 512],
                        PB[lt][:, :], bdnb[:, lt:lt + 1], None, ALU.add,
                        wait=[lms], inc="a0lat")
            lat_ev.append(m_ev)
    m_lat = lat_ev[-1]

    kprep_chain = chain
    wu_free = {}
    for w in range(8):
        wrt = [boot, kprep_chain, P.val("a0xf")]
        m_rt = P.dma("gpsimd", xsl[0][:, :],
                     ropeC_hbm[:, w * 512:(w + 1) * 512], wait=wrt,
                     inc="a0rt0")
        m_rt2 = P.dma("gpsimd", xsl[1][:, :],
                      ropeS_hbm[:, w * 512:(w + 1) * 512], wait=wrt,
                      inc="a0rt1")
        kmm = [None] * 4
        vmm = [None] * 4
        for lc in range(4):
            wpar = lc % 2
            eww = [boot]
            if w > 0 or lc >= 2:
                eww.append(wu_free[(w, lc - 2) if lc >= 2 else (w - 1, lc + 2)])
            m_wu = P.dma("gpsimd", wsl[wpar][:, :],
                         wup_in[lc * 128:(lc + 1) * 128, :],
                         wait=eww, inc=f"a0wu{wpar}")
            for g in range(4):
                wm = [m_lat, m_wu]
                if lc == 0:
                    wm.append(kprep_chain)
                kmm[g] = P.op("tensor", "matmul", PB[g][:, :],
                              wsl[wpar][:, g * 128:(g + 1) * 128],
                              latT[:, lc * T + w * 512:
                                   lc * T + (w + 1) * 512],
                              start=(lc == 0), stop=(lc == 3),
                              wait=wm, inc=("a0kp" if g == 3 else None))
            for sb in range(4):
                wm = [m_lat, m_wu]
                if lc == 0 and w >= 1:
                    wm.append(vev_last)
                vmm[sb] = P.op("tensor", "matmul", PB[4 + sb][:, :],
                               latT[:, lc * T + w * 512 + sb * 128:
                                    lc * T + w * 512 + (sb + 1) * 128],
                               wsl[wpar][:, 512:1024],
                               start=(lc == 0), stop=(lc == 3),
                               wait=wm, inc=("a0vp" if sb == 3 else None))
            wu_free[(w, lc)] = vmm[3]
        for sb in range(4):
            kt = w * 4 + sb
            for g in range(4):
                m = P.op("vector", "tensor_add",
                         vg_tile(g, kt)[:, 0:128],
                         PB[4 + sb][:, g * 128:(g + 1) * 128],
                         bupvb[:, g * 128:(g + 1) * 128],
                         wait=[vmm[3], boot], inc="a0vev")
                vev_last = m
        for g in range(4):
            tabs = (xsl[0][:, :], xsl[1][:, :])
            dst = knT[:, g * T + w * 512: g * T + (w + 1) * 512]
            kprep_chain, _ = emit_a0_prep(
                P, PB[g], scr, knwb[:, 0:1], bupkb[:, g:g + 1], tabs, dst,
                kmm[3], kprep_chain, m_tab=[m_rt, m_rt2])
    m_kT = kprep_chain
    m_v_all = vev_last

    prev_scores = []
    sel_done = None
    m01_done = None
    out_mils = {}
    for hg in range(8):
        qprep_chain = m_kT if hg == 0 else prev_scores[-1]
        for ow in range(2):
            nw = 0
            wq_free = {}
            qmm = [None] * 2
            for dk in range(16):
                wpar = nw % 2
                eww = [m_wq, qprep_chain, m_v_all]
                if nw >= 2:
                    eww.append(wq_free[nw - 2])
                m_ws = P.dma("sync", wsl[wpar][:, 0:256],
                             wqb[dk * 128:(dk + 1) * 128,
                                 hg * 256:(hg + 1) * 256],
                             wait=eww, inc=f"a0wq{wpar}")
                par = nxs % 2
                ew = [m_xnt, qprep_chain]
                if nxs >= 2:
                    ew.append(xs_free[nxs - 2])
                m_xs = P.dma("sync", xsl[par][:, :],
                             xnto[dk * 128:(dk + 1) * 128,
                                  ow * 512:(ow + 1) * 512],
                             wait=ew, inc=f"a0xs{par}")
                mmq = None
                for hl in range(2):
                    wm = [m_xs, m_ws]
                    if dk == 0:
                        wm.append(qprep_chain)
                        wm += prev_scores
                    mmq = P.op("tensor", "matmul", PB[hl][:, :],
                               wsl[wpar][:, hl * 128:(hl + 1) * 128],
                               xsl[par][:, :], start=(dk == 0),
                               stop=(dk == 15),
                               wait=wm, inc=("a0xf" if hl == 1 else None))
                    if dk == 15:
                        qmm[hl] = mmq
                xs_free[nxs] = mmq
                wq_free[nw] = mmq
                nxs += 1
                nw += 1
            wqr = [qmm[1], qprep_chain]
            m_qr1 = P.dma("gpsimd", xsl[0][:, :],
                          ropeCo_in[:, ow * 512:(ow + 1) * 512],
                          wait=wqr, inc="a0qr0")
            m_qr2 = P.dma("gpsimd", xsl[1][:, :],
                          ropeSo_in[:, ow * 512:(ow + 1) * 512],
                          wait=wqr, inc="a0qr1")
            for hl in range(2):
                h = hg * 2 + hl
                tabs = (xsl[0][:, :], xsl[1][:, :])
                dst = qnT[:, hl * TPC + ow * 512: hl * TPC + (ow + 1) * 512]
                qprep_chain, _ = emit_a0_prep(
                    P, PB[hl], scr, qnwb[:, 0:1], bqb[:, h:h + 1], tabs, dst,
                    qmm[1], qprep_chain, m_tab=[m_qr1, m_qr2])
        m_qp = qprep_chain

        if hg == 0:
            sel_ch = m_qp
            prev_tail = []
            for j in range(NTT):
                nk = j + 1
                smm = {}
                for kw in range(nk):
                    spb = PB[kw % 2]
                    wm = [m_qp, m_kT]
                    if kw >= 2:
                        wm.append(smm[kw - 2][1])
                    elif prev_tail:
                        wm += prev_tail
                    mm = P.op("tensor", "matmul", spb[:, :],
                              qnT[0:64, j * 128:(j + 1) * 128],
                              knT[0:64, kw * 512:(kw + 1) * 512],
                              start=True, stop=True, wait=wm, inc="a0sl")
                    m_ev = P.op("scalar", "activation",
                                out=selA[:, kw * 512:(kw + 1) * 512],
                                in_=spb[:, :], func=AF.Copy, scale=0.125,
                                wait=[mm, sel_ch], inc="a0sl")
                    mlast = m_ev
                    for sk in range(4):
                        kt = kw * 4 + sk
                        dp = kt - 4 * j
                        if 0 <= dp <= 15:
                            mlast = P.op(
                                "vector", "tensor_add",
                                selA[:, kt * 128:(kt + 1) * 128],
                                selA[:, kt * 128:(kt + 1) * 128],
                                mt(mAb, dp), wait=[m_ev, boot, m_mA],
                                inc="a0sl")
                    smm[kw] = (mm, mlast)
                m_row = smm[nk - 1][1]
                allw = [m for _, m in smm.values()]
                prev_tail = allw[-2:]
                ch = None
                for rd in range(8):
                    wmx = allw + ([ch] if ch else [])
                    m_mx = P.op("vector", "max", out=m8b[:, :],
                                in_=selA[:, 0:nk * 512], wait=wmx, inc="a0sl")
                    ch = P.op("vector", "match_replace", out=selA[:, 0:nk * 512],
                              in_to_replace=m8b[:, :],
                              in_values=selA[:, 0:nk * 512],
                              imm_value=-3.0e38, wait=[m_mx], inc="a0sl")
                for kt in range(nk * 4):
                    ch = P.op("vector", "tensor_scalar",
                              m01_tile(j, kt), selA[:, kt * 128:(kt + 1) * 128],
                              -1.0e38, None, ALU.is_le, wait=[ch], inc="a0sl")
                sel_ch = ch
            sel_done = sel_ch

        new_scores = []
        for ow in range(2):
            nkt = 16 + 16 * ow
            mb_ch = [sel_done, m_qp] + prev_scores + new_scores
            msk_mils = {}
            for kt in range(nkt):
                mdst = mstore[:, kt * 512:(kt + 1) * 512]
                for jj in range(4):
                    j = 4 * ow + jj
                    dp = kt - 4 * j
                    sl = mdst[:, jj * 128:(jj + 1) * 128]
                    if dp > 15 or kt > 4 * j + 3:
                        m_z = P.op("vector", "memset", sl, 0.0,
                                   wait=mb_ch, inc="a0mk")
                        msk_mils[(kt, jj)] = m_z
                        continue
                    m_c = P.op("vector", "tensor_copy", out=Tb[:, 0:128],
                               in_=m01_tile(j, kt), wait=mb_ch, inc="a0mk")
                    m_t = P.op("tensor", "transpose", psT[0],
                               Tb[:, 0:128], identb[:, :],
                               wait=[m_c], inc="a0mk")
                    if 0 <= dp <= 15:
                        m_e = P.op("vector", "tensor_max", sl, psT[0],
                                   mt(eyeb, dp), wait=[m_t, boot], inc="a0mk")
                        m_m = P.op("vector", "tensor_tensor", sl, sl,
                                   mt(mcb, dp), ALU.min,
                                   wait=[m_e], inc="a0mk")
                    else:
                        m_m = P.op("vector", "tensor_copy", out=sl,
                                   in_=psT[0], wait=[m_t], inc="a0mk")
                    msk_mils[(kt, jj)] = m_m
                    mb_ch = [m_m]
            m_mask = [m for m in msk_mils.values()]
            m_mask_last = mb_ch[-1] if mb_ch else m_qp

            for hl in range(2):
                h = hg * 2 + hl
                g = h // 4
                slope = None
                s2_hist = {}
                e_hist = {}
                prev_z = None
                av_last = None
                dn_last = None
                prev_norm = list(new_scores)
                for kt in range(nkt):
                    sp = PB[kt % 2]
                    wm = [m_qp, m_kT, m_mask_last]
                    if sel_done is not None:
                        wm.append(sel_done)
                    if s2_hist.get(kt % 2) is not None:
                        wm.append(s2_hist[kt % 2])
                    m_s = P.op("tensor", "matmul", sp[:, :],
                               knT[:, g * T + kt * 128:
                                    g * T + (kt + 1) * 128],
                               qnT[:, hl * TPC + ow * 512:
                                   hl * TPC + (ow + 1) * 512],
                               start=True, stop=True, wait=wm, inc="a0sc")
                    wt = [m_s]
                    if prev_z is not None:
                        wt.append(prev_z)
                    m_t = P.op("scalar", "activation", out=Tb[:, :],
                               in_=sp[:, :], func=AF.Tanh,
                               scale=1.0 / (30.0 * np.sqrt(128.0)),
                               wait=wt, inc="a0sc")
                    s2_hist[kt % 2] = m_t
                    m_cb = None
                    for jj in range(4):
                        j = 4 * ow + jj
                        dp = kt - 4 * j
                        m_cb = P.op("vector", "tensor_scalar",
                                    Cb[:, jj * 128:(jj + 1) * 128],
                                    mt(dtb, dp + 28),
                                    -ALIBI_SLOPES[h] / 30.0, 0.0,
                                    ALU.mult, ALU.min,
                                    wait=[boot] + ([prev_z] if prev_z else []),
                                    inc="a0sc")
                    m_z = P.op("vector", "tensor_add", Zb[:, :], Tb[:, :],
                               Cb[:, :], wait=[m_t, m_cb], inc="a0sc")
                    prev_z = m_z
                    E = Eb[kt % 2]
                    we = [m_z]
                    if e_hist.get(kt % 2) is not None:
                        we.append(e_hist[kt % 2])
                    m_e0 = P.op("scalar", "activation", out=E[:, :],
                                in_=Zb[:, :], func=AF.Exp,
                                scale=cstt[:, C_30:C_30 + 1],
                                bias=cstt[:, C_M30:C_M30 + 1],
                                wait=we, inc="a0sc")
                    prev_z = m_e0
                    m_em = P.op("vector", "tensor_mul", E[:, :], E[:, :],
                                mstore[:, kt * 512:(kt + 1) * 512],
                                wait=[m_e0, m_mask_last] + m_mask,
                                inc="a0sc")
                    wav = [m_em, m_v_all, m_vone]
                    if kt == 0:
                        wav += prev_norm + prev_scores
                    m_av = P.op("tensor", "matmul", PB[2][:, :],
                                vg_tile(g, kt)[:, 0:128], E[:, :],
                                start=(kt == 0), stop=(kt == nkt - 1),
                                wait=wav, inc="a0sc")
                    m_dn = P.op("tensor", "matmul", PB[3][0:1, :],
                                vg_tile(g, kt)[:, 128:129], E[:, :],
                                start=(kt == 0), stop=(kt == nkt - 1),
                                wait=wav, inc="a0sc")
                    e_hist[kt % 2] = m_dn
                    av_last, dn_last = m_av, m_dn
                m_dc = P.op("vector", "tensor_copy", out=denb[:, :],
                            in_=PB[3][0:1, :], wait=[dn_last], inc="a0sc")
                m_rc = P.op("vector", "reciprocal", recb, denb[:, :],
                            wait=[m_dc], inc="a0sc")
                m_rb = P.op("tensor", "matmul", PB[4][:, :], onesc[:, :],
                            recb, start=True, stop=True,
                            wait=[m_rc] + prev_norm + prev_scores, inc="a0sc")
                m_yc = P.op("scalar", "activation", out=ytb[:, :],
                            in_=PB[2][:, :], func=AF.Copy,
                            wait=[av_last], inc="a0sc")
                m_yn = P.op("vector", "tensor_tensor", ynb[:, :],
                            ytb[:, :], PB[4][:, :], ALU.mult,
                            wait=[m_yc, m_rb] + prev_norm, inc="a0sc")
                gpar = (h * 2 + ow) % 2
                m_gw = P.dma("gpsimd", gwt[gpar][:, :],
                             gw_in[h, :, :],
                             wait=[boot] + prev_norm + prev_scores,
                             inc=f"a0gw{gpar}")
                m_g = P.op("tensor", "matmul", PB[5][:, :],
                           gwt[gpar][:, :], ynb[:, :],
                           start=True, stop=True,
                           wait=[m_yn, m_gw] + prev_norm + prev_scores,
                           inc="a0sc")
                m_sg = P.op("scalar", "activation", out=gsb[:, :],
                            in_=PB[5][:, :], func=AF.Sigmoid,
                            bias=gbb[:, h:h + 1], wait=[m_g], inc="a0sc")
                m_yf = P.op("vector", "tensor_mul",
                            y_fm[:, h * TPC + ow * 512:
                                 h * TPC + (ow + 1) * 512],
                            ynb[:, :], gsb[:, :],
                            wait=[m_sg, m_yn] + ([sel_done] if sel_done
                                                 else []), inc="a0sc")
                new_scores = [m_dc, m_yc, m_yn, m_sg, m_yf]
        prev_scores = new_scores

    m_z_prev = {}
    stt_mils = {}
    stt_all = {}
    out_dma = {}
    m_bo0 = P.dma("gpsimd", wsl[0][:, :], bo_bc[:, 0:1024],
                  wait=[boot] + prev_scores, inc="a0bo")
    m_bo1 = P.dma("gpsimd", wsl[1][:, :], bo_bc[:, 1024:2048],
                  wait=[boot] + prev_scores, inc="a0bo")
    m_bo = P.val("a0bo")
    for pair in range(4):
        for h in range(16):
            par = h % 2
            w = [m_wo] + prev_scores
            if (pair, h - 2) in m_z_prev:
                w.append(m_z_prev[(pair, h - 2)])
            if pair > 0 and h < 2:
                w.append(m_z_prev[(pair - 1, h + 14)])
            m_wl = P.dma("sync", wos[par][:, :],
                         wob[h * 128:(h + 1) * 128, :], wait=w,
                         inc=f"a0wo{par}")
            mm = None
            for ti in range(2):
                tto = pair * 2 + ti
                for dq in range(4):
                    wm = [m_wl]
                    if h == 0 and (pair - 1, ti, dq) in stt_mils:
                        wm.append(stt_mils[(pair - 1, ti, dq)])
                    mm = P.op("tensor", "matmul", PB[ti * 4 + dq][:, :],
                              y_fm[:, h * TPC + tto * 128:
                                   h * TPC + (tto + 1) * 128],
                              wos[par][:, dq * 512:(dq + 1) * 512],
                              start=(h == 0), stop=(h == 15),
                              wait=wm,
                              inc=("a0zf" if (ti == 1 and dq == 3) else None))
            m_z_prev[(pair, h)] = mm
        m_z_done = m_z_prev[(pair, 15)]
        for ti in range(2):
            tto = pair * 2 + ti
            wxl = [m_z_done]
            if (pair - 1, ti) in stt_all:
                wxl.append(stt_all[(pair - 1, ti)])
            m_xl = P.dma("gpsimd", xop2[ti][:, :],
                         xin_own[tto * 128:(tto + 1) * 128, :],
                         wait=wxl, inc=f"a0xl{ti}")
            ms = None
            for dq in range(4):
                wv = [m_z_done, m_xl]
                if (pair - 1, ti) in out_dma:
                    wv.append(out_dma[(pair - 1, ti)])
                ms = P.op("vector", "tensor_add",
                          outb2[ti][:, dq * 512:(dq + 1) * 512],
                          PB[ti * 4 + dq][:, :],
                          xop2[ti][:, dq * 512:(dq + 1) * 512],
                          wait=wv, inc="a0st")
                ms = P.op("vector", "tensor_add",
                          outb2[ti][:, dq * 512:(dq + 1) * 512],
                          outb2[ti][:, dq * 512:(dq + 1) * 512],
                          wsl[dq // 2][:, (dq % 2) * 512:(dq % 2) * 512 + 512],
                          wait=[ms, m_bo], inc="a0st")
                stt_mils[(pair, ti, dq)] = ms
            stt_all[(pair, ti)] = ms
            m_od = P.dma("gpsimd", xo[tto * 128:(tto + 1) * 128, :],
                         outb2[ti][:, :], wait=[ms], inc=f"a0od{ti}")
            out_dma[(pair, ti)] = m_od
    P.op("sync", "nop", wait=[P.val("a0od0"), P.val("a0od1")])


def _alibi_slopes_host(n):
    import math

    def pot(p):
        start = 2.0 ** (-2.0 ** (-(math.log2(p) - 3)))
        return [start * start ** i for i in range(p)]
    if math.log2(n).is_integer():
        return pot(n)
    cp = 2 ** int(math.floor(math.log2(n)))
    sl = pot(cp)
    sl += _alibi_slopes_host(2 * cp)[0::2][:n - cp]
    return sl


ALIBI_SLOPES = _alibi_slopes_host(16)


def a0_host_tables():
    ppair = np.zeros((128, 128), np.float32)
    for i in range(128):
        ppair[i, i ^ 1] = 1.0
    inv = 1.0 / 10000.0 ** (np.arange(0, 128, 2, dtype=np.float64) / 128.0)
    t = np.arange(T, dtype=np.float64)
    f = t[:, None] * inv[None, :]
    emb = np.concatenate([f, f], 1)
    ce = np.cos(emb)[:, 0::2]
    se = np.sin(emb)[:, 0::2]
    C0 = np.zeros((128, T), np.float32)
    S0 = np.zeros((128, T), np.float32)
    for k in range(64):
        C0[2 * k] = ce[:, k]
        C0[2 * k + 1] = ce[:, k]
        S0[2 * k] = -se[:, k]
        S0[2 * k + 1] = se[:, k]
    return ppair, C0, S0


def a0_core_tables(r):
    tril = np.tril(np.ones((128, 128), np.float32))
    mc = np.zeros((16, 128, 128), np.float32)
    eye = np.zeros((16, 128, 128), np.float32)
    maskA = np.zeros((16, 128, 128), np.float32)
    for d in range(16):
        if d < r:
            mc[d] = 1.0
        elif d == r:
            mc[d] = tril.T
            eye[d] = np.eye(128, dtype=np.float32)
            maskA[d] = np.where(tril > 0, 0.0, -1.0e30)
        else:
            maskA[d] = -1.0e30
        if d < r:
            maskA[d] = 0.0
    dt44 = np.zeros((44, 128, 128), np.float32)
    p = np.arange(128, dtype=np.float32)
    for dp in range(-28, 16):
        dt44[dp + 28] = (r - dp) * 128.0 + p[None, :] - p[:, None]
    return mc, eye, maskA, dt44


def build_a0_launch():
    nc = bass.Bass("TRN2", target_bir_lowering=False, debug=False)
    dt_ = nc.dram_tensor
    xin = dt_("xin", [T, D], F32, kind="ExternalInput").ap()
    xin_own = dt_("xin_own", [TPC, D], F32, kind="ExternalInput").ap()
    wq = dt_("wq", [D, D], F32, kind="ExternalInput").ap()
    bq = dt_("bq", [128, 16], F32, kind="ExternalInput").ap()
    wdn = dt_("wdn", [D, 512], F32, kind="ExternalInput").ap()
    bdn = dt_("bdn", [128, 4], F32, kind="ExternalInput").ap()
    wup = dt_("wup", [512, 1024], F32, kind="ExternalInput").ap()
    bupk = dt_("bupk", [128, 4], F32, kind="ExternalInput").ap()
    bupv = dt_("bupv", [128, 512], F32, kind="ExternalInput").ap()
    wo = dt_("wo", [D, D], F32, kind="ExternalInput").ap()
    bo = dt_("bo", [128, D], F32, kind="ExternalInput").ap()
    qn = dt_("qn", [128, 1], F32, kind="ExternalInput").ap()
    kn = dt_("kn", [128, 1], F32, kind="ExternalInput").ap()
    gw = dt_("gw", [16, 128, 128], F32, kind="ExternalInput").ap()
    gb = dt_("gb", [128, 16], F32, kind="ExternalInput").ap()
    cst = dt_("cst", [128, NCONST], F32, kind="ExternalInput").ap()
    ident = dt_("ident", [128, 128], F32, kind="ExternalInput").ap()
    ppair = dt_("ppair", [128, 128], F32, kind="ExternalInput").ap()
    ropeC = dt_("ropeC", [128, T], F32, kind="ExternalInput").ap()
    ropeS = dt_("ropeS", [128, T], F32, kind="ExternalInput").ap()
    ropeCo = dt_("ropeCo", [128, TPC], F32, kind="ExternalInput").ap()
    ropeSo = dt_("ropeSo", [128, TPC], F32, kind="ExternalInput").ap()
    mc = dt_("mc", [16, 128, 128], F32, kind="ExternalInput").ap()
    eyem = dt_("eyem", [16, 128, 128], F32, kind="ExternalInput").ap()
    dtab = dt_("dtab", [44, 128, 128], F32, kind="ExternalInput").ap()
    maskA = dt_("maskA", [16, 128, 128], F32, kind="ExternalInput").ap()
    xo = dt_("xo", [TPC, D], F32, kind="ExternalOutput").ap()
    P = Prog(nc)
    emit_a0(P, xin, xin_own, xo, wq, bq, wdn, bdn, wup, bupk, bupv, wo, bo,
            qn, kn, gw, gb, cst, ident, ppair, ropeC, ropeS, ropeCo, ropeSo,
            mc, eyem, dtab, maskA, "a0")
    P.emit_and_close()
    return nc


def run_a0_stage(x, w_q, b_q, w_down, b_down, w_up, b_up, w_o, b_o,
                 qn_w, kn_w, gate_w, gate_b):
    nc = build_a0_launch()
    cstv = host_consts()
    identv = np.eye(128, dtype=np.float32)
    ppairv, C0, S0 = a0_host_tables()
    in_maps = []
    for c in range(NCORE):
        r = c % GSZ
        xb = x[c // GSZ]
        own = np.concatenate(
            [np.arange((r + GSZ * j) * 128, (r + GSZ * j) * 128 + 128)
             for j in range(NTT)])
        mcv, eyev, maskAv, dt44 = a0_core_tables(r)
        in_maps.append({
            "xin": np.ascontiguousarray(xb), "xin_own": _gather_own(xb, c),
            "wq": w_q, "bq": np.ascontiguousarray(b_q.reshape(16, 128).T),
            "wdn": w_down,
            "bdn": np.ascontiguousarray(b_down.reshape(4, 128).T),
            "wup": w_up,
            "bupk": np.ascontiguousarray(b_up[:512].reshape(4, 128).T),
            "bupv": np.tile(b_up[512:][None, :], (128, 1)).astype(np.float32),
            "wo": w_o, "bo": np.tile(b_o[None, :], (128, 1)).astype(np.float32),
            "qn": np.ascontiguousarray(qn_w[:, None]),
            "kn": np.ascontiguousarray(kn_w[:, None]),
            "gw": gate_w,
            "gb": np.ascontiguousarray(gate_b.T),
            "cst": cstv, "ident": identv, "ppair": ppairv,
            "ropeC": C0, "ropeS": S0,
            "ropeCo": np.ascontiguousarray(C0[:, own]),
            "ropeSo": np.ascontiguousarray(S0[:, own]),
            "mc": mcv, "eyem": eyev, "dtab": dt44, "maskA": maskAv,
        })
    res = run_bass_kernel_spmd(nc, in_maps, core_ids=list(range(NCORE)))
    out = np.empty_like(x)
    for c in range(NCORE):
        _scatter_own(out[c // GSZ], np.asarray(res.results[c]["xo"]), c)
    return out



def kernel(L0_attn_x, L0_attn_w_q, L0_attn_b_q, L0_attn_w_down, L0_attn_b_down,
           L0_attn_w_up, L0_attn_b_up, L0_attn_w_o, L0_attn_b_o, L0_attn_qn_w,
           L0_attn_kn_w, L0_attn_gate_w, L0_attn_gate_b, L0_ffn_Wq,
           L0_ffn_keys, L0_ffn_e_down, L0_ffn_e_up, L1_attn_w_qkv,
           L1_attn_w_out, L1_attn_ln_scale, L1_attn_ln_bias, L1_ffn_W1,
           L1_ffn_W2):
    x = np.ascontiguousarray(np.asarray(L0_attn_x, dtype=np.float32))
    x = run_f0_stage(x, L0_ffn_Wq, L0_ffn_keys, L0_ffn_e_down, L0_ffn_e_up)
    x = run_a0_stage(x, L0_attn_w_q, L0_attn_b_q, L0_attn_w_down,
                     L0_attn_b_down, L0_attn_w_up, L0_attn_b_up, L0_attn_w_o,
                     L0_attn_b_o, L0_attn_qn_w, L0_attn_kn_w, L0_attn_gate_w,
                     L0_attn_gate_b)
    x = run_f0_stage(x, L0_ffn_Wq, L0_ffn_keys, L0_ffn_e_down, L0_ffn_e_up)
    x = run_f1_stage(x, L1_ffn_W1, L1_ffn_W2)
    x = run_a1_stage(x, L1_attn_w_qkv, L1_attn_w_out, L1_attn_ln_scale,
                     L1_attn_ln_bias)
    x = run_f1_stage(x, L1_ffn_W1, L1_ffn_W2)
    return x


# version 07 (final)
# speedup vs baseline: 1.0306x; 1.0306x over previous
import numpy as np
from contextlib import ExitStack
import concourse.bass as bass
import concourse.mybir as mybir
from concourse.bass import IndirectOffsetOnAxis
from concourse.bass_utils import run_bass_kernel_spmd

F32 = mybir.dt.float32
BF16 = mybir.dt.bfloat16
I32 = mybir.dt.int32
U32 = mybir.dt.uint32
U8 = mybir.dt.uint8
AF = mybir.ActivationFunctionType
ALU = mybir.AluOpType
AX = mybir.AxisListType.X

B, T, D = 2, 4096, 2048
NCORE = 8
TPC = 1024
NTT = TPC // 128
GSZ = 4

ENGS = ("tensor", "vector", "scalar", "gpsimd", "sync")

C_EPS5, C_NHALF, C_P1, C_EPS6, C_M30, C_30 = 0, 1, 2, 3, 4, 5
NCONST = 8
RBIG = 12582912.0
DFF = 8192


def host_consts():
    v = np.zeros((128, NCONST), np.float32)
    v[:, C_EPS5] = 1e-5
    v[:, C_NHALF] = -0.5
    v[:, C_P1] = 0.1
    v[:, C_EPS6] = 1e-6
    v[:, C_M30] = -30.0
    v[:, C_30] = 30.0
    return v


class Prog:

    def __init__(self, nc):
        self.nc = nc
        self.es = ExitStack()
        self.ops = {e: [] for e in ENGS}
        self.sem_handles = {}
        self.counts = {}

    def sem(self, name):
        if name not in self.sem_handles:
            self.sem_handles[name] = self.es.enter_context(self.nc.semaphore(name))
            self.counts[name] = 0
        return name

    def val(self, name):
        pre = name + "~"
        out = [(n, c) for n, c in self.counts.items()
               if (n == name or n.startswith(pre)) and c > 0]
        return out

    def add(self, eng, fn, wait=(), inc=None, inc_amt=1):
        mil = None
        if inc is not None:
            inc = f"{inc}~{eng}"
            self.sem(inc)
            self.counts[inc] += inc_amt
            mil = (inc, self.counts[inc])
        flat = []
        for m in wait:
            if m is None:
                continue
            if isinstance(m, list):
                flat.extend(x for x in m if x is not None)
            else:
                flat.append(m)
        wait = tuple((s, v) for s, v in flat if v > 0)
        for s, _ in wait:
            self.sem(s)
        self.ops[eng].append((fn, wait, inc, inc_amt))
        return mil

    def op(self, eng, name, *args, wait=(), inc=None, inc_amt=1, **kw):
        def fn(e, name=name, args=args, kw=kw):
            return getattr(e, name)(*args, **kw)
        return self.add(eng, fn, wait=wait, inc=inc, inc_amt=inc_amt)

    def dma(self, eng, out, in_, wait=(), inc=None, **kw):
        return self.add(eng, lambda e: e.dma_start(out=out, in_=in_, **kw),
                        wait=wait, inc=inc, inc_amt=16)

    def dma_t(self, eng, out, in_, wait=(), inc=None):
        return self.add(eng, lambda e: e.dma_start_transpose(out, in_),
                        wait=wait, inc=inc, inc_amt=16)

    def sbuf(self, name, shape, dtype=F32):
        return self.es.enter_context(self.nc.sbuf_tensor(name, shape, dtype))

    def psum(self, name, shape, dtype=F32):
        return self.es.enter_context(self.nc.psum_tensor(name, shape, dtype))

    def emit_and_close(self):
        with self.es:
            blk = self.es.enter_context(self.nc.Block())
            handles = self.sem_handles
            for ename in ENGS:
                oplist = self.ops[ename]
                if not oplist:
                    continue

                def make_body(oplist):
                    def body(eng):
                        hw = {}
                        for fn, waits, inc, inc_amt in oplist:
                            for s, val in waits:
                                if hw.get(s, 0) < val:
                                    eng.wait_ge(handles[s], val)
                                    hw[s] = val
                            inst = fn(eng)
                            if inc is not None:
                                inst.then_inc(handles[inc], inc_amt)
                    return body

                getattr(blk, ename)(make_body(oplist))



def emit_ln_stats(P, Xs, cst, stats, sq_scr, tag, x_ready, n_tiles=NTT):
    s_sq = f"{tag}sq"
    s_sum = f"{tag}sm"
    s_ch = f"{tag}ch"
    sums = stats[:, 0 * n_tiles:1 * n_tiles]
    sqs = stats[:, 1 * n_tiles:2 * n_tiles]
    means = stats[:, 2 * n_tiles:3 * n_tiles]
    m2 = stats[:, 3 * n_tiles:4 * n_tiles]
    ve = stats[:, 4 * n_tiles:5 * n_tiles]
    rstd = stats[:, 5 * n_tiles:6 * n_tiles]
    m_prev_sq = None
    for tt in range(n_tiles):
        w = tuple(x_ready[tt]) if x_ready else ()
        m_prev_sq = P.add("scalar",
              (lambda X=Xs[tt], o=sqs[:, tt:tt + 1]:
               lambda e: e.activation(out=sq_scr[:, 0:D], in_=X,
                                      func=AF.Square, accum_out=o))(),
              wait=tuple(w) + ((m_prev_sq,) if m_prev_sq else ()), inc=s_sq)
        P.add("vector",
              (lambda X=Xs[tt], o=sums[:, tt:tt + 1]:
               lambda e: e.tensor_reduce(o, X, axis=AX, op=ALU.add))(),
              wait=w, inc=s_sum)
    m_mean = P.add("vector", lambda e: e.tensor_scalar_mul(means, sums, 1.0 / D),
                   wait=[P.val(s_sum)], inc=s_ch)
    m_m2 = P.add("vector", lambda e: e.tensor_mul(m2, means, means),
                 wait=[m_mean], inc=s_ch)
    m_ve = P.add("vector", lambda e: e.scalar_tensor_tensor(
        ve, sqs, 1.0 / D, m2, ALU.mult, ALU.subtract),
        wait=[m_m2, P.val(s_sq)], inc=s_ch)
    m_ln = P.add("scalar", lambda e: e.activation(
        out=ve, in_=ve, func=AF.Ln, bias=cst[:, C_EPS5:C_EPS5 + 1]),
        wait=[m_ve], inc=s_ch)
    m_rstd = P.add("scalar", lambda e: e.activation(
        out=rstd, in_=ve, func=AF.Exp, scale=cst[:, C_NHALF:C_NHALF + 1]),
        wait=[m_ln], inc=s_ch)
    return m_rstd, means, rstd


def emit_ln_to_xnT(P, Xs, cst, identb, xnT_dst, tag, x_ready, xn_bufs, psTs,
                   n_tiles=NTT, keep_xn=None):
    stats = P.sbuf(f"{tag}_stats", [128, 6 * n_tiles], F32)
    sq_scr = P.sbuf(f"{tag}_sqscr", [128, D], BF16)
    m_rstd, means, rstd = emit_ln_stats(P, Xs, cst, stats, sq_scr, tag,
                                        x_ready, n_tiles)
    s_xn = f"{tag}xn"
    s_tr = f"{tag}tr"
    s_ev = f"{tag}ev"
    xn_dones = []
    ev_mils = []
    tr_mils = []
    for tt in range(n_tiles):
        buf = xn_bufs[tt % 2] if keep_xn is None else keep_xn(tt)
        extra = []
        if keep_xn is None and tt >= 2:
            extra.append(tr_mils[(tt - 1) * 16 - 1])
        m_xn = P.add("vector",
                     (lambda X=Xs[tt], o=buf, m=means[:, tt:tt + 1],
                      r=rstd[:, tt:tt + 1]:
                      lambda e: e.tensor_scalar(o[:, 0:D], X, m, r,
                                                ALU.subtract, ALU.mult))(),
                     wait=[m_rstd] + extra, inc=s_xn)
        xn_dones.append(m_xn)
        for dc in range(16):
            k = tt * 16 + dc
            ps = psTs[k % 2]
            w = [m_xn]
            if k >= 2:
                w.append(ev_mils[k - 2])
            m_tr = P.add("tensor",
                         (lambda b=buf, dc=dc, ps=ps:
                          lambda e: e.transpose(
                              ps[:, 0:128], b[:, dc * 128:(dc + 1) * 128],
                              identb[:]))(),
                         wait=w, inc=s_tr)
            tr_mils.append(m_tr)
            m_ev = P.add("vector",
                         (lambda o=xnT_dst(tt, dc), ps=ps:
                          lambda e: e.tensor_copy(out=o, in_=ps[:, 0:128]))(),
                         wait=[m_tr], inc=s_ev)
            ev_mils.append(m_ev)
    return ev_mils[-1], xn_dones



def emit_f1(P, Xs, cst, identb, w1b, w2b, m_w1b, m_w2b, tag, x_ready, out_cb):
    xn_bufs = [P.sbuf(f"{tag}_xnb{i}", [128, D], BF16) for i in range(2)]
    xnT = P.sbuf(f"{tag}_xnT", [128, 16 * TPC], BF16)
    aT = P.sbuf(f"{tag}_aT", [128, 64 * 512], BF16)
    w1t = [P.sbuf(f"{tag}_w1t{i}", [128, 512], BF16) for i in range(4)]
    w2t = [P.sbuf(f"{tag}_w2t{i}", [128, 512], BF16) for i in range(4)]
    q1 = [P.sbuf(f"{tag}_q1{i}", [128, 512], F32) for i in range(2)]
    q2 = [P.sbuf(f"{tag}_q2{i}", [128, 512], F32) for i in range(2)]
    q3 = [P.sbuf(f"{tag}_q3{i}", [128, 512], F32) for i in range(2)]
    zev = [P.sbuf(f"{tag}_ze{i}", [128, 512], F32) for i in range(2)]
    hps = [P.psum(f"{tag}_hps{i}", [128, 512], F32) for i in range(4)]
    zps = [P.psum(f"{tag}_zps{i}", [128, 512], F32) for i in range(4)]
    psTs = [zps[0].bitcast(BF16)[:, 0:128], zps[1].bitcast(BF16)[:, 0:128]]

    def xnT_dst(tt, dc):
        return xnT[:, dc * TPC + tt * 128: dc * TPC + (tt + 1) * 128]

    m_xnT, _ = emit_ln_to_xnT(P, Xs, cst, identb, xnT_dst, tag + "L",
                              x_ready, xn_bufs, psTs)

    s_w1 = f"{tag}w1"
    s_w1f = f"{tag}w1f"
    s_aq1 = f"{tag}a1"
    s_aq2 = f"{tag}a2"
    s_aq3 = f"{tag}a3"
    s_a = f"{tag}a"
    s_w2 = f"{tag}w2"
    s_w2f = f"{tag}w2f"
    s_z = f"{tag}z"
    s_zev = f"{tag}ze"
    s_zst = f"{tag}zs"

    w1f_mils = []
    w2f_mils = []
    aq2_hist = {}
    aq3_hist = {}
    gelu_hist = {}
    aq1_mils = []
    gelu_mils = [None] * 128
    last_z_of_tw0 = None
    stt_mils = {}
    dma_out_mils = {}

    nslice = 0
    nf = 0
    for tw in range(2):
        for quad in range(16):
            for dc in range(16):
                par = nslice % 4
                f0, d0 = quad * 512, dc * 128
                w = [m_w1b[d0 // 256]]
                if nslice >= 4:
                    w.append(w1f_mils[nslice - 4])
                m_ld = P.dma("sync", w1t[par][:, :],
                             w1b[d0:d0 + 128, f0:f0 + 512], wait=w,
                             inc=s_w1 + str(par))
                rh = xnT[:, dc * TPC + tw * 512: dc * TPC + tw * 512 + 512]
                for fl in range(4):
                    wm = [m_ld]
                    if dc == 0:
                        wm.append(m_xnT)
                        if quad + tw * 16 >= 1:
                            wm.append(aq1_mils[-1])
                    mm = P.add(
                        "tensor",
                        (lambda ps=hps[fl],
                         lt=w1t[par][:, fl * 128:(fl + 1) * 128],
                         rh=rh, dc=dc:
                         lambda e: e.matmul(ps[:, :], lt, rh,
                                            start=(dc == 0), stop=(dc == 15)))(),
                        wait=wm,
                        inc=(s_w1f if fl == 3 else None))
                    if fl == 3:
                        w1f_mils.append(mm)
                nslice += 1
            for fl in range(4):
                ft = quad * 4 + fl
                apar = nf % 2
                w1_ = [w1f_mils[-1]]
                if apar in aq2_hist:
                    w1_.append(aq2_hist[apar])
                m1 = P.add("vector",
                           (lambda o=q1[apar], i=hps[fl]:
                            lambda e: e.tensor_scalar(o[:, :], i[:, :], 10.0,
                                                      RBIG, ALU.mult, ALU.add))(),
                           wait=w1_, inc=s_aq1)
                aq1_mils.append(m1)
                w2_ = [m1]
                if apar in aq3_hist:
                    w2_.append(aq3_hist[apar])
                m2_ = P.add("vector",
                            (lambda o=q2[apar], i=q1[apar]:
                             lambda e: e.tensor_scalar(o[:, :], i[:, :], RBIG,
                                                       32767.0, ALU.subtract,
                                                       ALU.min))(),
                            wait=w2_, inc=s_aq2)
                aq2_hist[apar] = m2_
                w3_ = [m2_]
                if apar in gelu_hist:
                    w3_.append(gelu_hist[apar])
                m3 = P.add("vector",
                           (lambda o=q3[apar], i=q2[apar]:
                            lambda e: e.tensor_scalar(o[:, :], i[:, :],
                                                      -32768.0, 0.1,
                                                      ALU.max, ALU.mult))(),
                           wait=w3_, inc=s_aq3)
                aq3_hist[apar] = m3
                wg = [m3]
                if tw == 1:
                    wg.append(last_z_of_tw0)
                mg = P.add("scalar",
                           (lambda o=aT[:, ft * 512:(ft + 1) * 512], i=q3[apar]:
                            lambda e: e.activation(out=o, in_=i[:, :],
                                                   func=AF.Gelu_apprx_tanh))(),
                           wait=wg, inc=s_a)
                gelu_hist[apar] = mg
                gelu_mils[ft] = mg
                nf += 1
        w2slice = 0
        for dq in range(4):
            zmm_last = [None] * 4
            for ft in range(64):
                par = (tw * 256 + dq * 64 + ft) % 4
                w = [m_w2b[ft // 16]]
                if len(w2f_mils) >= 4:
                    w.append(w2f_mils[-4])
                m_ld = P.dma("sync", w2t[par][:, :],
                             w2b[ft * 128:(ft + 1) * 128,
                                 dq * 512:(dq + 1) * 512],
                             wait=w, inc=s_w2 + str(par))
                for ttl in range(4):
                    tt = tw * 4 + ttl
                    wm = [m_ld, gelu_mils[ft]]
                    if ft == 0:
                        wm.append(m_xnT)
                        if (tw, dq - 1, ttl) in stt_mils:
                            wm.append(stt_mils[(tw, dq - 1, ttl)])
                        if tw == 1 and (0, 3, ttl) in stt_mils:
                            wm.append(stt_mils[(0, 3, ttl)])
                    mm = P.add(
                        "tensor",
                        (lambda ps=zps[ttl],
                         lt=aT[:, ft * 512 + ttl * 128: ft * 512 + (ttl + 1) * 128],
                         rh=w2t[par], ft=ft:
                         lambda e: e.matmul(ps[:, :], lt, rh[:, :],
                                            start=(ft == 0), stop=(ft == 63)))(),
                        wait=wm,
                        inc=(s_w2f if ttl == 3 else (s_z if ft == 63 else None)))
                    if ttl == 3:
                        w2f_mils.append(mm)
                    if ft == 63:
                        zmm_last[ttl] = mm
                w2slice += 1
            for ttl in range(4):
                tt = tw * 4 + ttl
                epar = (tw * 16 + dq * 4 + ttl) % 2
                w = [zmm_last[ttl] if zmm_last[ttl] is not None else P.val(s_z)]
                if epar in dma_out_mils:
                    w.append(dma_out_mils[epar])
                m_stt = P.add(
                    "vector",
                    (lambda o=zev[epar], zp=zps[ttl],
                     xs=Xs[tt][:, dq * 512:(dq + 1) * 512]:
                     lambda e: e.scalar_tensor_tensor(
                         o[:, :], zp[:, :], 0.5, xs, ALU.mult, ALU.add))(),
                    wait=w, inc=s_zst)
                stt_mils[(tw, dq, ttl)] = m_stt
                m_out = out_cb(tt, dq, zev[epar], m_stt)
                dma_out_mils[epar] = m_out
        if tw == 0:
            last_z_of_tw0 = w2f_mils[-1]
    return



def build_f1_launch():
    nc = bass.Bass("TRN2", target_bir_lowering=False, debug=False)
    xin = nc.dram_tensor("xin", [TPC, D], F32, kind="ExternalInput").ap()
    w1 = nc.dram_tensor("w1", [D, DFF], F32, kind="ExternalInput").ap()
    w2 = nc.dram_tensor("w2", [DFF, D], F32, kind="ExternalInput").ap()
    cst_in = nc.dram_tensor("cst", [128, NCONST], F32, kind="ExternalInput").ap()
    ident_in = nc.dram_tensor("ident", [128, 128], F32, kind="ExternalInput").ap()
    xo = nc.dram_tensor("xo", [TPC, D], F32, kind="ExternalOutput").ap()
    w1b = nc.dram_tensor("w1b", [D, DFF], BF16, kind="Internal").ap()
    w2b = nc.dram_tensor("w2b", [DFF, D], BF16, kind="Internal").ap()

    P = Prog(nc)
    Xbuf = P.sbuf("Xbuf", [128, NTT * D], F32)
    Xs = [Xbuf[:, tt * D:(tt + 1) * D] for tt in range(NTT)]
    cst = P.sbuf("cstt", [128, NCONST], F32)
    identb = P.sbuf("identb", [128, 128], BF16)

    P.dma("gpsimd", cst[:, :], cst_in, inc="boot")
    P.dma("gpsimd", identb[:, :], ident_in, inc="boot")
    for tt in range(NTT):
        P.dma("gpsimd", Xs[tt], xin[tt * 128:(tt + 1) * 128, :], inc="xld")
    x_ready = [[P.val("xld"), P.val("boot")] for _ in range(NTT)]
    m_w1b = []
    for i in range(8):
        m = P.dma("gpsimd", w1b[i * 256:(i + 1) * 256, :],
                  w1[i * 256:(i + 1) * 256, :], inc=f"wc1_{i}")
        m_w1b.append(m)
    m_w2b = []
    for i in range(4):
        m = P.dma("gpsimd", w2b[i * 2048:(i + 1) * 2048, :],
                  w2[i * 2048:(i + 1) * 2048, :], inc=f"wc2_{i}")
        m_w2b.append(m)

    zoctr = [0]

    def out_cb(tt, dq, src, mil):
        zoctr[0] += 1
        return P.dma("gpsimd", xo[tt * 128:(tt + 1) * 128,
                                  dq * 512:(dq + 1) * 512],
                     src[:, :], wait=[mil], inc=f"zout{zoctr[0] % 2}")

    emit_f1(P, Xs, cst, identb, w1b, w2b, m_w1b, m_w2b, "f1", x_ready, out_cb)
    P.add("sync", lambda e: e.nop(), wait=[P.val("zout0"), P.val("zout1")])
    P.emit_and_close()
    return nc


def _core_slices(c):
    r = c % GSZ
    return [(r + GSZ * j) * 128 for j in range(NTT)]


def _gather_own(xb, c):
    return np.concatenate([xb[o:o + 128] for o in _core_slices(c)], 0)


def _scatter_own(dst, xo, c):
    for j, o in enumerate(_core_slices(c)):
        dst[o:o + 128] = xo[j * 128:(j + 1) * 128]


def run_f1_stage(x, w1, w2):
    nc = build_f1_launch()
    cst = host_consts()
    ident = np.eye(128, dtype=np.float32)
    w1t = np.ascontiguousarray(np.asarray(w1).T)
    w2t = np.ascontiguousarray(np.asarray(w2).T)
    in_maps = []
    for c in range(NCORE):
        xb = x[c // GSZ]
        in_maps.append({"xin": np.ascontiguousarray(_gather_own(xb, c)),
                        "w1": w1t, "w2": w2t, "cst": cst, "ident": ident})
    res = run_bass_kernel_spmd(nc, in_maps, core_ids=list(range(NCORE)))
    out = np.empty_like(x)
    for c in range(NCORE):
        _scatter_own(out[c // GSZ], np.asarray(res.results[c]["xo"]), c)
    return out



NB_T = T // 128


def emit_stream_ln(P, xin, xin_own, xnt, xnto, xbuf, xnb, blkb, st1, sq_scr,
                   cstt, identb, psT, boot, tag):
    nrm_mils = {}
    blk_dma = {}
    sq_hist = {}
    ev1 = []

    def stream_tile(idx, src_rows, dst_cols_ap):
        xb = xbuf[idx % 2]
        c0 = 8 * (idx % 2)
        w = [boot]
        if idx >= 2:
            w.append(nrm_mils[idx - 2])
        m_ld = P.dma("gpsimd", xb[:, :], src_rows, wait=w,
                     inc=f"{tag}x1ld{idx % 2}")
        wst = [m_ld]
        if idx >= 1:
            wst.append(sq_hist[idx - 1])
        m_sq = P.op("scalar", "activation", out=sq_scr[:, :], in_=xb[:, :],
                    func=AF.Square, accum_out=st1[:, c0 + 1:c0 + 2], wait=wst,
                    inc=f"{tag}p1")
        sq_hist[idx] = m_sq
        m_sm = P.op("vector", "tensor_reduce", st1[:, c0:c0 + 1], xb[:, :],
                    axis=AX, op=ALU.add, wait=[m_ld], inc=f"{tag}p1")
        m_me = P.op("vector", "tensor_scalar_mul", st1[:, c0 + 2:c0 + 3],
                    st1[:, c0:c0 + 1], 1.0 / D, wait=[m_sm], inc=f"{tag}p1")
        m_m2 = P.op("vector", "tensor_mul", st1[:, c0 + 3:c0 + 4],
                    st1[:, c0 + 2:c0 + 3], st1[:, c0 + 2:c0 + 3],
                    wait=[m_me], inc=f"{tag}p1")
        m_ve = P.op("vector", "scalar_tensor_tensor", st1[:, c0 + 4:c0 + 5],
                    st1[:, c0 + 1:c0 + 2], 1.0 / D, st1[:, c0 + 3:c0 + 4],
                    ALU.mult, ALU.subtract,
                    wait=[m_m2, m_sq], inc=f"{tag}p1")
        m_ln = P.op("scalar", "activation", out=st1[:, c0 + 4:c0 + 5],
                    in_=st1[:, c0 + 4:c0 + 5],
                    func=AF.Ln, bias=cstt[:, C_EPS5:C_EPS5 + 1],
                    wait=[m_ve], inc=f"{tag}p1")
        m_rs = P.op("scalar", "activation", out=st1[:, c0 + 5:c0 + 6],
                    in_=st1[:, c0 + 4:c0 + 5],
                    func=AF.Exp, scale=cstt[:, C_NHALF:C_NHALF + 1],
                    wait=[m_ln], inc=f"{tag}p1")
        xn = xnb[idx % 2]
        wn = [m_rs]
        if idx >= 2:
            wn.append(ev1[(idx - 2) * 16 + 15])
        m_n = P.op("vector", "tensor_scalar", xn[:, :], xb[:, :],
                   st1[:, c0 + 2:c0 + 3], st1[:, c0 + 5:c0 + 6],
                   ALU.subtract, ALU.mult,
                   wait=wn, inc=f"{tag}p1")
        nrm_mils[idx] = m_n
        bb = blkb[idx % 2]
        for dc in range(16):
            k = idx * 16 + dc
            ps = psT[k % 2]
            wtr = [m_n]
            if k >= 2:
                wtr.append(ev1[k - 2])
            m_tr = P.op("tensor", "transpose", ps,
                        xn[:, dc * 128:(dc + 1) * 128], identb[:, :],
                        wait=wtr, inc=f"{tag}p1tr")
            wev = [m_tr]
            if idx >= 2 and dc == 0:
                wev.append(blk_dma[idx - 2])
            m_ev = P.op("vector", "tensor_copy",
                        out=bb[:, dc * 128:(dc + 1) * 128], in_=ps,
                        wait=wev, inc=f"{tag}p1ev")
            ev1.append(m_ev)
        m_bd = P.dma("sync", dst_cols_ap,
                     bb[:, :].rearrange("p (dc c) -> p dc c", c=128),
                     wait=[ev1[-1]], inc=f"{tag}p1blk{idx % 2}")
        blk_dma[idx] = m_bd

    for bt in range(NB_T):
        stream_tile(bt, xin[bt * 128:(bt + 1) * 128, :],
                    xnt.rearrange("(dc p) t -> p dc t", p=128)[
                        :, :, bt * 128:(bt + 1) * 128])
    for j in range(NTT):
        stream_tile(NB_T + j, xin_own[j * 128:(j + 1) * 128, :],
                    xnto.rearrange("(dc p) t -> p dc t", p=128)[
                        :, :, j * 128:(j + 1) * 128])
    return P.val(f"{tag}p1blk0") + P.val(f"{tag}p1blk1")



C_I128, C_NI128 = 6, 7


def emit_a1_prep(P, src_ps, sw_ps, st_ps, sb_ps, sq_ps, scr, tabs, dst,
                 m_src, chain, m_tab=None):
    Ct, St = tabs
    kb, ks = scr["kb"], scr["ks"]
    t1b, t2b, t3b = scr["t1b"], scr["t2b"], scr["t3b"]
    stb, m2b, veb, rsb = scr["stb"], scr["m2b"], scr["veb"], scr["rsb"]
    cstt, lnsbt = scr["cst"], scr["lnsb"]
    ones1, sel0, sel1 = scr["ones1"], scr["sel0"], scr["sel1"]
    reads = []
    ma = P.op("scalar", "activation", out=kb[:, :], in_=src_ps[:, :],
              func=AF.Copy, wait=[m_src, chain], inc="pp")
    reads.append(ma)
    msw = P.op("tensor", "matmul", sw_ps[:, :], scr["p64b"][:, :], kb[:, :],
               start=True, stop=True, wait=[ma], inc="pp")
    wt1 = [ma, chain] + ([m_tab] if m_tab else [])
    m1 = P.op("vector", "tensor_tensor", t1b[:, :], kb[:, :], Ct, ALU.mult,
              wait=wt1, inc="pp")
    m2_ = P.op("vector", "scalar_tensor_tensor", t2b[:, :], sw_ps[:, :], 1.0,
               St, ALU.mult, ALU.mult, wait=[msw, m1], inc="pp")
    reads.append(m2_)
    m3 = P.op("vector", "tensor_add", t3b[:, :], t1b[:, :], t2b[:, :],
              wait=[m2_], inc="pp")
    msq = P.op("scalar", "activation", out=ks[:, :], in_=t3b[:, :],
               func=AF.Square, wait=[m3], inc="pp")
    ms1 = P.op("tensor", "matmul", st_ps[0:1, :], ones1[:, :], t3b[:, :],
               start=True, stop=True, wait=[m3], inc="pp")
    ms2 = P.op("tensor", "matmul", sw_ps[0:1, :], ones1[:, :], ks[:, :],
               start=True, stop=True, wait=[msq, m2_], inc="pp")
    mst1 = P.op("vector", "tensor_copy", out=stb[:, 0:512], in_=st_ps[0:1, :],
                wait=[ms1], inc="pp")
    reads.append(mst1)
    mst = P.op("vector", "tensor_copy", out=stb[:, 512:1024],
               in_=sw_ps[0:1, :], wait=[ms2, mst1], inc="pp")
    reads.append(mst)
    mb1 = P.op("tensor", "matmul", sb_ps[:, :], scr["onesc"][:, :],
               stb[:, 0:512], start=True, stop=True, wait=[mst1], inc="pp")
    mb2 = P.op("tensor", "matmul", sq_ps[:, :], scr["onesc"][:, :],
               stb[:, 512:1024], start=True, stop=True, wait=[mst], inc="pp")
    mm2 = P.op("scalar", "activation", out=m2b[:, :], in_=sb_ps[:, :],
               func=AF.Square, scale=cstt[:, C_I128:C_I128 + 1],
               wait=[mb1], inc="pp")
    mve = P.op("vector", "scalar_tensor_tensor", veb[:, :], sq_ps[:, :],
               1.0 / 128.0, m2b[:, :], ALU.mult, ALU.subtract,
               wait=[mb2, mm2], inc="pp")
    reads.append(mve)
    mln = P.op("scalar", "activation", out=veb[:, :], in_=veb[:, :],
               func=AF.Ln, bias=cstt[:, C_EPS5:C_EPS5 + 1],
               wait=[mve], inc="pp")
    mrs = P.op("scalar", "activation", out=rsb[:, :], in_=veb[:, :],
               func=AF.Exp, scale=cstt[:, C_NHALF:C_NHALF + 1],
               wait=[mln], inc="pp")
    mc = P.op("vector", "scalar_tensor_tensor", t1b[:, :], sb_ps[:, :],
              -1.0 / 128.0, t3b[:, :], ALU.mult, ALU.add,
              wait=[mrs, m3], inc="pp")
    reads.append(mc)
    mt4 = P.op("vector", "tensor_mul", t2b[:, :], t1b[:, :], rsb[:, :],
               wait=[mc], inc="pp")
    mout = P.op("vector", "tensor_scalar", dst, t2b[:, :],
                lnsbt[:, 0:1], lnsbt[:, 1:2], ALU.mult, ALU.add,
                wait=[mt4], inc="pp")
    return mout, reads


def emit_a1(P, xin, xin_own, xo, wqkv, wout, lnsb_in, cst_in, ident_in,
            p64_in, ropeC_hbm, ropeS_hbm, ropeCo_in, ropeSo_in, sel_in,
            masks_in, tag):
    nc = P.nc
    xnt = nc.dram_tensor(f"{tag}_xnt", [D, T], BF16, kind="Internal").ap()
    xnto = nc.dram_tensor(f"{tag}_xnto", [D, TPC], BF16, kind="Internal").ap()
    wqb = nc.dram_tensor(f"{tag}_wqb", [D, 3 * D], BF16, kind="Internal").ap()
    wob = nc.dram_tensor(f"{tag}_wob", [D, D], BF16, kind="Internal").ap()

    identb = P.sbuf(f"{tag}_identb", [128, 128], BF16)
    p64b = P.sbuf(f"{tag}_p64b", [128, 128], BF16)
    cstt = P.sbuf(f"{tag}_cstt", [128, NCONST], F32)
    lnsbt = P.sbuf(f"{tag}_lnsbt", [128, 2], F32)
    selb = P.sbuf(f"{tag}_selb", [2, 256], BF16)
    ones1 = P.sbuf(f"{tag}_ones1", [128, 1], BF16)
    onesc = P.sbuf(f"{tag}_onesc", [1, 128], F32)
    ropeCo = P.sbuf(f"{tag}_ropeCo", [128, TPC], BF16)
    ropeSo = P.sbuf(f"{tag}_ropeSo", [128, TPC], BF16)
    maskb = P.sbuf(f"{tag}_maskb", [128, 16 * 128], BF16)
    for dst_, src_ in ((identb, ident_in), (p64b, p64_in), (cstt, cst_in),
                       (lnsbt, lnsb_in), (selb, sel_in),
                       (ropeCo, ropeCo_in), (ropeSo, ropeSo_in)):
        P.dma("gpsimd", dst_[:, :], src_, inc="boot")
    P.dma("gpsimd", maskb[:, :],
          masks_in.rearrange("n p c -> p n c"), inc="boot")
    P.op("vector", "memset", ones1[:, :], 1.0, inc="bootv")
    P.op("vector", "memset", onesc[:, :], 1.0, inc="bootv")
    bootd = P.val("boot")
    bootv = P.val("bootv")
    boot = bootd

    def _BOOT():
        return [bootd, bootv]

    def mask_tile(delta):
        return maskb[:, delta * 128:(delta + 1) * 128]

    m_wqb = P.dma("gpsimd", wqb[:, :], wqkv, wait=[boot], inc="wc")
    m_wob = P.dma("gpsimd", wob[:, :], wout, wait=[m_wqb], inc="wc")

    PB = [P.psum(f"{tag}_pb{i}", [128, 512], F32) for i in range(8)]
    psT = [PB[0][:, :].bitcast(BF16)[:, 0:128],
           PB[1][:, :].bitcast(BF16)[:, 0:128]]

    xbuf = [P.sbuf(f"{tag}_xb{i}", [128, D], F32) for i in range(2)]
    xnb = [P.sbuf(f"{tag}_xnb{i}", [128, D], BF16) for i in range(2)]
    blkb = [P.sbuf(f"{tag}_blkb{i}", [128, 16 * 128], BF16) for i in range(2)]
    st1 = P.sbuf(f"{tag}_st1", [128, 16], F32)
    sq_scr = P.sbuf(f"{tag}_sqscr", [128, D], F32)
    outb2 = P.sbuf(f"{tag}_outb2", [128, D], F32)

    m_xnt = emit_stream_ln(P, xin, xin_own, xnt, xnto, xbuf, xnb, blkb, st1,
                           sq_scr, cstt, identb, psT, boot, tag)
    m_xnt2 = []

    y_fm = P.sbuf(f"{tag}_yfm", [128, 16 * TPC], BF16)
    knT = P.sbuf(f"{tag}_knT", [128, 4 * T], BF16)
    VW = 132
    Vg = P.sbuf(f"{tag}_Vg", [128, 4 * NB_T * VW], BF16)
    qnT = P.sbuf(f"{tag}_qnT", [128, 4 * TPC], BF16)
    wblk = P.sbuf(f"{tag}_wblk", [128, 16 * 512], BF16)
    xsl = [P.sbuf(f"{tag}_xsl{i}", [128, 512], BF16) for i in range(3)]
    rtb = [P.sbuf(f"{tag}_rtb{i}", [128, 1024], BF16) for i in range(2)]
    scr = {
        "kb": P.sbuf(f"{tag}_kb", [128, 512], BF16),
        "ks": P.sbuf(f"{tag}_ks", [128, 512], BF16),
        "t1b": P.sbuf(f"{tag}_t1b", [128, 512], BF16),
        "t2b": P.sbuf(f"{tag}_t2b", [128, 512], BF16),
        "t3b": P.sbuf(f"{tag}_t3b", [128, 512], BF16),
        "stb": P.sbuf(f"{tag}_stb", [1, 1024], F32),
        "onesc": None,
        "m2b": P.sbuf(f"{tag}_m2b", [128, 512], F32),
        "veb": P.sbuf(f"{tag}_veb", [128, 512], F32),
        "rsb": P.sbuf(f"{tag}_rsb", [128, 512], BF16),
        "cst": cstt, "lnsb": lnsbt, "ones1": ones1,
        "sel0": selb[:, 0:128], "sel1": selb[:, 128:256],
        "p64b": p64b,
    }
    scr["onesc"] = onesc
    s2b = P.sbuf(f"{tag}_s2b", [128, 512], BF16)
    a4b = [P.sbuf(f"{tag}_a4b{i}", [128, 512], BF16) for i in range(2)]
    ytb = P.sbuf(f"{tag}_ytb", [128, 512], BF16)
    denb = P.sbuf(f"{tag}_denb", [1, 512], F32)
    recb = P.sbuf(f"{tag}_recb", [1, 512], F32)

    def vg_tile(hl, kt):
        o = (hl * NB_T + kt) * VW
        return Vg[:, o:o + VW]

    for hl in range(4):
        for kt in range(NB_T):
            P.op("gpsimd", "memset", vg_tile(hl, kt)[:, 128:132], 0.0,
                 wait=[P.val("vone")], inc="vone")
            P.op("gpsimd", "memset", vg_tile(hl, kt)[:, 128:129], 1.0,
                 wait=[P.val("vone")], inc="vone")
    m_vone = P.val("vone")

    chain = bootv
    last_score_read = []
    nxs = 0
    xs_free = {}
    kprep_reads_all = {}

    for g in range(4):
        m_wk = P.dma("sync", wblk[:, :].rearrange("p (dk c) -> p dk c", c=512),
                     wqb.rearrange("(dk p) c -> p dk c", p=128)[
                         :, :, D + g * 512:D + (g + 1) * 512],
                     wait=[m_wqb] + last_score_read, inc="wblk")
        for w in range(8):
            rpar = w % 2
            wrt = [boot, chain]
            m_rt = P.dma("gpsimd", rtb[rpar][:, 0:512],
                         ropeC_hbm[:, w * 512:(w + 1) * 512],
                         wait=wrt, inc=f"rt{rpar}")
            m_rt2 = P.dma("gpsimd", rtb[rpar][:, 512:1024],
                          ropeS_hbm[:, w * 512:(w + 1) * 512],
                          wait=wrt, inc=f"rt{rpar}")
            kms = [None] * 4
            mm = None
            for dk in range(16):
                par = nxs % 3
                ew = [m_xnt, m_xnt2]
                if nxs >= 3:
                    ew.append(xs_free[nxs - 3])
                m_xs = P.dma("sync", xsl[par][:, :],
                             xnt[dk * 128:(dk + 1) * 128,
                                 w * 512:(w + 1) * 512],
                             wait=ew, inc=f"xs{par}")
                for hl in range(4):
                    wm = [m_xs, m_wk]
                    if dk == 0:
                        wm += last_score_read
                        if kprep_reads_all.get(hl) is not None:
                            wm.append(kprep_reads_all[hl])
                    mm = P.op("tensor", "matmul", PB[hl][:, :],
                              wblk[:, dk * 512 + hl * 128:
                                   dk * 512 + (hl + 1) * 128],
                              xsl[par][:, :],
                              start=(dk == 0), stop=(dk == 15),
                              wait=wm,
                              inc=("xf" if hl == 3 else None))
                    if dk == 15:
                        kms[hl] = mm
                xs_free[nxs] = mm
                nxs += 1
            for hl in range(4):
                tabs = (rtb[rpar][:, 0:512], rtb[rpar][:, 512:1024])
                dst = knT[:, hl * T + w * 512: hl * T + (w + 1) * 512]
                chain, reads = emit_a1_prep(
                    P, PB[hl], PB[4][:, :], PB[5], PB[6][:, :], PB[7][:, :],
                    scr, tabs, dst, kms[3], chain, m_tab=m_rt2)
                kprep_reads_all[hl] = chain
        kprep_final = chain
        m_wv = P.dma("sync", wblk[:, :].rearrange("p (dk c) -> p dk c", c=512),
                     wqb.rearrange("(dk p) c -> p dk c", p=128)[
                         :, :, 2 * D + g * 512:2 * D + (g + 1) * 512],
                     wait=[m_wqb, kprep_final], inc="wblk")
        vev_mils = []
        for w in range(8):
            vms = [None] * 4
            mm = None
            for dk in range(16):
                par = nxs % 3
                ew = [m_xnt, m_xnt2]
                if nxs >= 3:
                    ew.append(xs_free[nxs - 3])
                m_xs = P.dma("sync", xsl[par][:, :],
                             xnt[dk * 128:(dk + 1) * 128,
                                 w * 512:(w + 1) * 512],
                             wait=ew, inc=f"xs{par}")
                for sb in range(4):
                    wm = [m_xs, m_wv]
                    if dk == 0:
                        wm.append(kprep_final)
                        if vev_mils:
                            wm.append(vev_mils[-1])
                    mm = P.op("tensor", "matmul", PB[4 + sb][:, :],
                              xsl[par][:, sb * 128:(sb + 1) * 128],
                              wblk[:, dk * 512:(dk + 1) * 512],
                              start=(dk == 0), stop=(dk == 15),
                              wait=wm,
                              inc=("xf" if sb == 3 else None))
                    if dk == 15:
                        vms[sb] = mm
                xs_free[nxs] = mm
                nxs += 1
            for sb in range(4):
                kt = w * 4 + sb
                for hl in range(4):
                    m = P.op("vector", "tensor_copy",
                             out=vg_tile(hl, kt)[:, 0:128],
                             in_=PB[4 + sb][:, hl * 128:(hl + 1) * 128],
                             wait=[vms[3]], inc="vev")
                    vev_mils.append(m)
        m_v_all = vev_mils[-1]
        m_wq = P.dma("sync", wblk[:, :].rearrange("p (dk c) -> p dk c", c=512),
                     wqb.rearrange("(dk p) c -> p dk c", p=128)[
                         :, :, g * 512:(g + 1) * 512],
                     wait=[m_wqb, P.val("vev")], inc="wblk")
        for ow in range(2):
            qms = [None] * 4
            mm = None
            for dk in range(16):
                par = nxs % 3
                ew = [m_xnt, m_xnt2]
                if nxs >= 3:
                    ew.append(xs_free[nxs - 3])
                m_xs = P.dma("sync", xsl[par][:, :],
                             xnto[dk * 128:(dk + 1) * 128,
                                  ow * 512:(ow + 1) * 512],
                             wait=ew, inc=f"xs{par}")
                for hl in range(4):
                    wm = [m_xs, m_wq]
                    if dk == 0:
                        wm.append(P.val("pp"))
                    mm = P.op("tensor", "matmul", PB[hl][:, :],
                              wblk[:, dk * 512 + hl * 128:
                                   dk * 512 + (hl + 1) * 128],
                              xsl[par][:, :],
                              start=(dk == 0), stop=(dk == 15),
                              wait=wm,
                              inc=("xf" if hl == 3 else None))
                    if dk == 15:
                        qms[hl] = mm
                xs_free[nxs] = mm
                nxs += 1
            for hl in range(4):
                tabs = (ropeCo[:, ow * 512:(ow + 1) * 512],
                        ropeSo[:, ow * 512:(ow + 1) * 512])
                dst = qnT[:, hl * TPC + ow * 512: hl * TPC + (ow + 1) * 512]
                chain, reads = emit_a1_prep(
                    P, PB[hl], PB[4][:, :], PB[5], PB[6][:, :], PB[7][:, :],
                    scr, tabs, dst, qms[3], chain)
        m_prep_all = chain
        last_score_read = []
        prev_norm = []
        for hl in range(4):
            for ow in range(2):
                amax = 15 + 16 * ow
                s2_hist = {}
                av_hist = {}
                prev_a4mul = None
                av_last = None
                dn_last = None
                for kt in range(amax + 1):
                    sp = PB[kt % 2]
                    wm = [m_prep_all]
                    if s2_hist.get(kt % 2) is not None:
                        wm.append(s2_hist[kt % 2])
                    m_s = P.op("tensor", "matmul", sp[:, :],
                               knT[:, hl * T + kt * 128:
                                    hl * T + (kt + 1) * 128],
                               qnT[:, hl * TPC + ow * 512:
                                   hl * TPC + (ow + 1) * 512],
                               start=True, stop=True, wait=wm, inc="sc")
                    ws2 = [m_s]
                    if prev_a4mul is not None:
                        ws2.append(prev_a4mul)
                    m_s2 = P.op("scalar", "activation", out=s2b[:, :],
                                in_=sp[:, :], func=AF.Square,
                                wait=ws2, inc="sc")
                    s2_hist[kt % 2] = m_s2
                    a4 = a4b[kt % 2]
                    wa4 = [m_s2]
                    if av_hist.get(kt % 2) is not None:
                        wa4.append(av_hist[kt % 2])
                    m_a4 = P.op("vector", "tensor_mul", a4[:, :], s2b[:, :],
                                s2b[:, :], wait=wa4, inc="sc")
                    prev_a4mul = m_a4
                    fix = m_a4
                    for j in range(4):
                        delta = kt - 4 * (4 * ow + j)
                        if 0 <= delta <= 15:
                            fix = P.op("vector", "tensor_mul",
                                       a4[:, j * 128:(j + 1) * 128],
                                       a4[:, j * 128:(j + 1) * 128],
                                       mask_tile(delta),
                                       wait=[m_a4, boot], inc="sc")
                            prev_a4mul = fix
                    wav = [fix, m_v_all, m_vone]
                    if kt == 0:
                        wav += prev_norm
                    m_av = P.op("tensor", "matmul", PB[2][:, :],
                                vg_tile(hl, kt)[:, 0:128], a4[:, :],
                                start=(kt == 0), stop=(kt == amax),
                                wait=wav, inc="sc")
                    m_dn = P.op("tensor", "matmul", PB[3][0:1, :],
                                vg_tile(hl, kt)[:, 128:129], a4[:, :],
                                start=(kt == 0), stop=(kt == amax),
                                wait=wav, inc="sc")
                    av_hist[kt % 2] = m_dn
                    av_last, dn_last = m_av, m_dn
                m_dc = P.op("vector", "tensor_copy", out=denb[:, :],
                            in_=PB[3][0:1, :], wait=[dn_last], inc="sc")
                m_rc = P.op("vector", "reciprocal", recb[:, :], denb[:, :],
                            wait=[m_dc], inc="sc")
                m_rb = P.op("tensor", "matmul", PB[4][:, :], onesc[:, :],
                            recb[:, :], start=True, stop=True,
                            wait=[m_rc] + prev_norm, inc="sc")
                m_yc = P.op("scalar", "activation", out=ytb[:, :],
                            in_=PB[2][:, :], func=AF.Copy,
                            wait=[av_last], inc="sc")
                m_yf = P.op("vector", "tensor_tensor",
                            y_fm[:, (4 * g + hl) * TPC + ow * 512:
                                 (4 * g + hl) * TPC + (ow + 1) * 512],
                            ytb[:, :], PB[4][:, :], ALU.mult,
                            wait=[m_yc, m_rb], inc="sc")
                prev_norm = [m_dc, m_yc, m_yf]
                last_score_read += prev_norm
        chain = last_score_read[-1]

    wos = blkb
    xop = xbuf
    outb = [sq_scr, outb2]
    m_z_prev = {}
    stt_mils = {}
    stt_all = {}
    out_dma = {}
    for pair in range(4):
        for h in range(16):
            par = h % 2
            w = [m_wob] + last_score_read
            if (pair, h - 2) in m_z_prev:
                w.append(m_z_prev[(pair, h - 2)])
            if pair > 0 and h < 2:
                w.append(m_z_prev[(pair - 1, h + 14)])
            if pair == 0 and h < 2:
                w.append(m_xnt)
            m_wl = P.dma("sync", wos[par][:, :],
                         wob[h * 128:(h + 1) * 128, :], wait=w,
                         inc=f"wo{par}")
            mm = None
            for ti in range(2):
                tto = pair * 2 + ti
                for dq in range(4):
                    wm = [m_wl]
                    if h == 0 and (pair - 1, ti, dq) in stt_mils:
                        wm.append(stt_mils[(pair - 1, ti, dq)])
                    mm = P.op("tensor", "matmul", PB[ti * 4 + dq][:, :],
                              y_fm[:, h * TPC + tto * 128:
                                   h * TPC + (tto + 1) * 128],
                              wos[par][:, dq * 512:(dq + 1) * 512],
                              start=(h == 0), stop=(h == 15),
                              wait=wm,
                              inc=("zpf" if (ti == 1 and dq == 3) else None))
            m_z_prev[(pair, h)] = mm
        m_z_done = m_z_prev[(pair, 15)]
        for ti in range(2):
            tto = pair * 2 + ti
            wxl = []
            if (pair - 1, ti) in stt_all:
                wxl.append(stt_all[(pair - 1, ti)])
            if pair == 0:
                wxl.append(m_xnt)
            m_xl = P.dma("gpsimd", xop[ti][:, :],
                         xin_own[tto * 128:(tto + 1) * 128, :],
                         wait=wxl, inc=f"xol{ti}")
            ms = None
            for dq in range(4):
                w = [m_z_done, m_xl]
                if (pair - 1, ti) in out_dma:
                    w.append(out_dma[(pair - 1, ti)])
                ms = P.op("vector", "tensor_add",
                          outb[ti][:, dq * 512:(dq + 1) * 512],
                          PB[ti * 4 + dq][:, :],
                          xop[ti][:, dq * 512:(dq + 1) * 512],
                          wait=w, inc="stt")
                stt_mils[(pair, ti, dq)] = ms
            stt_all[(pair, ti)] = ms
            m_od = P.dma("gpsimd", xo[tto * 128:(tto + 1) * 128, :],
                         outb[ti][:, :], wait=[ms], inc=f"xodma{ti}")
            out_dma[(pair, ti)] = m_od
    P.op("sync", "nop", wait=[P.val("xodma0"), P.val("xodma1")])


def a1_host_tables():
    p64 = np.zeros((128, 128), np.float32)
    for i in range(128):
        p64[i, (i + 64) % 128] = 1.0
    invf = (1.0 / 10000.0 ** (np.arange(0, 128, 2, np.float32) / 128.0))
    t = np.arange(T, dtype=np.float64)[None, :]
    ang = t * invf.astype(np.float64)[:, None]
    cos = np.cos(ang).astype(np.float32)
    sin = np.sin(ang).astype(np.float32)
    C2 = np.concatenate([cos, cos], 0)
    S2 = np.concatenate([-sin, sin], 0)
    sel = np.zeros((2, 256), np.float32)
    sel[0, 0:128] = 1.0
    sel[1, 128:256] = 1.0
    return p64, C2, S2, sel


def a1_core_tables(C2, S2, r):
    own = np.concatenate([np.arange((r + GSZ * j) * 128, (r + GSZ * j) * 128 + 128)
                          for j in range(NTT)])
    Co = np.ascontiguousarray(C2[:, own])
    So = np.ascontiguousarray(S2[:, own])
    tril = np.tril(np.ones((128, 128), np.float32))
    masks = np.zeros((16, 128, 128), np.float32)
    for d in range(16):
        if d < r:
            masks[d] = 1.0
        elif d == r:
            masks[d] = tril.T
    return Co, So, masks


def build_a1_launch():
    nc = bass.Bass("TRN2", target_bir_lowering=False, debug=False)
    xin = nc.dram_tensor("xin", [T, D], F32, kind="ExternalInput").ap()
    xin_own = nc.dram_tensor("xin_own", [TPC, D], F32, kind="ExternalInput").ap()
    wqkv = nc.dram_tensor("wqkv", [D, 3 * D], F32, kind="ExternalInput").ap()
    wout = nc.dram_tensor("wout", [D, D], F32, kind="ExternalInput").ap()
    lnsb = nc.dram_tensor("lnsb", [128, 2], F32, kind="ExternalInput").ap()
    cst = nc.dram_tensor("cst", [128, NCONST], F32, kind="ExternalInput").ap()
    ident = nc.dram_tensor("ident", [128, 128], F32, kind="ExternalInput").ap()
    p64 = nc.dram_tensor("p64", [128, 128], F32, kind="ExternalInput").ap()
    ropeC = nc.dram_tensor("ropeC", [128, T], F32, kind="ExternalInput").ap()
    ropeS = nc.dram_tensor("ropeS", [128, T], F32, kind="ExternalInput").ap()
    ropeCo = nc.dram_tensor("ropeCo", [128, TPC], F32, kind="ExternalInput").ap()
    ropeSo = nc.dram_tensor("ropeSo", [128, TPC], F32, kind="ExternalInput").ap()
    sel = nc.dram_tensor("sel", [2, 256], F32, kind="ExternalInput").ap()
    masks = nc.dram_tensor("masks", [16, 128, 128], F32, kind="ExternalInput").ap()
    xo = nc.dram_tensor("xo", [TPC, D], F32, kind="ExternalOutput").ap()
    P = Prog(nc)
    emit_a1(P, xin, xin_own, xo, wqkv, wout, lnsb, cst, ident, p64,
            ropeC, ropeS, ropeCo, ropeSo, sel, masks, "a1")
    P.emit_and_close()
    return nc


def run_a1_stage(x, w_qkv, w_out, ln_scale, ln_bias):
    nc = build_a1_launch()
    cstv = host_consts()
    identv = np.eye(128, dtype=np.float32)
    p64v, C2, S2, selv = a1_host_tables()
    lnsbv = np.stack([ln_scale, ln_bias], 1).astype(np.float32)
    in_maps = []
    for c in range(NCORE):
        r = c % GSZ
        xb = x[c // GSZ]
        Co, So, maskv = a1_core_tables(C2, S2, r)
        in_maps.append({
            "xin": np.ascontiguousarray(xb), "xin_own": _gather_own(xb, c),
            "wqkv": w_qkv, "wout": w_out, "lnsb": lnsbv, "cst": cstv,
            "ident": identv, "p64": p64v, "ropeC": C2, "ropeS": S2,
            "ropeCo": Co, "ropeSo": So, "sel": selv, "masks": maskv,
        })
    res = run_bass_kernel_spmd(nc, in_maps, core_ids=list(range(NCORE)))
    out = np.empty_like(x)
    for c in range(NCORE):
        _scatter_own(out[c // GSZ], np.asarray(res.results[c]["xo"]), c)
    return out



def emit_f0(P, Xs, cst, identb, wq_in, keys_in, e_down, e_up, iota64_in,
            tag, x_ready, out_cb):
    nc = P.nc
    xn_all = P.sbuf(f"{tag}_xn", [128, NTT * D], BF16)
    xnT = P.sbuf(f"{tag}_xnT", [128, 16 * TPC], BF16)
    wqb = P.sbuf(f"{tag}_wqb", [128, 16 * 512], BF16)
    keysb = P.sbuf(f"{tag}_keysb", [128, 4 * 128], BF16)
    iota64 = P.sbuf(f"{tag}_iota", [128, 64], F32)
    qT = P.sbuf(f"{tag}_qT", [128, 4 * TPC], BF16)
    simb = P.sbuf(f"{tag}_simb", [128, 8 * 128], F32)
    s12 = P.sbuf(f"{tag}_s12", [128, 4 * 16], F32)
    i12 = P.sbuf(f"{tag}_i12", [128, 4 * 16], U32)
    i12f = P.sbuf(f"{tag}_i12f", [128, 4 * 16], F32)
    allsc = P.sbuf(f"{tag}_allsc", [128, 4 * 64], F32)
    allix = P.sbuf(f"{tag}_allix", [128, 4 * 64], F32)
    fsc = P.sbuf(f"{tag}_fsc", [128, 32], F32)
    fpos = P.sbuf(f"{tag}_fpos", [128, 32], U32)
    fposf = P.sbuf(f"{tag}_fposf", [128, 32], F32)
    ohb = P.sbuf(f"{tag}_ohb", [128, 2 * 512], F32)
    fidxf = P.sbuf(f"{tag}_fidxf", [128, 32], F32)
    fidx = P.sbuf(f"{tag}_fidx", [128, NTT * 32], I32)
    hid = P.sbuf(f"{tag}_hid", [128, 32], F32)
    coef = P.sbuf(f"{tag}_coef", [128, NTT * 32], F32)
    relu_s = P.sbuf(f"{tag}_relu", [128, 32], F32)
    silu_h = P.sbuf(f"{tag}_silu", [128, 32], F32)
    wd = [P.sbuf(f"{tag}_wd{i}", [128, D], BF16) for i in range(2)]
    wu = [P.sbuf(f"{tag}_wu{i}", [128, D], BF16) for i in range(2)]
    prodb = None
    accb = [P.sbuf(f"{tag}_acc{i}", [128, D], F32) for i in range(2)]
    scrV = ohb[:, :].bitcast(BF16)[:, 0:D]
    scrP = simb[:, :].bitcast(BF16)[:, 0:D]
    prodb = [xnT[:, 0:D], xnT[:, D:2 * D]]
    wd += [wqb[:, 0:D], wqb[:, D:2 * D]]
    wu += [wqb[:, 2 * D:3 * D], wqb[:, 3 * D:4 * D]]
    psA = P.psum(f"{tag}_psA", [128, 512], F32)
    psB = P.psum(f"{tag}_psB", [128, 512], F32)
    psTs = [psA[:, :].bitcast(BF16)[:, 0:128],
            psB[:, :].bitcast(BF16)[:, 0:128]]

    m_wq = P.dma("gpsimd", wqb[:, :].rearrange("p (dk c) -> p dk c", c=512),
                 wq_in.rearrange("(dk p) c -> p dk c", p=128), inc="f0wq")
    m_keys = P.dma("gpsimd", keysb[:, :], keys_in, inc="f0ky")
    m_iota = P.dma("gpsimd", iota64[:, :], iota64_in, inc="f0io")

    def xnT_dst(tt, dc):
        return xnT[:, dc * TPC + tt * 128: dc * TPC + (tt + 1) * 128]

    m_xnT, xn_dones = emit_ln_to_xnT(
        P, Xs, cst, identb, xnT_dst, tag + "L", x_ready, [None, None], psTs,
        keep_xn=lambda tt: xn_all[:, tt * D:(tt + 1) * D])

    qev = []
    for qt in range(4):
        for twi in range(2):
            k8 = qt * 2 + twi
            ps = psA if k8 % 2 == 0 else psB
            mm = None
            for dk in range(16):
                w = [m_wq]
                if dk == 0:
                    w.append(m_xnT)
                    if len(qev) >= 2:
                        w.append(qev[-2])
                mm = P.op("tensor", "matmul", ps[:, :],
                          wqb[:, dk * 512 + qt * 128:dk * 512 + (qt + 1) * 128],
                          xnT[:, dk * TPC + twi * 512:
                              dk * TPC + (twi + 1) * 512],
                          start=(dk == 0), stop=(dk == 15), wait=w, inc="f0qp")
            m_ev = P.op("scalar", "activation",
                        out=qT[:, qt * TPC + twi * 512:
                               qt * TPC + (twi + 1) * 512],
                        in_=ps[:, :], func=AF.Copy, wait=[mm], inc="f0qe")
            qev.append(m_ev)
    m_q_all = qev[-1]

    sim_ev_hist = {}
    dn_free = {}
    dn_act = {}
    up_free = {}
    res_free = {}
    prev_res = None
    prev_out_dma = None
    up_ch = None
    tk_chain = m_iota
    hid_chain = None
    prev_dn_pool = None
    for tt in range(NTT):
        xn_tile = xn_all[:, tt * D:(tt + 1) * D]
        sim_mms = []
        for hf in range(8):
            h, half = hf // 2, hf % 2
            ps = psA if hf < 4 else psB
            po = (hf % 4) * 128
            w = [m_q_all, m_keys]
            bank = 0 if hf < 4 else 1
            if (bank,) in sim_ev_hist:
                w.append(sim_ev_hist[(bank,)])
            lhs = qT[half * 64: half * 64 + 64,
                     h * TPC + tt * 128: h * TPC + (tt + 1) * 128]
            mm = P.op("tensor", "matmul", ps[:, po:po + 128], lhs,
                      keysb[half * 64:(half + 1) * 64,
                            h * 128:(h + 1) * 128],
                      start=True, stop=True, wait=w, inc="f0sm")
            wev = [mm]
            if prev_dn_pool is not None:
                wev.append(prev_dn_pool)
            m_ev = P.op("vector", "tensor_copy",
                        out=simb[:, hf * 128:(hf + 1) * 128],
                        in_=ps[:, po:po + 128], wait=wev, inc="f0se")
            sim_ev_hist[(bank,)] = m_ev
            sim_mms.append(m_ev)
        ch = tk_chain
        for h in range(4):
            for half in range(2):
                hf = h * 2 + half
                co = h * 16 + half * 8
                m_mx = P.op("vector", "max", out=s12[:, co:co + 8],
                            in_=simb[:, hf * 128:(hf + 1) * 128],
                            wait=[sim_mms[hf], ch], inc="f0tk")
                m_ix = P.op("vector", "max_index", out=i12[:, co:co + 8],
                            in_max=s12[:, co:co + 8],
                            in_values=simb[:, hf * 128:(hf + 1) * 128],
                            wait=[m_mx], inc="f0tk")
                ch = m_ix
        m_cast = P.op("vector", "tensor_copy", out=i12f[:, :], in_=i12[:, :],
                      wait=[ch], inc="f0tk")
        ch = m_cast
        for h in range(4):
            s1 = s12[:, h * 16:h * 16 + 8].rearrange(
                "p a -> p a ()").to_broadcast([128, 8, 8])
            s2 = s12[:, h * 16 + 8:h * 16 + 16].rearrange(
                "p b -> p () b").to_broadcast([128, 8, 8])
            m_as = P.op("vector", "tensor_tensor",
                        allsc[:, h * 64:(h + 1) * 64].rearrange(
                            "p (a b) -> p a b", b=8),
                        s1, s2, ALU.add, wait=[ch], inc="f0tk")
            i1 = i12f[:, h * 16:h * 16 + 8].rearrange(
                "p a -> p a ()").to_broadcast([128, 8, 8])
            i2 = i12f[:, h * 16 + 8:h * 16 + 16].rearrange(
                "p b -> p () b").to_broadcast([128, 8, 8])
            m_ai = P.op("vector", "scalar_tensor_tensor",
                        allix[:, h * 64:(h + 1) * 64].rearrange(
                            "p (a b) -> p a b", b=8),
                        i1, 128.0, i2, ALU.mult, ALU.add,
                        wait=[m_as], inc="f0tk")
            m_f = P.op("vector", "max", out=fsc[:, h * 8:(h + 1) * 8],
                       in_=allsc[:, h * 64:(h + 1) * 64],
                       wait=[m_ai], inc="f0tk")
            m_fp = P.op("vector", "max_index", out=fpos[:, h * 8:(h + 1) * 8],
                        in_max=fsc[:, h * 8:(h + 1) * 8],
                        in_values=allsc[:, h * 64:(h + 1) * 64],
                        wait=[m_f], inc="f0tk")
            ch = m_fp
        m_cp = P.op("vector", "tensor_copy", out=fposf[:, :], in_=fpos[:, :],
                    wait=[ch], inc="f0tk")
        m_rd = None
        for hh in range(2):
            oh4 = ohb[:, :].rearrange("p (h s c) -> p h s c", s=8, c=64)
            fp2 = fposf[:, hh * 16:(hh + 1) * 16].rearrange(
                "p (h s) -> p h s ()", s=8).to_broadcast([128, 2, 8, 64])
            io2 = iota64[:, :].rearrange("p c -> p () () c").to_broadcast(
                [128, 2, 8, 64])
            woh = [m_cp] + ([m_rd] if m_rd else [])
            m_oh = P.op("vector", "tensor_tensor", oh4, fp2, io2,
                        ALU.is_equal, wait=woh, inc="f0tk")
            aix4 = allix[:, hh * 128:(hh + 1) * 128].rearrange(
                "p (h c) -> p h () c", c=64).to_broadcast([128, 2, 8, 64])
            m_pr = P.op("vector", "tensor_tensor", oh4, oh4,
                        aix4, ALU.mult, wait=[m_oh], inc="f0tk")
            m_rd = P.op("vector", "tensor_reduce",
                        fidxf[:, hh * 16:(hh + 1) * 16].rearrange(
                            "p (h s) -> p h s", s=8),
                        oh4, axis=AX, op=ALU.add, wait=[m_pr], inc="f0tk")
        m_fi = P.op("vector", "tensor_copy",
                    out=fidx[:, tt * 32:(tt + 1) * 32], in_=fidxf[:, :],
                    wait=[m_rd], inc="f0tk")
        wh = [hid_chain] if hid_chain else []
        m_rl = P.op("vector", "tensor_scalar_max", relu_s[:, :], fsc[:, :],
                    0.0, wait=[m_rd] + wh, inc="f0hd")
        for sl in range(32):
            par = sl % 4
            wg = [m_fi]
            if par in dn_free:
                wg.append(dn_free[par])
            m_g = P.add("gpsimd",
                        (lambda o=wd[par], idxap=fidx[:, tt * 32 + sl:
                                                      tt * 32 + sl + 1]:
                         lambda e: e.indirect_dma_start(
                             out=o[:, :], out_offset=None,
                             in_=e_down[:, :],
                             in_offset=IndirectOffsetOnAxis(ap=idxap, axis=0)))(),
                        wait=wg, inc=f"f0gd{par}", inc_amt=16)
            pp = sl % 2
            wp = [m_g]
            if pp in dn_act:
                wp.append(dn_act[pp])
            m_p = P.op("vector", "tensor_tensor", prodb[pp][:, :],
                       wd[par][:, :], xn_tile, ALU.mult, wait=wp, inc="f0dn")
            wa = [m_p] + wh
            if (1 - pp) in dn_act:
                wa.append(dn_act[1 - pp])
            m_a = P.op("scalar", "activation", out=scrV[:, :],
                       in_=prodb[pp][:, :], func=AF.Copy,
                       accum_out=hid[:, sl:sl + 1], wait=wa,
                       inc="f0dna")
            dn_act[pp] = m_a
            dn_free[par] = m_p
        m_dn_all = [dn_act[0], dn_act[1]]
        prev_dn_pool = None
        m_si = P.op("scalar", "activation", out=silu_h[:, :], in_=hid[:, :],
                    func=AF.Silu, wait=m_dn_all, inc="f0hd")
        m_cf = P.op("vector", "tensor_mul", coef[:, tt * 32:(tt + 1) * 32],
                    silu_h[:, :], relu_s[:, :], wait=[m_si, m_rl], inc="f0hd")
        hid_chain = m_cf
        tk_chain = m_cf
        for sl in range(32):
            par = sl % 4
            wg = [m_fi]
            if par in up_free:
                wg += up_free[par]
            m_g = P.add("gpsimd",
                        (lambda o=wu[par], idxap=fidx[:, tt * 32 + sl:
                                                      tt * 32 + sl + 1]:
                         lambda e: e.indirect_dma_start(
                             out=o[:, :], out_offset=None,
                             in_=e_up[:, :],
                             in_offset=IndirectOffsetOnAxis(ap=idxap, axis=0)))(),
                        wait=wg, inc=f"f0gu{par}", inc_amt=16)
            cc = coef[:, tt * 32 + sl:tt * 32 + sl + 1]
            aw = [m_g, m_cf]
            if up_ch is not None:
                aw += up_ch
            if sl <= 1 and prev_res is not None:
                aw.append(prev_res)
            if sl == 0 and prev_out_dma is not None:
                aw.append(prev_out_dma)
            src_acc = accb[(sl + 1) % 2]
            dst_acc = accb[sl % 2]
            if sl == 0:
                m_u = P.op("vector", "tensor_scalar",
                           dst_acc[:, :], wu[par][:, :], cc, None,
                           ALU.mult, wait=aw, inc="f0up")
            else:
                m_u = P.op("vector", "scalar_tensor_tensor",
                           dst_acc[:, :], wu[par][:, :], cc,
                           src_acc[:, :], ALU.mult, ALU.add,
                           wait=aw, inc="f0up")
            up_ch = [m_u]
            up_free[par] = [m_u]
        wr = list(up_ch) + list(x_ready[tt])
        m_res = P.op("vector", "scalar_tensor_tensor", accb[0][:, :],
                     accb[1][:, :], 0.5, Xs[tt], ALU.mult, ALU.add,
                     wait=wr, inc="f0rs")
        prev_res = m_res
        m_out = out_cb(tt, accb[0], m_res)
        prev_out_dma = m_out


def build_f0_launch():
    nc = bass.Bass("TRN2", target_bir_lowering=False, debug=False)
    xin = nc.dram_tensor("xin", [TPC, D], F32, kind="ExternalInput").ap()
    wq = nc.dram_tensor("wq", [D, 512], F32, kind="ExternalInput").ap()
    keys = nc.dram_tensor("keys", [128, 512], F32, kind="ExternalInput").ap()
    ed = nc.dram_tensor("ed", [16384, D], F32, kind="ExternalInput").ap()
    eu = nc.dram_tensor("eu", [16384, D], F32, kind="ExternalInput").ap()
    io64 = nc.dram_tensor("io64", [128, 64], F32, kind="ExternalInput").ap()
    cst_in = nc.dram_tensor("cst", [128, NCONST], F32, kind="ExternalInput").ap()
    ident_in = nc.dram_tensor("ident", [128, 128], F32, kind="ExternalInput").ap()
    xo = nc.dram_tensor("xo", [TPC, D], F32, kind="ExternalOutput").ap()

    P = Prog(nc)
    Xbuf = P.sbuf("Xbuf", [128, NTT * D], F32)
    Xs = [Xbuf[:, tt * D:(tt + 1) * D] for tt in range(NTT)]
    cst = P.sbuf("cstt", [128, NCONST], F32)
    identb = P.sbuf("identb", [128, 128], BF16)
    P.dma("gpsimd", cst[:, :], cst_in, inc="boot")
    P.dma("gpsimd", identb[:, :], ident_in, inc="boot")
    for tt in range(NTT):
        P.dma("gpsimd", Xs[tt], xin[tt * 128:(tt + 1) * 128, :], inc="xld")
    x_ready = [[P.val("xld"), P.val("boot")] for _ in range(NTT)]

    oc = [0]

    def out_cb(tt, src, mil):
        oc[0] += 1
        return P.dma("gpsimd", xo[tt * 128:(tt + 1) * 128, :], src[:, :],
                     wait=[mil], inc=f"f0o{oc[0] % 2}")

    emit_f0(P, Xs, cst, identb, wq, keys, ed, eu, io64, "f0", x_ready, out_cb)
    P.add("sync", lambda e: e.nop(), wait=[P.val("f0o0"), P.val("f0o1")])
    P.emit_and_close()
    return nc


def f0_host_tables():
    io64 = np.tile(np.arange(64, dtype=np.float32)[None, :], (128, 1))
    return io64


def run_f0_stage(x, Wq, keys, e_down, e_up):
    nc = build_f0_launch()
    cstv = host_consts()
    identv = np.eye(128, dtype=np.float32)
    io64 = f0_host_tables()
    keys_t = np.ascontiguousarray(
        keys.transpose(2, 3, 0, 1).reshape(128, 512))
    in_maps = []
    for c in range(NCORE):
        xb = x[c // GSZ]
        in_maps.append({"xin": _gather_own(xb, c), "wq": Wq, "keys": keys_t,
                        "ed": e_down, "eu": e_up, "io64": io64, "cst": cstv,
                        "ident": identv})
    res = run_bass_kernel_spmd(nc, in_maps, core_ids=list(range(NCORE)))
    out = np.empty_like(x)
    for c in range(NCORE):
        _scatter_own(out[c // GSZ], np.asarray(res.results[c]["xo"]), c)
    return out



A0_BIG = 1.0e4


def emit_a0_prep(P, bank, scr, nw_ap, bias_ap, tabs, dst, m_src, chain,
                 m_tab=None):
    kb, ks = scr["kb"], scr["ks"]
    t1b, t2b, t3b = scr["t1b"], scr["t2b"], scr["t3b"]
    stb, veb, rsb = scr["stb"], scr["veb"], scr["rsb"]
    cstt = scr["cst"]
    reads = []
    ma = P.op("vector", "tensor_scalar", kb[:, :], bank[:, :],
              bias_ap, None, ALU.add, wait=[m_src, chain], inc="a0pp")
    msq = P.op("scalar", "activation", out=ks[:, :], in_=kb[:, :],
               func=AF.Square, wait=[ma], inc="a0pp")
    ms2 = P.op("tensor", "matmul", bank[0:1, :], scr["ones1"][:, :], ks[:, :],
               start=True, stop=True, wait=[msq], inc="a0pp")
    mst = P.op("vector", "tensor_copy", out=stb[:, 0:512], in_=bank[0:1, :],
               wait=[ms2], inc="a0pp")
    mb2 = P.op("tensor", "matmul", bank[:, :], scr["onesc"][:, :],
               stb[:, 0:512], start=True, stop=True, wait=[mst], inc="a0pp")
    mve = P.op("vector", "tensor_scalar", veb[:, :], bank[:, :],
               1.0 / 128.0, None, ALU.mult, wait=[mb2], inc="a0pp")
    mln = P.op("scalar", "activation", out=veb[:, :], in_=veb[:, :],
               func=AF.Ln, bias=cstt[:, C_EPS6:C_EPS6 + 1],
               wait=[mve], inc="a0pp")
    mrs = P.op("scalar", "activation", out=rsb[:, :], in_=veb[:, :],
               func=AF.Exp, scale=cstt[:, C_NHALF:C_NHALF + 1],
               wait=[mln], inc="a0pp")
    mn1 = P.op("vector", "tensor_mul", t1b[:, :], kb[:, :], rsb[:, :],
               wait=[mrs], inc="a0pp")
    mn2 = P.op("vector", "tensor_scalar", t3b[:, :], t1b[:, :], nw_ap, None,
               ALU.mult, wait=[mn1], inc="a0pp")
    msw = P.op("tensor", "matmul", bank[:, :], scr["ppair"][:, :],
               t3b[:, :], start=True, stop=True, wait=[mn2, mve], inc="a0pp")
    Ct, St = tabs
    wt1 = [mn2]
    if m_tab:
        wt1 += m_tab if isinstance(m_tab, list) else [m_tab]
    m1 = P.op("vector", "tensor_tensor", t1b[:, :], t3b[:, :], Ct, ALU.mult,
              wait=wt1, inc="a0pp")
    m2_ = P.op("vector", "scalar_tensor_tensor", t2b[:, :], bank[:, :], 1.0,
               St, ALU.mult, ALU.mult, wait=[msw, m1], inc="a0pp")
    reads.append(m2_)
    mout = P.op("vector", "tensor_add", dst, t1b[:, :], t2b[:, :],
                wait=[m2_], inc="a0pp")
    return mout, reads


def emit_a0(P, xin, xin_own, xo, wq_in, bq_r, wdn_in, bdn_r, wup_in, bupk_r,
            bupv_bc, wo_in, bo_bc, qn_r, kn_r, gw_in, gb_r, cst_in, ident_in,
            ppair_in, ropeC_hbm, ropeS_hbm, ropeCo_in, ropeSo_in,
            mc_in, eyem_in, dtab_in, maskA_in, tag):
    nc = P.nc
    ND = 44
    xnt = nc.dram_tensor(f"{tag}_xnt", [D, T], BF16, kind="Internal").ap()
    xnto = nc.dram_tensor(f"{tag}_xnto", [D, TPC], BF16, kind="Internal").ap()
    wqb = nc.dram_tensor(f"{tag}_wqb", [D, D], BF16, kind="Internal").ap()
    wob = nc.dram_tensor(f"{tag}_wob", [D, D], BF16, kind="Internal").ap()

    identb = P.sbuf(f"{tag}_identb", [128, 128], BF16)
    ppairb = P.sbuf(f"{tag}_ppairb", [128, 128], BF16)
    cstt = P.sbuf(f"{tag}_cstt", [128, NCONST], F32)
    ones1 = P.sbuf(f"{tag}_ones1", [128, 1], BF16)
    onesc = P.sbuf(f"{tag}_onesc", [1, 128], F32)
    onescb = P.sbuf(f"{tag}_onescb", [1, 128], BF16)
    bqb = P.sbuf(f"{tag}_bqb", [128, 16], F32)
    bdnb = P.sbuf(f"{tag}_bdnb", [128, 4], F32)
    bupkb = P.sbuf(f"{tag}_bupkb", [128, 4], F32)
    bupvb = P.sbuf(f"{tag}_bupvb", [128, 512], BF16)
    qnwb = P.sbuf(f"{tag}_qnwb", [128, 1], F32)
    knwb = P.sbuf(f"{tag}_knwb", [128, 1], F32)
    gbb = P.sbuf(f"{tag}_gbb", [128, 16], F32)
    mcb = P.sbuf(f"{tag}_mcb", [128, 16 * 128], BF16)
    eyeb = P.sbuf(f"{tag}_eyeb", [128, 16 * 128], BF16)
    dtb = P.sbuf(f"{tag}_dtb", [128, ND * 128], BF16)
    wds = [P.sbuf(f"{tag}_wds{i}", [128, 512], BF16) for i in range(2)]
    gwt = [P.sbuf(f"{tag}_gwt{i}", [128, 128], BF16) for i in range(2)]

    for dst_, src_ in ((identb, ident_in), (ppairb, ppair_in), (cstt, cst_in),
                       (bqb, bq_r), (bdnb, bdn_r), (bupkb, bupk_r),
                       (bupvb, bupv_bc), (qnwb, qn_r),
                       (knwb, kn_r), (gbb, gb_r)):
        P.dma("gpsimd", dst_[:, :], src_, inc="boot")
    P.dma("gpsimd", mcb[:, :].rearrange("p (n c) -> p n c", c=128),
          mc_in.rearrange("n p c -> p n c"), inc="boot")
    P.dma("gpsimd", eyeb[:, :].rearrange("p (n c) -> p n c", c=128),
          eyem_in.rearrange("n p c -> p n c"), inc="boot")
    P.dma("gpsimd", dtb[:, :].rearrange("p (n c) -> p n c", c=128),
          dtab_in.rearrange("n p c -> p n c"), inc="boot")
    mAb = None
    P.op("vector", "memset", ones1[:, :], 1.0, inc="bootv")
    P.op("vector", "memset", onesc[:, :], 1.0, inc="bootv")
    P.op("vector", "memset", onescb[:, :], 1.0, inc="bootv")
    boot = P.val("boot") + P.val("bootv")

    m_wq = P.dma("gpsimd", wqb[:, :], wq_in, wait=[boot], inc="wcq")
    m_wo = P.dma("gpsimd", wob[:, :], wo_in, wait=[boot], inc="wco")

    def mt(buf, d):
        return buf[:, d * 128:(d + 1) * 128]

    PB = [P.psum(f"{tag}_pb{i}", [128, 512], F32) for i in range(8)]
    psT = [PB[0][:, :].bitcast(BF16)[:, 0:128],
           PB[1][:, :].bitcast(BF16)[:, 0:128]]

    AR1 = P.sbuf(f"{tag}_ar1", [128, 8192], F32)
    AR2 = P.sbuf(f"{tag}_ar2", [128, 16384], BF16)
    y_fm = P.sbuf(f"{tag}_yfm", [128, 16 * TPC], BF16)
    knT = P.sbuf(f"{tag}_knT", [128, 4 * T], BF16)
    VW = 130
    Vg = P.sbuf(f"{tag}_Vg", [128, 4 * NB_T * VW], BF16)
    qnT = P.sbuf(f"{tag}_qnT", [128, 2 * TPC], BF16)
    wsl = [P.sbuf(f"{tag}_wsl{i}", [128, 1024], BF16) for i in range(2)]
    xsl = [P.sbuf(f"{tag}_xsl{i}", [128, 512], BF16) for i in range(2)]
    ynb = P.sbuf(f"{tag}_ynb", [128, 512], BF16)
    denb = P.sbuf(f"{tag}_denb", [1, 512], F32)
    ytb = P.sbuf(f"{tag}_ytb", [128, 512], BF16)
    gsb = P.sbuf(f"{tag}_gsb", [128, 512], BF16)
    scr = {
        "kb": P.sbuf(f"{tag}_kb", [128, 512], BF16),
        "ks": ynb[:, :],
        "t1b": ytb[:, :],
        "t2b": gsb[:, :],
        "t3b": P.sbuf(f"{tag}_t3b", [128, 512], BF16),
        "stb": denb[:, :],
        "veb": P.sbuf(f"{tag}_veb", [128, 512], BF16),
        "rsb": P.sbuf(f"{tag}_rsb", [128, 512], BF16),
        "cst": cstt, "ones1": ones1, "onesc": onesc, "ppair": ppairb,
    }
    Tb = P.sbuf(f"{tag}_Tb", [128, 512], BF16)
    Cb = scr["kb"]
    Zb = scr["rsb"]
    Eb = [P.sbuf(f"{tag}_Eb{i}", [128, 512], BF16) for i in range(2)]
    recb = AR1[0:1, 6656:7168]
    m8b = P.sbuf(f"{tag}_m8b", [128, 8], F32)
    st1 = P.sbuf(f"{tag}_st1", [128, 16], F32)
    sq_scr = knT[:, 0:D]

    xbuf = [AR1[:, 0:2048], AR1[:, 2048:4096]]
    ar1b = AR1[:, :].bitcast(BF16)
    xnb = [ar1b[:, 8192:10240], ar1b[:, 10240:12288]]
    blkb = [ar1b[:, 12288:14336], ar1b[:, 14336:16384]]
    selA = ar1b[:, 0:4096]
    m01u = AR1[:, :].bitcast(U8)[:, 8192:8192 + 144 * 128]
    latT = AR2[:, :]
    mstore = AR2[:, :]
    ar2f = AR2[:, :].bitcast(F32)
    wos = [AR2[:, 0:2048], AR2[:, 2048:4096]]
    xop2 = [ar2f[:, 2048:4096], ar2f[:, 4096:6144]]
    outb2 = [ar2f[:, 6144:8192], AR1[:, 4096:6144]]

    mAb = y_fm[:, 0:2048]
    m_mA = P.dma("gpsimd", mAb.rearrange("p (n c) -> p n c", c=128),
                 maskA_in.rearrange("n p c -> p n c"), inc="boot2")

    def m01_tile(j, kt):
        idx = sum(4 * (jj + 1) for jj in range(j)) + kt
        return m01u[:, idx * 128:(idx + 1) * 128]

    def vg_tile(g, kt):
        o = (g * NB_T + kt) * VW
        return Vg[:, o:o + VW]

    for g in range(4):
        for kt in range(NB_T):
            P.op("gpsimd", "memset", vg_tile(g, kt)[:, 128:130], 0.0,
                 wait=[P.val("vone")], inc="vone")
            P.op("gpsimd", "memset", vg_tile(g, kt)[:, 128:129], 1.0,
                 wait=[P.val("vone")], inc="vone")
    m_vone = P.val("vone")

    m_xnt = emit_stream_ln(P, xin, xin_own, xnt, xnto, xbuf, xnb, blkb, st1,
                           sq_scr, cstt, identb, psT, boot, tag)

    chain = P.val("bootv")
    nxs = 0
    xs_free = {}
    wd_free = {}
    nwd = 0
    lat_ev = []
    for w in range(8):
        lms = None
        for dk in range(16):
            par = nxs % 2
            ew = [m_xnt]
            if nxs >= 2:
                ew.append(xs_free[nxs - 2])
            m_xs = P.dma("sync", xsl[par][:, :],
                         xnt[dk * 128:(dk + 1) * 128, w * 512:(w + 1) * 512],
                         wait=ew, inc=f"a0xs{par}")
            wdp = nwd % 2
            eww = [boot]
            if nwd >= 2:
                eww.append(wd_free[nwd - 2])
            m_wd = P.dma("gpsimd", wds[wdp][:, :],
                         wdn_in[dk * 128:(dk + 1) * 128, :],
                         wait=eww, inc=f"a0wd{wdp}")
            for lt in range(4):
                wm = [m_xs, m_wd]
                if dk == 0 and len(lat_ev) >= 4:
                    wm.append(lat_ev[-1])
                lms = P.op("tensor", "matmul", PB[lt][:, :],
                           wds[wdp][:, lt * 128:(lt + 1) * 128],
                           xsl[par][:, :], start=(dk == 0), stop=(dk == 15),
                           wait=wm, inc=("a0xf" if lt == 3 else None))
            xs_free[nxs] = lms
            wd_free[nwd] = lms
            nxs += 1
            nwd += 1
        for lt in range(4):
            m_ev = P.op("vector", "tensor_scalar",
                        latT[:, lt * T + w * 512: lt * T + (w + 1) * 512],
                        PB[lt][:, :], bdnb[:, lt:lt + 1], None, ALU.add,
                        wait=[lms], inc="a0lat")
            lat_ev.append(m_ev)
    m_lat = lat_ev[-1]

    kprep_chain = chain
    wu_free = {}
    for w in range(8):
        wrt = [boot, kprep_chain, P.val("a0xf")]
        m_rt = P.dma("gpsimd", xsl[0][:, :],
                     ropeC_hbm[:, w * 512:(w + 1) * 512], wait=wrt,
                     inc="a0rt0")
        m_rt2 = P.dma("gpsimd", xsl[1][:, :],
                      ropeS_hbm[:, w * 512:(w + 1) * 512], wait=wrt,
                      inc="a0rt1")
        kmm = [None] * 4
        vmm = [None] * 4
        for lc in range(4):
            wpar = lc % 2
            eww = [boot]
            if w > 0 or lc >= 2:
                eww.append(wu_free[(w, lc - 2) if lc >= 2 else (w - 1, lc + 2)])
            m_wu = P.dma("gpsimd", wsl[wpar][:, :],
                         wup_in[lc * 128:(lc + 1) * 128, :],
                         wait=eww, inc=f"a0wu{wpar}")
            for g in range(4):
                wm = [m_lat, m_wu]
                if lc == 0:
                    wm.append(kprep_chain)
                kmm[g] = P.op("tensor", "matmul", PB[g][:, :],
                              wsl[wpar][:, g * 128:(g + 1) * 128],
                              latT[:, lc * T + w * 512:
                                   lc * T + (w + 1) * 512],
                              start=(lc == 0), stop=(lc == 3),
                              wait=wm, inc=("a0kp" if g == 3 else None))
            for sb in range(4):
                wm = [m_lat, m_wu]
                if lc == 0 and w >= 1:
                    wm.append(vev_last)
                vmm[sb] = P.op("tensor", "matmul", PB[4 + sb][:, :],
                               latT[:, lc * T + w * 512 + sb * 128:
                                    lc * T + w * 512 + (sb + 1) * 128],
                               wsl[wpar][:, 512:1024],
                               start=(lc == 0), stop=(lc == 3),
                               wait=wm, inc=("a0vp" if sb == 3 else None))
            wu_free[(w, lc)] = vmm[3]
        for sb in range(4):
            kt = w * 4 + sb
            for g in range(4):
                m = P.op("vector", "tensor_add",
                         vg_tile(g, kt)[:, 0:128],
                         PB[4 + sb][:, g * 128:(g + 1) * 128],
                         bupvb[:, g * 128:(g + 1) * 128],
                         wait=[vmm[3], boot], inc="a0vev")
                vev_last = m
        for g in range(4):
            tabs = (xsl[0][:, :], xsl[1][:, :])
            dst = knT[:, g * T + w * 512: g * T + (w + 1) * 512]
            kprep_chain, _ = emit_a0_prep(
                P, PB[g], scr, knwb[:, 0:1], bupkb[:, g:g + 1], tabs, dst,
                kmm[3], kprep_chain, m_tab=[m_rt, m_rt2])
    m_kT = kprep_chain
    m_v_all = vev_last

    prev_scores = []
    sel_done = None
    m01_done = None
    out_mils = {}
    for hg in range(8):
        qprep_chain = m_kT if hg == 0 else prev_scores[-1]
        for ow in range(2):
            nw = 0
            wq_free = {}
            qmm = [None] * 2
            for dk in range(16):
                wpar = nw % 2
                eww = [m_wq, qprep_chain, m_v_all]
                if nw >= 2:
                    eww.append(wq_free[nw - 2])
                m_ws = P.dma("sync", wsl[wpar][:, 0:256],
                             wqb[dk * 128:(dk + 1) * 128,
                                 hg * 256:(hg + 1) * 256],
                             wait=eww, inc=f"a0wq{wpar}")
                par = nxs % 2
                ew = [m_xnt, qprep_chain]
                if nxs >= 2:
                    ew.append(xs_free[nxs - 2])
                m_xs = P.dma("sync", xsl[par][:, :],
                             xnto[dk * 128:(dk + 1) * 128,
                                  ow * 512:(ow + 1) * 512],
                             wait=ew, inc=f"a0xs{par}")
                mmq = None
                for hl in range(2):
                    wm = [m_xs, m_ws]
                    if dk == 0:
                        wm.append(qprep_chain)
                        wm += prev_scores
                    mmq = P.op("tensor", "matmul", PB[hl][:, :],
                               wsl[wpar][:, hl * 128:(hl + 1) * 128],
                               xsl[par][:, :], start=(dk == 0),
                               stop=(dk == 15),
                               wait=wm, inc=("a0xf" if hl == 1 else None))
                    if dk == 15:
                        qmm[hl] = mmq
                xs_free[nxs] = mmq
                wq_free[nw] = mmq
                nxs += 1
                nw += 1
            wqr = [qmm[1], qprep_chain]
            m_qr1 = P.dma("gpsimd", xsl[0][:, :],
                          ropeCo_in[:, ow * 512:(ow + 1) * 512],
                          wait=wqr, inc="a0qr0")
            m_qr2 = P.dma("gpsimd", xsl[1][:, :],
                          ropeSo_in[:, ow * 512:(ow + 1) * 512],
                          wait=wqr, inc="a0qr1")
            for hl in range(2):
                h = hg * 2 + hl
                tabs = (xsl[0][:, :], xsl[1][:, :])
                dst = qnT[:, hl * TPC + ow * 512: hl * TPC + (ow + 1) * 512]
                qprep_chain, _ = emit_a0_prep(
                    P, PB[hl], scr, qnwb[:, 0:1], bqb[:, h:h + 1], tabs, dst,
                    qmm[1], qprep_chain, m_tab=[m_qr1, m_qr2])
        m_qp = qprep_chain

        if hg == 0:
            sel_ch = m_qp
            prev_tail = []
            for j in range(NTT):
                nk = j + 1
                smm = {}
                for kw in range(nk):
                    spb = PB[kw % 2]
                    wm = [m_qp, m_kT]
                    if kw >= 2:
                        wm.append(smm[kw - 2][1])
                    elif prev_tail:
                        wm += prev_tail
                    mm = P.op("tensor", "matmul", spb[:, :],
                              qnT[0:64, j * 128:(j + 1) * 128],
                              knT[0:64, kw * 512:(kw + 1) * 512],
                              start=True, stop=True, wait=wm, inc="a0sl")
                    m_ev = P.op("scalar", "activation",
                                out=selA[:, kw * 512:(kw + 1) * 512],
                                in_=spb[:, :], func=AF.Copy, scale=0.125,
                                wait=[mm, sel_ch], inc="a0sl")
                    mlast = m_ev
                    for sk in range(4):
                        kt = kw * 4 + sk
                        dp = kt - 4 * j
                        if 0 <= dp <= 15:
                            mlast = P.op(
                                "vector", "tensor_add",
                                selA[:, kt * 128:(kt + 1) * 128],
                                selA[:, kt * 128:(kt + 1) * 128],
                                mt(mAb, dp), wait=[m_ev, boot, m_mA],
                                inc="a0sl")
                    smm[kw] = (mm, mlast)
                m_row = smm[nk - 1][1]
                allw = [m for _, m in smm.values()]
                prev_tail = allw[-2:]
                ch = None
                for rd in range(8):
                    wmx = allw + ([ch] if ch else [])
                    m_mx = P.op("vector", "max", out=m8b[:, :],
                                in_=selA[:, 0:nk * 512], wait=wmx, inc="a0sl")
                    ch = P.op("vector", "match_replace", out=selA[:, 0:nk * 512],
                              in_to_replace=m8b[:, :],
                              in_values=selA[:, 0:nk * 512],
                              imm_value=-3.0e38, wait=[m_mx], inc="a0sl")
                for kt in range(nk * 4):
                    ch = P.op("vector", "tensor_scalar",
                              m01_tile(j, kt), selA[:, kt * 128:(kt + 1) * 128],
                              -1.0e38, None, ALU.is_le, wait=[ch], inc="a0sl")
                sel_ch = ch
            sel_done = sel_ch

        new_scores = []
        for ow in range(2):
            nkt = 16 + 16 * ow
            mb_ch = [sel_done, m_qp] + prev_scores + new_scores
            msk_mils = {}
            for kt in range(nkt):
                mdst = mstore[:, kt * 512:(kt + 1) * 512]
                for jj in range(4):
                    j = 4 * ow + jj
                    dp = kt - 4 * j
                    sl = mdst[:, jj * 128:(jj + 1) * 128]
                    if dp > 15 or kt > 4 * j + 3:
                        m_z = P.op("vector", "memset", sl, 0.0,
                                   wait=mb_ch, inc="a0mk")
                        msk_mils[(kt, jj)] = m_z
                        continue
                    m_c = P.op("vector", "tensor_copy", out=Tb[:, 0:128],
                               in_=m01_tile(j, kt), wait=mb_ch, inc="a0mk")
                    m_t = P.op("tensor", "transpose", psT[0],
                               Tb[:, 0:128], identb[:, :],
                               wait=[m_c], inc="a0mk")
                    if 0 <= dp <= 15:
                        m_e = P.op("vector", "tensor_max", sl, psT[0],
                                   mt(eyeb, dp), wait=[m_t, boot], inc="a0mk")
                        m_m = P.op("vector", "tensor_tensor", sl, sl,
                                   mt(mcb, dp), ALU.min,
                                   wait=[m_e], inc="a0mk")
                    else:
                        m_m = P.op("vector", "tensor_copy", out=sl,
                                   in_=psT[0], wait=[m_t], inc="a0mk")
                    msk_mils[(kt, jj)] = m_m
                    mb_ch = [m_m]
            m_mask = [m for m in msk_mils.values()]
            m_mask_last = mb_ch[-1] if mb_ch else m_qp

            for hl in range(2):
                h = hg * 2 + hl
                g = h // 4
                slope = None
                s2_hist = {}
                e_hist = {}
                prev_z = None
                av_last = None
                dn_last = None
                prev_norm = list(new_scores)
                for kt in range(nkt):
                    sp = PB[kt % 2]
                    wm = [m_qp, m_kT, m_mask_last]
                    if sel_done is not None:
                        wm.append(sel_done)
                    if s2_hist.get(kt % 2) is not None:
                        wm.append(s2_hist[kt % 2])
                    m_s = P.op("tensor", "matmul", sp[:, :],
                               knT[:, g * T + kt * 128:
                                    g * T + (kt + 1) * 128],
                               qnT[:, hl * TPC + ow * 512:
                                   hl * TPC + (ow + 1) * 512],
                               start=True, stop=True, wait=wm, inc="a0sc")
                    wt = [m_s]
                    if prev_z is not None:
                        wt.append(prev_z)
                    m_t = P.op("scalar", "activation", out=Tb[:, :],
                               in_=sp[:, :], func=AF.Tanh,
                               scale=1.0 / (30.0 * np.sqrt(128.0)),
                               wait=wt, inc="a0sc")
                    s2_hist[kt % 2] = m_t
                    m_cb = None
                    for jj in range(4):
                        j = 4 * ow + jj
                        dp = kt - 4 * j
                        m_cb = P.op("vector", "tensor_scalar",
                                    Cb[:, jj * 128:(jj + 1) * 128],
                                    mt(dtb, dp + 28),
                                    -ALIBI_SLOPES[h] / 30.0, 0.0,
                                    ALU.mult, ALU.min,
                                    wait=[boot] + ([prev_z] if prev_z else []),
                                    inc="a0sc")
                    m_z = P.op("vector", "tensor_add", Zb[:, :], Tb[:, :],
                               Cb[:, :], wait=[m_t, m_cb], inc="a0sc")
                    prev_z = m_z
                    E = Eb[kt % 2]
                    we = [m_z]
                    if e_hist.get(kt % 2) is not None:
                        we.append(e_hist[kt % 2])
                    m_e0 = P.op("scalar", "activation", out=E[:, :],
                                in_=Zb[:, :], func=AF.Exp,
                                scale=cstt[:, C_30:C_30 + 1],
                                bias=cstt[:, C_M30:C_M30 + 1],
                                wait=we, inc="a0sc")
                    prev_z = m_e0
                    m_em = P.op("vector", "tensor_mul", E[:, :], E[:, :],
                                mstore[:, kt * 512:(kt + 1) * 512],
                                wait=[m_e0, m_mask_last] + m_mask,
                                inc="a0sc")
                    wav = [m_em, m_v_all, m_vone]
                    if kt == 0:
                        wav += prev_norm + prev_scores
                    m_av = P.op("tensor", "matmul", PB[2][:, :],
                                vg_tile(g, kt)[:, 0:128], E[:, :],
                                start=(kt == 0), stop=(kt == nkt - 1),
                                wait=wav, inc="a0sc")
                    m_dn = P.op("tensor", "matmul", PB[3][0:1, :],
                                vg_tile(g, kt)[:, 128:129], E[:, :],
                                start=(kt == 0), stop=(kt == nkt - 1),
                                wait=wav, inc="a0sc")
                    e_hist[kt % 2] = m_dn
                    av_last, dn_last = m_av, m_dn
                m_dc = P.op("vector", "tensor_copy", out=denb[:, :],
                            in_=PB[3][0:1, :], wait=[dn_last], inc="a0sc")
                m_rc = P.op("vector", "reciprocal", recb, denb[:, :],
                            wait=[m_dc], inc="a0sc")
                m_rb = P.op("tensor", "matmul", PB[4][:, :], onesc[:, :],
                            recb, start=True, stop=True,
                            wait=[m_rc] + prev_norm + prev_scores, inc="a0sc")
                m_yc = P.op("scalar", "activation", out=ytb[:, :],
                            in_=PB[2][:, :], func=AF.Copy,
                            wait=[av_last], inc="a0sc")
                m_yn = P.op("vector", "tensor_tensor", ynb[:, :],
                            ytb[:, :], PB[4][:, :], ALU.mult,
                            wait=[m_yc, m_rb] + prev_norm, inc="a0sc")
                gpar = (h * 2 + ow) % 2
                m_gw = P.dma("gpsimd", gwt[gpar][:, :],
                             gw_in[h, :, :],
                             wait=[boot] + prev_norm + prev_scores,
                             inc=f"a0gw{gpar}")
                m_g = P.op("tensor", "matmul", PB[5][:, :],
                           gwt[gpar][:, :], ynb[:, :],
                           start=True, stop=True,
                           wait=[m_yn, m_gw] + prev_norm + prev_scores,
                           inc="a0sc")
                m_sg = P.op("scalar", "activation", out=gsb[:, :],
                            in_=PB[5][:, :], func=AF.Sigmoid,
                            bias=gbb[:, h:h + 1], wait=[m_g], inc="a0sc")
                m_yf = P.op("vector", "tensor_mul",
                            y_fm[:, h * TPC + ow * 512:
                                 h * TPC + (ow + 1) * 512],
                            ynb[:, :], gsb[:, :],
                            wait=[m_sg, m_yn] + ([sel_done] if sel_done
                                                 else []), inc="a0sc")
                new_scores = [m_dc, m_yc, m_yn, m_sg, m_yf]
        prev_scores = new_scores

    m_z_prev = {}
    stt_mils = {}
    stt_all = {}
    out_dma = {}
    m_bo0 = P.dma("gpsimd", wsl[0][:, :], bo_bc[:, 0:1024],
                  wait=[boot] + prev_scores, inc="a0bo")
    m_bo1 = P.dma("gpsimd", wsl[1][:, :], bo_bc[:, 1024:2048],
                  wait=[boot] + prev_scores, inc="a0bo")
    m_bo = P.val("a0bo")
    for pair in range(4):
        for h in range(16):
            par = h % 2
            w = [m_wo] + prev_scores
            if (pair, h - 2) in m_z_prev:
                w.append(m_z_prev[(pair, h - 2)])
            if pair > 0 and h < 2:
                w.append(m_z_prev[(pair - 1, h + 14)])
            m_wl = P.dma("sync", wos[par][:, :],
                         wob[h * 128:(h + 1) * 128, :], wait=w,
                         inc=f"a0wo{par}")
            mm = None
            for ti in range(2):
                tto = pair * 2 + ti
                for dq in range(4):
                    wm = [m_wl]
                    if h == 0 and (pair - 1, ti, dq) in stt_mils:
                        wm.append(stt_mils[(pair - 1, ti, dq)])
                    mm = P.op("tensor", "matmul", PB[ti * 4 + dq][:, :],
                              y_fm[:, h * TPC + tto * 128:
                                   h * TPC + (tto + 1) * 128],
                              wos[par][:, dq * 512:(dq + 1) * 512],
                              start=(h == 0), stop=(h == 15),
                              wait=wm,
                              inc=("a0zf" if (ti == 1 and dq == 3) else None))
            m_z_prev[(pair, h)] = mm
        m_z_done = m_z_prev[(pair, 15)]
        for ti in range(2):
            tto = pair * 2 + ti
            wxl = [m_z_done]
            if (pair - 1, ti) in stt_all:
                wxl.append(stt_all[(pair - 1, ti)])
            m_xl = P.dma("gpsimd", xop2[ti][:, :],
                         xin_own[tto * 128:(tto + 1) * 128, :],
                         wait=wxl, inc=f"a0xl{ti}")
            ms = None
            for dq in range(4):
                wv = [m_z_done, m_xl]
                if (pair - 1, ti) in out_dma:
                    wv.append(out_dma[(pair - 1, ti)])
                ms = P.op("vector", "tensor_add",
                          outb2[ti][:, dq * 512:(dq + 1) * 512],
                          PB[ti * 4 + dq][:, :],
                          xop2[ti][:, dq * 512:(dq + 1) * 512],
                          wait=wv, inc="a0st")
                ms = P.op("vector", "tensor_add",
                          outb2[ti][:, dq * 512:(dq + 1) * 512],
                          outb2[ti][:, dq * 512:(dq + 1) * 512],
                          wsl[dq // 2][:, (dq % 2) * 512:(dq % 2) * 512 + 512],
                          wait=[ms, m_bo], inc="a0st")
                stt_mils[(pair, ti, dq)] = ms
            stt_all[(pair, ti)] = ms
            m_od = P.dma("gpsimd", xo[tto * 128:(tto + 1) * 128, :],
                         outb2[ti][:, :], wait=[ms], inc=f"a0od{ti}")
            out_dma[(pair, ti)] = m_od
    P.op("sync", "nop", wait=[P.val("a0od0"), P.val("a0od1")])


def _alibi_slopes_host(n):
    import math

    def pot(p):
        start = 2.0 ** (-2.0 ** (-(math.log2(p) - 3)))
        return [start * start ** i for i in range(p)]
    if math.log2(n).is_integer():
        return pot(n)
    cp = 2 ** int(math.floor(math.log2(n)))
    sl = pot(cp)
    sl += _alibi_slopes_host(2 * cp)[0::2][:n - cp]
    return sl


ALIBI_SLOPES = _alibi_slopes_host(16)


def a0_host_tables():
    ppair = np.zeros((128, 128), np.float32)
    for i in range(128):
        ppair[i, i ^ 1] = 1.0
    inv = 1.0 / 10000.0 ** (np.arange(0, 128, 2, dtype=np.float64) / 128.0)
    t = np.arange(T, dtype=np.float64)
    f = t[:, None] * inv[None, :]
    emb = np.concatenate([f, f], 1)
    ce = np.cos(emb)[:, 0::2]
    se = np.sin(emb)[:, 0::2]
    C0 = np.zeros((128, T), np.float32)
    S0 = np.zeros((128, T), np.float32)
    for k in range(64):
        C0[2 * k] = ce[:, k]
        C0[2 * k + 1] = ce[:, k]
        S0[2 * k] = -se[:, k]
        S0[2 * k + 1] = se[:, k]
    return ppair, C0, S0


def a0_core_tables(r):
    tril = np.tril(np.ones((128, 128), np.float32))
    mc = np.zeros((16, 128, 128), np.float32)
    eye = np.zeros((16, 128, 128), np.float32)
    maskA = np.zeros((16, 128, 128), np.float32)
    for d in range(16):
        if d < r:
            mc[d] = 1.0
        elif d == r:
            mc[d] = tril.T
            eye[d] = np.eye(128, dtype=np.float32)
            maskA[d] = np.where(tril > 0, 0.0, -1.0e30)
        else:
            maskA[d] = -1.0e30
        if d < r:
            maskA[d] = 0.0
    dt44 = np.zeros((44, 128, 128), np.float32)
    p = np.arange(128, dtype=np.float32)
    for dp in range(-28, 16):
        dt44[dp + 28] = (r - dp) * 128.0 + p[None, :] - p[:, None]
    return mc, eye, maskA, dt44


def build_a0_launch():
    nc = bass.Bass("TRN2", target_bir_lowering=False, debug=False)
    dt_ = nc.dram_tensor
    xin = dt_("xin", [T, D], F32, kind="ExternalInput").ap()
    xin_own = dt_("xin_own", [TPC, D], F32, kind="ExternalInput").ap()
    wq = dt_("wq", [D, D], F32, kind="ExternalInput").ap()
    bq = dt_("bq", [128, 16], F32, kind="ExternalInput").ap()
    wdn = dt_("wdn", [D, 512], F32, kind="ExternalInput").ap()
    bdn = dt_("bdn", [128, 4], F32, kind="ExternalInput").ap()
    wup = dt_("wup", [512, 1024], F32, kind="ExternalInput").ap()
    bupk = dt_("bupk", [128, 4], F32, kind="ExternalInput").ap()
    bupv = dt_("bupv", [128, 512], F32, kind="ExternalInput").ap()
    wo = dt_("wo", [D, D], F32, kind="ExternalInput").ap()
    bo = dt_("bo", [128, D], F32, kind="ExternalInput").ap()
    qn = dt_("qn", [128, 1], F32, kind="ExternalInput").ap()
    kn = dt_("kn", [128, 1], F32, kind="ExternalInput").ap()
    gw = dt_("gw", [16, 128, 128], F32, kind="ExternalInput").ap()
    gb = dt_("gb", [128, 16], F32, kind="ExternalInput").ap()
    cst = dt_("cst", [128, NCONST], F32, kind="ExternalInput").ap()
    ident = dt_("ident", [128, 128], F32, kind="ExternalInput").ap()
    ppair = dt_("ppair", [128, 128], F32, kind="ExternalInput").ap()
    ropeC = dt_("ropeC", [128, T], F32, kind="ExternalInput").ap()
    ropeS = dt_("ropeS", [128, T], F32, kind="ExternalInput").ap()
    ropeCo = dt_("ropeCo", [128, TPC], F32, kind="ExternalInput").ap()
    ropeSo = dt_("ropeSo", [128, TPC], F32, kind="ExternalInput").ap()
    mc = dt_("mc", [16, 128, 128], F32, kind="ExternalInput").ap()
    eyem = dt_("eyem", [16, 128, 128], F32, kind="ExternalInput").ap()
    dtab = dt_("dtab", [44, 128, 128], F32, kind="ExternalInput").ap()
    maskA = dt_("maskA", [16, 128, 128], F32, kind="ExternalInput").ap()
    xo = dt_("xo", [TPC, D], F32, kind="ExternalOutput").ap()
    P = Prog(nc)
    emit_a0(P, xin, xin_own, xo, wq, bq, wdn, bdn, wup, bupk, bupv, wo, bo,
            qn, kn, gw, gb, cst, ident, ppair, ropeC, ropeS, ropeCo, ropeSo,
            mc, eyem, dtab, maskA, "a0")
    P.emit_and_close()
    return nc


def run_a0_stage(x, w_q, b_q, w_down, b_down, w_up, b_up, w_o, b_o,
                 qn_w, kn_w, gate_w, gate_b):
    nc = build_a0_launch()
    cstv = host_consts()
    identv = np.eye(128, dtype=np.float32)
    ppairv, C0, S0 = a0_host_tables()
    in_maps = []
    for c in range(NCORE):
        r = c % GSZ
        xb = x[c // GSZ]
        own = np.concatenate(
            [np.arange((r + GSZ * j) * 128, (r + GSZ * j) * 128 + 128)
             for j in range(NTT)])
        mcv, eyev, maskAv, dt44 = a0_core_tables(r)
        in_maps.append({
            "xin": np.ascontiguousarray(xb), "xin_own": _gather_own(xb, c),
            "wq": w_q, "bq": np.ascontiguousarray(b_q.reshape(16, 128).T),
            "wdn": w_down,
            "bdn": np.ascontiguousarray(b_down.reshape(4, 128).T),
            "wup": w_up,
            "bupk": np.ascontiguousarray(b_up[:512].reshape(4, 128).T),
            "bupv": np.tile(b_up[512:][None, :], (128, 1)).astype(np.float32),
            "wo": w_o, "bo": np.tile(b_o[None, :], (128, 1)).astype(np.float32),
            "qn": np.ascontiguousarray(qn_w[:, None]),
            "kn": np.ascontiguousarray(kn_w[:, None]),
            "gw": gate_w,
            "gb": np.ascontiguousarray(gate_b.T),
            "cst": cstv, "ident": identv, "ppair": ppairv,
            "ropeC": C0, "ropeS": S0,
            "ropeCo": np.ascontiguousarray(C0[:, own]),
            "ropeSo": np.ascontiguousarray(S0[:, own]),
            "mc": mcv, "eyem": eyev, "dtab": dt44, "maskA": maskAv,
        })
    res = run_bass_kernel_spmd(nc, in_maps, core_ids=list(range(NCORE)))
    out = np.empty_like(x)
    for c in range(NCORE):
        _scatter_own(out[c // GSZ], np.asarray(res.results[c]["xo"]), c)
    return out



def kernel(L0_attn_x, L0_attn_w_q, L0_attn_b_q, L0_attn_w_down, L0_attn_b_down,
           L0_attn_w_up, L0_attn_b_up, L0_attn_w_o, L0_attn_b_o, L0_attn_qn_w,
           L0_attn_kn_w, L0_attn_gate_w, L0_attn_gate_b, L0_ffn_Wq,
           L0_ffn_keys, L0_ffn_e_down, L0_ffn_e_up, L1_attn_w_qkv,
           L1_attn_w_out, L1_attn_ln_scale, L1_attn_ln_bias, L1_ffn_W1,
           L1_ffn_W2):
    x = np.ascontiguousarray(np.asarray(L0_attn_x, dtype=np.float32))
    x = run_f0_stage(x, L0_ffn_Wq, L0_ffn_keys, L0_ffn_e_down, L0_ffn_e_up)
    x = run_a0_stage(x, L0_attn_w_q, L0_attn_b_q, L0_attn_w_down,
                     L0_attn_b_down, L0_attn_w_up, L0_attn_b_up, L0_attn_w_o,
                     L0_attn_b_o, L0_attn_qn_w, L0_attn_kn_w, L0_attn_gate_w,
                     L0_attn_gate_b)
    x = run_f0_stage(x, L0_ffn_Wq, L0_ffn_keys, L0_ffn_e_down, L0_ffn_e_up)
    x = run_f1_stage(x, L1_ffn_W1, L1_ffn_W2)
    x = run_a1_stage(x, L1_attn_w_qkv, L1_attn_w_out, L1_attn_ln_scale,
                     L1_attn_ln_bias)
    x = run_f1_stage(x, L1_ffn_W1, L1_ffn_W2)
    return x
